# version 01
# baseline (speedup 1.0000x reference)
import numpy as np
import concourse.bass as bass
import concourse.mybir as mybir
from contextlib import ExitStack

F32 = mybir.dt.float32
BF16 = mybir.dt.bfloat16
AF = mybir.ActivationFunctionType
ALU = mybir.AluOpType
AX = mybir.AxisListType

T = 2048
HALO = 3
TH = T + HALO
D = 1024
DIN = 2048
CH = 2176
NH0 = 32
QSUB = 256
S1 = 8192
DH1 = 256
CAP = 5120


class Sem:
    def __init__(self, ck, name):
        self.h = ck.stack.enter_context(ck.nc.semaphore(name))
        self.n = 0

    def inc(self, inst, k=1):
        inst.then_inc(self.h, k)
        self.n += k
        return self.n

    def dinc(self, inst):
        return self.inc(inst, 16)

    def flush(self, eng):
        eng.wait_ge(self.h, self.n)


class CoreKernel:
    def __init__(self):
        self.nc = bass.Bass("TRN2", target_bir_lowering=False)
        self.stack = ExitStack()
        self.sems = {}
        self._sb = {}
        self.PS = [self.ps(f"psf{i}", [128, 512], F32) for i in range(6)]
        self.PB = [self.ps(f"psb{i}", [128, 512], BF16) for i in range(2)]

    def sem(self, name):
        if name not in self.sems:
            self.sems[name] = Sem(self, name)
        return self.sems[name]

    def sb(self, name, shape, dtype):
        t = self.stack.enter_context(self.nc.sbuf_tensor(name, list(shape), dtype))
        self._sb[name] = t
        return t

    def ps(self, name, shape, dtype=F32):
        return self.stack.enter_context(self.nc.psum_tensor(name, list(shape), dtype))

    def din(self, name, shape, dtype=F32):
        return self.nc.dram_tensor(name, list(shape), dtype, kind="ExternalInput")

    def dout(self, name, shape, dtype=F32):
        return self.nc.dram_tensor(name, list(shape), dtype, kind="ExternalOutput")

    def dint(self, name, shape, dtype=F32):
        return self.nc.dram_tensor(name, list(shape), dtype, kind="Internal")

    def close(self):
        self.stack.close()


def chunks(total, step):
    out = []
    o = 0
    while o < total:
        out.append((o, min(step, total - o)))
        o += step
    return out



def host_inputs_phase1(I, core):
    b, q = divmod(core, 4)
    u = np.asarray(I["L0_attn_u"])
    s0 = q * T
    rows = np.zeros((17 * 128, D), np.float32)
    lo = max(0, s0 - HALO)
    off = HALO - (s0 - lo)
    rows[off:off + (s0 + T - lo)] = u[b, lo:s0 + T]
    w1 = np.asarray(I["L0_attn_W_in1"])
    w_in1T = np.ascontiguousarray(w1[DIN:].T)
    cw = np.asarray(I["L0_attn_conv_w"])
    conv_wT = np.ascontiguousarray(np.transpose(cw, (2, 1, 0)))
    conv_b_t = np.ascontiguousarray(np.asarray(I["L0_attn_conv_b"]).reshape(17, 128).T)
    a_col = np.ascontiguousarray(
        np.broadcast_to(np.asarray(I["L0_attn_prelu_a"]).reshape(1, 1), (128, 1))
    ).astype(np.float32)
    alog_col = np.zeros((32, 1), np.float32)
    alog_col[:, 0] = np.asarray(I["L0_attn_A_log"])
    idn = np.eye(128, dtype=np.float32)
    import ml_dtypes
    return {
        "u_loc": rows,
        "w_in1T": w_in1T.astype(np.float32),
        "conv_wT": conv_wT.astype(np.float32),
        "conv_b_t": conv_b_t.astype(np.float32),
        "prelu_a_col": a_col,
        "alog_col": alog_col,
        "idn128": idn.astype(ml_dtypes.bfloat16),
    }


def p1_dram_decls(ck, kind="ExternalOutput"):
    mk = ck.dout if kind == "ExternalOutput" else ck.dint
    d = {}
    d["u_loc"] = ck.din("u_loc", [17 * 128, D], F32)
    d["w_in1T"] = ck.din("w_in1T", [D, 2208], F32)
    d["conv_wT"] = ck.din("conv_wT", [4, CH, CH], F32)
    d["conv_b_t"] = ck.din("conv_b_t", [128, 17], F32)
    d["prelu_a_col"] = ck.din("prelu_a_col", [128, 1], F32)
    d["alog_col"] = ck.din("alog_col", [32, 1], F32)
    d["idn128"] = ck.din("idn128", [128, 128], BF16)
    d["xn0T"] = mk("xn0T", [D, T], BF16)
    d["X_t"] = mk("X_t", [T, DIN], BF16)
    d["B_n"] = mk("B_n", [64, T], BF16)
    d["BC_t"] = mk("BC_t", [T, 128], BF16)
    d["C_n"] = mk("C_n", [64, T], BF16)
    d["caT"] = mk("caT", [NH0, T], F32)
    d["ca_t"] = mk("ca_t", [T, NH0], F32)
    return d


def emit_phase1(ck, block, dram):
    nc = ck.nc
    ld = ck.sem("g_ld")
    wld = ck.sem("g_wld")
    pe = ck.sem("g_pe")
    dv = ck.sem("g_dv")
    ac = ck.sem("g_ac")
    ev = ck.sem("g_ev")
    st = ck.sem("g_st")
    stxs = [ck.sem(f"g_stxs{j}") for j in range(4)]
    stcs = [ck.sem(f"g_stcs{j}") for j in range(2)]
    sto = ck.sem("g_sto")

    NT = 17

    u_t = ck.sb("p1_u", [128, 3 * D], F32)
    mu = ck.sb("p1_mu", [128, NT], F32)
    nmu = ck.sb("p1_nmu", [128, NT], F32)
    rstd = ck.sb("p1_rstd", [128, NT], F32)
    sq = ck.sb("p1_sq", [128, D], F32)
    var = ck.sb("p1_var", [128, NT], F32)
    xn_t = ck.sb("p1_xnt", [128, 2 * D], BF16)
    idn = ck.sb("idn", [128, 128], BF16)
    idnf = ck.sb("idnf", [128, 128], F32)
    xn0T = ck.sb("p1_xn0T", [128, 8 * 2176], BF16)
    wshare = ck.sb("p1_wsh", [128, 8 * 2208], BF16)
    w1 = wshare
    cwbuf = [wshare[:, 0:68 * 128], wshare[:, 68 * 128:136 * 128]]
    zxbc = ck.sb("p1_zxbc", [128, 17 * TH], BF16)
    dtb = ck.sb("p1_dtb", [32, TH], F32)
    caT = ck.sb("p1_caT", [32, T], F32)
    cbias = ck.sb("p1_cb", [128, 17], F32)
    acol = ck.sb("p1_acol", [128, 1], F32)
    cvo = ck.sb("p1_cvo", [128, 2 * T], BF16)
    cvf = ck.sb("p1_cvf", [128, 512], F32)
    cvr = ck.sb("p1_cvr", [128, 512], F32)
    alog = ck.sb("p1_alog", [32, 1], F32)
    aneg = ck.sb("p1_aneg", [32, 1], F32)
    catt = ck.sb("p1_catt", [128, 64], F32)
    xtev = ck.sb("p1_xtev", [128, 4 * 128], BF16)

    PS = ck.PS
    PB = ck.PB

    dv0 = ck.sems["g_dv"].n
    ac0 = ck.sems["g_ac"].n
    dvc_s1, dvc_s3, dvc_s5 = {}, {}, {}
    acc_s2, acc_s4 = {}, {}
    _dv = dv0
    for i in range(NT + 2):
        if i <= NT - 1:
            _dv += 2; dvc_s1[i] = _dv
        if 1 <= i <= NT:
            _dv += 3; dvc_s3[i - 1] = _dv
        if 2 <= i <= NT + 1:
            _dv += 1; dvc_s5[i - 2] = _dv
    _ac = ac0
    for i in range(NT + 1):
        if i <= NT - 1:
            _ac += 1; acc_s2[i] = _ac
        if 1 <= i <= NT:
            _ac += 1; acc_s4[i - 1] = _ac

    lds = [ck.sem(f"g_lds{j}") for j in range(3)]
    ldm = ck.sem("g_ldm")
    @block.sync
    def _(sync):
        for i in range(NT):
            if i >= 3:
                sync.wait_ge(dv.h, dvc_s5[i - 3])
            lds[i % 3].dinc(sync.dma_start(out=u_t[:, (i % 3) * D:(i % 3 + 1) * D],
                           in_=dram["u_loc"][i * 128:(i + 1) * 128, :]))
        ldm.dinc(sync.dma_start(out=cbias[:, :], in_=dram["conv_b_t"][:, :]))
        ldm.dinc(sync.dma_start(out=acol[:, :], in_=dram["prelu_a_col"][:, :]))
        ldm.dinc(sync.dma_start(out=alog[:, :], in_=dram["alog_col"][:, :]))
    ld_u = [16 * (i // 3 + 1) for i in range(NT)]
    ld_misc = ldm.n

    wldi = ck.sem("g_wldi")
    @block.gpsimd
    def _(gpsimd):
        wldi.dinc(gpsimd.dma_start(out=idn[:, :], in_=dram["idn128"][:, :]))
        for kb in range(8):
            wld.dinc(gpsimd.dma_start(out=w1[:, kb * 2208:(kb + 1) * 2208],
                             in_=dram["w_in1T"][kb * 128:(kb + 1) * 128, :]))
    wld_w1 = wld.n

    pe0 = ck.sems["g_pe"].n
    pe_tr_xn = lambda i: pe0 + 8 * (i + 1)

    def ln_s1(vector, i):
        vector.wait_ge(lds[i % 3].h, ld_u[i])
        ut = u_t[:, (i % 3) * D:(i % 3 + 1) * D]
        dv.inc(vector.reduce_sum(mu[:, i:i + 1], ut, axis=AX.X))
        dv.flush(vector)
        dv.inc(vector.tensor_scalar_mul(nmu[:, i:i + 1], mu[:, i:i + 1], -1.0 / D))
        assert dv.n == dvc_s1[i]

    def ln_s3(vector, i):
        vector.wait_ge(ac.h, acc_s2[i])
        dv.inc(vector.reduce_sum(var[:, i:i + 1], sq[:, :], axis=AX.X))
        dv.flush(vector)
        dv.inc(vector.tensor_scalar(var[:, i:i + 1], var[:, i:i + 1], 1.0 / D, 1e-5,
                                    ALU.mult, ALU.add))
        dv.flush(vector)
        dv.inc(vector.reciprocal(rstd[:, i:i + 1], var[:, i:i + 1]))
        assert dv.n == dvc_s3[i]

    def ln_s5(vector, i):
        vector.wait_ge(ac.h, acc_s4[i])
        if i >= 2:
            vector.wait_ge(pe.h, pe_tr_xn(i - 2))
        ut = u_t[:, (i % 3) * D:(i % 3 + 1) * D]
        dv.inc(vector.tensor_scalar(xn_t[:, (i % 2) * D:(i % 2 + 1) * D], ut,
                                    nmu[:, i:i + 1], rstd[:, i:i + 1],
                                    ALU.add, ALU.mult))
        assert dv.n == dvc_s5[i]

    ev_tr0 = ck.sems["g_ev"].n

    def ln_evtr(vector, j):
        for jd in range(8):
            k = j * 8 + jd
            vector.wait_ge(pe.h, pe0 + k + 1)
            ev.inc(vector.tensor_copy(
                xn0T[:, jd * 2176 + j * 128: jd * 2176 + (j + 1) * 128],
                PB[k % 2][:, 0:128]))

    @block.vector
    def _(vector):
        for i in range(NT + 3):
            if i <= NT - 1:
                ln_s1(vector, i)
            if 1 <= i <= NT:
                ln_s3(vector, i - 1)
            if 2 <= i <= NT + 1:
                ln_s5(vector, i - 2)
            if 3 <= i <= NT + 2:
                ln_evtr(vector, i - 3)

    @block.scalar
    def _(scalar):
        for i in range(NT + 1):
            if i <= NT - 1:
                scalar.wait_ge(dv.h, dvc_s1[i])
                if i >= 1:
                    scalar.wait_ge(dv.h, dvc_s3[i - 1])
                ut = u_t[:, (i % 3) * D:(i % 3 + 1) * D]
                ac.inc(scalar.activation(out=sq[:, :], in_=ut, func=AF.Square,
                                         bias=nmu[:, i:i + 1]))
                assert ac.n == acc_s2[i]
            if 1 <= i <= NT:
                j = i - 1
                scalar.wait_ge(dv.h, dvc_s3[j])
                ac.inc(scalar.activation(out=rstd[:, j:j + 1], in_=rstd[:, j:j + 1],
                                         func=AF.Sqrt))
                assert ac.n == acc_s4[j]

    @block.tensor
    def _(tensor):
        tensor.wait_ge(wldi.h, 16)
        for i in range(NT):
            tensor.wait_ge(dv.h, dvc_s5[i])
            for jd in range(8):
                k = i * 8 + jd
                if k >= 2:
                    tensor.wait_ge(ev.h, ev_tr0 + k - 1)
                pe.inc(tensor.transpose(
                    PB[k % 2][:, 0:128],
                    xn_t[:, (i % 2) * D + jd * 128:(i % 2) * D + (jd + 1) * 128],
                    idn[:, :]))
    assert pe.n == pe0 + NT * 8

    ev_tr_end = ev.n

    @block.sync
    def _(sync):
        for jd in range(8):
            sync.wait_ge(ev.h, ev_tr0 + (NT - 1) * 8 + jd + 1)
            sto.dinc(sync.dma_start(out=dram["xn0T"][jd * 128:(jd + 1) * 128, :],
                           in_=xn0T[:, jd * 2176 + HALO: jd * 2176 + HALO + T]))

    tcs = chunks(TH, 512)
    pe_zx0 = pe.n
    ev_zx0 = ev.n
    @block.tensor
    def _(tensor):
        tensor.wait_ge(wld.h, wld_w1)
        tensor.wait_ge(ev.h, ev_tr_end)
        cnt = 0
        for m in range(18):
            mc0 = m * 128
            mw = 128 if m < 17 else 32
            for (t0, tw) in tcs:
                pidx = cnt % 2
                if cnt >= 2:
                    tensor.wait_ge(ev.h, ev_zx0 + cnt - 1)
                for kb in range(7):
                    tensor.matmul(PS[pidx][0:mw, 0:tw],
                                  w1[:, kb * 2208 + mc0: kb * 2208 + mc0 + mw],
                                  xn0T[:, kb * 2176 + t0: kb * 2176 + t0 + tw],
                                  start=(kb == 0), stop=False)
                pe.inc(tensor.matmul(PS[pidx][0:mw, 0:tw],
                                     w1[:, 7 * 2208 + mc0: 7 * 2208 + mc0 + mw],
                                     xn0T[:, 7 * 2176 + t0: 7 * 2176 + t0 + tw],
                                     start=False, stop=True))
                cnt += 1
    pe_zx_end = pe.n
    nzx = 18 * len(tcs)

    @block.scalar
    def _(scalar):
        cnt = 0
        for m in range(18):
            mw = 128 if m < 17 else 32
            for (t0, tw) in tcs:
                pidx = cnt % 2
                scalar.wait_ge(pe.h, pe_zx0 + cnt + 1)
                if m < 17:
                    ev.inc(scalar.copy(zxbc[:, m * TH + t0: m * TH + t0 + tw],
                                       PS[pidx][0:mw, 0:tw]))
                else:
                    ev.inc(scalar.copy(dtb[:, t0:t0 + tw], PS[pidx][0:mw, 0:tw]))
                cnt += 1
    ev_zx_end = ev.n

    wld_cv0 = wld.n
    ac_cv0 = ac.n
    dv_cv0 = dv.n
    ev_cv0 = ev.n
    pe_cv0 = pe.n

    pe_conv_chunk = {}
    pe_tr_k = {}
    dv_cv_done = {m: dv_cv0 + 4 * (m + 1) for m in range(17)}
    pe_after_convmm = {}
    _pe = pe_cv0
    for m in range(17):
        for ti in range(4):
            _pe += 1
            pe_conv_chunk[(m, ti)] = _pe
        pe_after_convmm[m] = _pe
        ms = [m - 1] if m >= 1 else []
        if m == 16:
            ms.append(16)
        for mm_ in ms:
            for i in range(16):
                pe_tr_k[mm_ * 16 + i] = _pe + i + 1
            _pe += 16

    cws = [ck.sem("g_cws0"), ck.sem("g_cws1")]
    @block.tensor
    def _(tensor):
        tensor.wait_ge(ev.h, ev_zx_end)
        cnt = 0
        for m in range(17):
            buf = m % 2
            tensor.wait_ge(cws[buf].h, 16 * (m // 2 + 1))
            for ti in range(4):
                t0 = ti * 512
                pidx = 2 + (cnt % 2)
                if cnt >= 2:
                    tensor.wait_ge(ac.h, ac_cv0 + cnt - 1)
                mm = None
                first = True
                for k in range(4):
                    for kb in range(17):
                        mm = tensor.matmul(
                            PS[pidx][:, 0:512],
                            cwbuf[buf][:, (k * 17 + kb) * 128:(k * 17 + kb + 1) * 128],
                            zxbc[:, kb * TH + t0 + k: kb * TH + t0 + k + 512],
                            start=first, stop=(k == 3 and kb == 16))
                        first = False
                pe.inc(mm)
                cnt += 1
            ms2 = [m - 1] if m >= 1 else []
            if m == 16:
                ms2.append(16)
            for mtr in ms2:
                tensor.wait_ge(dv.h, dv_cv_done[mtr])
                for i in range(16):
                    k = mtr * 16 + i
                    if k >= 2:
                        tensor.wait_ge(ev.h, ev_cv0 + k - 1)
                    pe.inc(tensor.transpose(
                        PB[k % 2][:, 0:128],
                        cvo[:, (mtr % 2) * T + i * 128:(mtr % 2) * T + (i + 1) * 128],
                        idn[:, :]))
        assert pe.n == _pe, (pe.n, _pe)

    @block.gpsimd
    def _(gpsimd):
        for m in range(17):
            buf = m % 2
            if m < 2:
                gpsimd.wait_ge(pe.h, pe_zx_end)
            else:
                gpsimd.wait_ge(pe.h, pe_after_convmm[m - 2])
            src = dram["conv_wT"][:, :, m * 128:(m + 1) * 128]
            src = src.rearrange("k (ih il) o -> il k ih o", il=128)
            dst = cwbuf[buf].rearrange("p (k ih o) -> p k ih o", k=4, ih=17)
            cws[buf].dinc(gpsimd.dma_start(out=dst, in_=src))

    @block.scalar
    def _(scalar):
        cnt = 0
        for m in range(17):
            for ti in range(4):
                pidx = 2 + (cnt % 2)
                scalar.wait_ge(pe.h, pe_conv_chunk[(m, ti)])
                if cnt >= 1:
                    scalar.wait_ge(dv.h, dv_cv0 + cnt)
                ac.inc(scalar.add(cvf[:, :], PS[pidx][:, 0:512], cbias[:, m:m + 1]))
                cnt += 1

    @block.vector
    def _(vector):
        cnt = 0
        for m in range(17):
            buf = m % 2
            for ti in range(4):
                vector.wait_ge(ac.h, ac_cv0 + cnt + 1)
                if m >= 2 and ti == 0:
                    vector.wait_ge(pe.h, pe_tr_k[(m - 2) * 16 + 15])
                vector.tensor_relu(cvr[:, :], cvf[:, :])
                vector.tensor_sub(cvf[:, :], cvf[:, :], cvr[:, :])
                dv.inc(vector.scalar_tensor_tensor(
                    cvo[:, buf * T + ti * 512: buf * T + ti * 512 + 512],
                    cvf[:, :], acol[0:128, 0:1], cvr[:, :],
                    ALU.mult, ALU.add))
                cnt += 1
            ms = [m - 1] if m >= 1 else []
            if m == 16:
                ms.append(16)
            for mtr in ms:
                for i in range(16):
                    k = mtr * 16 + i
                    vector.wait_ge(pe.h, pe_tr_k[k])
                    if k >= 4:
                        vector.wait_ge(stxs[k % 4].h, 16 * (k // 4))
                    ev.inc(vector.tensor_copy(
                        xtev[:, (k % 4) * 128:(k % 4 + 1) * 128],
                        PB[k % 2][:, 0:128]))

    @block.sync
    def _(sync):
        for k in range(17 * 16):
            m, i = divmod(k, 16)
            sync.wait_ge(ev.h, ev_cv0 + k + 1)
            dst = (dram["X_t"][i * 128:(i + 1) * 128, m * 128:(m + 1) * 128] if m < 16
                   else dram["BC_t"][i * 128:(i + 1) * 128, :])
            stxs[k % 4].dinc(sync.dma_start(out=dst,
                           in_=xtev[:, (k % 4) * 128:(k % 4 + 1) * 128]))

    @block.gpsimd
    def _(gpsimd):
        gpsimd.wait_ge(dv.h, dv_cv_done[16])
        st.dinc(gpsimd.dma_start(out=dram["B_n"][:, :], in_=cvo[0:64, 0:T]))
        st.dinc(gpsimd.dma_start(out=dram["C_n"][:, :], in_=cvo[64:128, 0:T]))

    sp1h = u_t[0:32, 2 * D:3 * D]
    sp2h = sq[0:32, 0:D]
    ones32 = u_t[0:32, 0:T]
    at_T = dtb[:, HALO:TH]
    ac_dt0 = ac.n
    dv_dt0 = dv.n
    @block.vector
    def _(vector):
        vector.wait_ge(ev.h, ev_zx_end)
        dv.inc(vector.tensor_relu(caT[:, :], dtb[:, HALO:TH]))
        vector.memset(ones32, 1.0)

    @block.scalar
    def _(scalar):
        scalar.wait_ge(ev.h, ev_zx_end)
        scalar.wait_ge(dv.h, dv_dt0 + 1)
        for h in range(2):
            c0 = HALO + h * D
            scalar.activation(out=sp1h, in_=dtb[:, c0:c0 + D], func=AF.Abs)
            scalar.activation(out=sp2h, in_=sp1h, func=AF.Exp, scale=-1.0)
            scalar.activation(out=sp1h, in_=sp2h, func=AF.Ln, bias=1.0)
            scalar.copy(dtb[:, c0:c0 + D], sp1h)
        ac.inc(scalar.activation(out=aneg[:, :], in_=alog[:, :], func=AF.Exp))
    ac_dt1 = ac.n

    @block.vector
    def _(vector):
        vector.wait_ge(ac.h, ac_dt1)
        vector.wait_ge(ldm.h, ld_misc)
        vector.tensor_add(at_T, at_T, caT[:, :])
        dv.inc(vector.tensor_scalar_mul(aneg[:, :], aneg[:, :], -1.0))
        dv.flush(vector)
        dv.inc(vector.tensor_scalar_mul(at_T, at_T, aneg[0:32, 0:1]))
        dv.flush(vector)
        dv.inc(vector.tensor_tensor_scan(caT[:, :], at_T, ones32, 0.0,
                                         ALU.add, ALU.mult))
    dv_scan = dv.n

    @block.vector
    def _(vector):
        vector.wait_ge(wldi.h, 16)
        dv.inc(vector.tensor_copy(idnf[:, :], idn[:, :]))
    dv_idnf = dv.n

    pe_ca0 = pe.n
    ev_ca0 = ev.n
    @block.tensor
    def _(tensor):
        tensor.wait_ge(dv.h, dv_idnf)
        tensor.wait_ge(dv.h, dv_scan)
        for i in range(16):
            if i >= 2:
                tensor.wait_ge(ev.h, ev_ca0 + i - 1)
            pe.inc(tensor.transpose(PS[4 + (i % 2)][0:128, 0:32],
                                    caT[:, i * 128:(i + 1) * 128], idnf[0:32, 0:32]))

    @block.vector
    def _(vector):
        for i in range(16):
            vector.wait_ge(pe.h, pe_ca0 + i + 1)
            if i >= 2:
                vector.wait_ge(stcs[i % 2].h, 16 * (i // 2))
            ev.inc(vector.tensor_copy(catt[:, (i % 2) * 32:(i % 2) * 32 + 32],
                                      PS[4 + (i % 2)][0:128, 0:32]))

    @block.sync
    def _(sync):
        for i in range(16):
            sync.wait_ge(ev.h, ev_ca0 + i + 1)
            stcs[i % 2].dinc(sync.dma_start(out=dram["ca_t"][i * 128:(i + 1) * 128, :],
                           in_=catt[:, (i % 2) * 32:(i % 2) * 32 + 32]))
        sync.wait_ge(dv.h, dv_scan)
        sto.dinc(sync.dma_start(out=dram["caT"][:, :], in_=caT[:, :]))

    @block.gpsimd
    def _(gpsimd):
        gpsimd.wait_ge(st.h, st.n)
        for sm in stxs + stcs:
            gpsimd.wait_ge(sm.h, sm.n)
        gpsimd.wait_ge(sto.h, sto.n)

    return {}



import dataclasses


def bc0(col, n):
    return dataclasses.replace(col, ap=[col.ap[0], [0, n]])


def host_inputs_phase2(I, core):
    b, q = divmod(core, 4)
    sel32 = np.eye(32, dtype=np.float32)
    um = np.zeros((128, 256), np.float32)
    for p in range(128):
        um[p, :p] = -3e38
    m1 = np.zeros((4, 1), np.float32)
    m1[:q, 0] = 1.0
    m2 = np.zeros((4, 4), np.float32)
    for r in range(4):
        for j in range(4):
            if r < j < q:
                m2[j, r] = 1.0
    idn = np.eye(128, dtype=np.float32)
    import ml_dtypes
    return {"sel32": sel32, "umaskw": um, "m1": m1, "m2": m2,
            "idn128b": idn.astype(ml_dtypes.bfloat16)}


def p2_dram_decls(ck, kind="ExternalOutput", p1kind="ExternalInput"):
    mk = ck.dout if kind == "ExternalOutput" else ck.dint
    mki = ck.din if p1kind == "ExternalInput" else ck.dint
    d = {}
    if p1kind == "ExternalInput":
        d["X_t"] = ck.din("X_t", [T, DIN], BF16)
        d["BC_t"] = ck.din("BC_t", [T, 128], BF16)
        d["B_n"] = ck.din("B_n", [64, T], BF16)
        d["C_n"] = ck.din("C_n", [64, T], BF16)
        d["caT"] = ck.din("caT", [NH0, T], F32)
        d["ca_t"] = ck.din("ca_t", [T, NH0], F32)
    d["sel32"] = ck.din("sel32", [32, 32], F32)
    d["umaskw"] = ck.din("umaskw", [128, 256], F32)
    d["m1"] = ck.din("m1", [4, 1], F32)
    d["m2"] = ck.din("m2", [4, 4], F32)
    d["Y"] = mk("Y", [T, DIN], F32)
    d["cc2_in"] = ck.dint("cc2_in", [65, T], F32)
    d["cc2_out"] = ck.dint("cc2_out", [4 * 65, T], F32)
    return d


def emit_phase2(ck, block, dram, sb=None, groups=None):
    nc = ck.nc
    pe = ck.sem("g_pe")
    dv = ck.sem("g_dv")
    ac = ck.sem("g_ac")
    ev = ck.sem("g_ev")
    ld2 = ck.sem("p2_ld")
    gld2 = ck.sem("p2_gld")
    cc = ck.sem("p2_cc")
    ccd = ck.sem("p2_ccd")
    stys = [ck.sem(f"p2_sty{j}") for j in range(2)]

    NSC = 8
    Xs = ck.sb("p2_X", [128, 16 * DIN], BF16)
    Bt = ck.sb("p2_Bt", [128, 16 * 64], BF16)
    Bn = ck.sb("p2_Bn", [64, T], BF16)
    Cn = ck.sb("p2_Cn", [64, T], BF16)
    caT = ck.sb("p2_caT", [32, T], F32)
    ca_t = ck.sb("p2_cat", [128, 16 * 32], F32)
    nca_t = ck.sb("p2_ncat", [128, 16 * 32], F32)
    ES = ck.sb("p2_es", [128, 16 * 32], F32)
    TC = ck.sb("p2_tc", [32, 2 * NSC], F32)
    L = ck.sb("p2_L", [64, DIN], F32)
    P = ck.sb("p2_P", [64, DIN], F32)
    P16 = ck.sb("p2_P16", [64, 64], BF16)
    Lg = ck.sb("p2_Lg", [64, DIN], F32)
    W4 = ck.sb("p2_W4", [4, 32], F32)
    W4T = ck.sb("p2_W4T", [32, 4], F32)
    wcols = ck.sb("p2_wcols", [64, 32], F32)
    TOT = ck.sb("p2_TOT", [4, 32], F32)
    sel32 = ck.sb("p2_sel", [32, 32], F32)
    seldg = ck.sb("p2_seldg", [32, 32], F32)
    ones32c = ck.sb("p2_ones32c", [32, 1], F32)
    umaskw = ck.sb("p2_umask", [128, 256], F32)
    m1 = ck.sb("p2_m1", [4, 1], F32)
    m2 = ck.sb("p2_m2", [4, 4], F32)
    idnf4 = ck.sb("p2_idnf4", [4, 4], F32)
    zcol = ck.sb("p2_zcol", [128, 1], F32)
    arg0 = ck.sb("p2_arg0", [128, 2 * 256], F32)
    arg1 = ck.sb("p2_arg1", [128, 2 * 128], F32)
    SD0 = ck.sb("p2_sd0", [128, 2 * 256], BF16)
    SD1 = ck.sb("p2_sd1", [128, 2 * 128], BF16)
    Xd = ck.sb("p2_xd", [128, 2 * 128], BF16)
    E2c = ck.sb("p2_e2c", [128, 2 * 2], F32)
    tcc = ck.sb("p2_tcc", [64, 2], F32)
    noffc = ck.sb("p2_noffc", [128, 2], F32)
    zrow = ck.sb("p2_zrow", [1, T], F32)
    Ych = ck.sb("p2_ych", [128, 2 * 2 * DIN], F32)
    ytmp = ck.sb("p2_ytmp", [128, 2 * 64], F32)
    xdw = ck.sb("p2_xdw", [128, 16 * 64], BF16)
    PS = ck.PS

    @block.sync
    def _(sync):
        for i in range(16):
            ld2.dinc(sync.dma_start(out=Xs[:, i * DIN:(i + 1) * DIN],
                                    in_=dram["X_t"][i * 128:(i + 1) * 128, :]))
            ld2.dinc(sync.dma_start(out=Bt[:, i * 64:(i + 1) * 64],
                                    in_=dram["BC_t"][i * 128:(i + 1) * 128, 0:64]))
            ld2.dinc(sync.dma_start(out=ca_t[:, i * 32:(i + 1) * 32],
                                    in_=dram["ca_t"][i * 128:(i + 1) * 128, :]))
        ld2.dinc(sync.dma_start(out=Bn[:, :], in_=dram["B_n"][:, :]))
        ld2.dinc(sync.dma_start(out=Cn[:, :], in_=dram["C_n"][:, :]))
        ld2.dinc(sync.dma_start(out=caT[:, :], in_=dram["caT"][:, :]))
        ld2.dinc(sync.dma_start(out=sel32[:, :], in_=dram["sel32"][:, :]))
        ld2.dinc(sync.dma_start(out=umaskw[:, :], in_=dram["umaskw"][:, :]))
        ld2.dinc(sync.dma_start(out=m1[:, :], in_=dram["m1"][:, :]))
        ld2.dinc(sync.dma_start(out=m2[:, :], in_=dram["m2"][:, :]))
        ld2.dinc(sync.dma_start(out=idnf4[:, :], in_=dram["sel32"][0:4, 0:4]))
    ld_all = ld2.n

    dv_pro0 = dv.n
    @block.vector
    def _(vector):
        vector.wait_ge(ld2.h, ld_all)
        vector.memset(zcol[:, :], 0.0)
        vector.memset(zrow[:, :], 0.0)
        vector.memset(ones32c[:, :], 1.0)
        dv.inc(vector.tensor_scalar_mul(nca_t[:, :], ca_t[:, :], -1.0))
        dv.inc(vector.tensor_scalar_mul(seldg[:, :], sel32[:, :],
                                        caT[0:32, T - 1:T]))
        vector.tensor_copy(TC[:, 0:NSC], caT[0:32, QSUB - 1:T:QSUB])
        vector.memset(TC[:, NSC:NSC + 1], 0.0)
        dv.inc(vector.tensor_copy(TC[:, NSC + 1:2 * NSC], caT[0:32, QSUB - 1:T - QSUB:QSUB]))
        dv.flush(vector)
        dv.inc(vector.tensor_sub(TC[:, 0:NSC], TC[:, 0:NSC], TC[:, NSC:2 * NSC]))
    dv_pro = dv.n

    pe_pro0 = pe.n
    @block.tensor
    def _(tensor):
        tensor.wait_ge(dv.h, dv_pro)
        pe.inc(tensor.matmul(PS[4][:, 0:32], bc0(ones32c[:, 0:1], 128),
                             seldg[:, :], start=True, stop=True))
    pe_pro = pe.n

    ac_pro0 = ac.n
    @block.scalar
    def _(scalar):
        scalar.wait_ge(dv.h, dv_pro)
        ac.inc(scalar.activation(out=TC[:, 0:NSC], in_=TC[:, 0:NSC], func=AF.Exp))
    ac_pro = ac.n

    dv_es0 = dv.n
    @block.vector
    def _(vector):
        vector.wait_ge(pe.h, pe_pro)
        for i in range(15):
            vector.tensor_add(ES[:, i * 32:(i + 1) * 32],
                              nca_t[:, i * 32:(i + 1) * 32], PS[4][:, 0:32])
        dv.inc(vector.tensor_add(ES[:, 15 * 32:16 * 32],
                                 nca_t[:, 15 * 32:16 * 32], PS[4][:, 0:32]))
    dv_es = dv.n

    ac_es0 = ac.n
    @block.scalar
    def _(scalar):
        scalar.wait_ge(dv.h, dv_es)
        ac.inc(scalar.activation(out=ES[:, :], in_=ES[:, :], func=AF.Exp))
    ac_es = ac.n

    dvA = {}
    peA = {}
    evA = {}
    for h in range(32):
        @block.vector
        def _(vector, h=h):
            vector.wait_ge(ac.h, ac_es)
            if h >= 1:
                vector.wait_ge(pe.h, peA[h - 1])
            for i in range(16):
                inst = vector.tensor_scalar_mul(
                    xdw[:, i * 64:(i + 1) * 64],
                    Xs[:, i * DIN + h * 64: i * DIN + h * 64 + 64],
                    ES[:, i * 32 + h: i * 32 + h + 1])
            dv.inc(inst)
            dvA[h] = dv.n

        @block.tensor
        def _(tensor, h=h):
            tensor.wait_ge(dv.h, dvA[h])
            if h >= 2:
                tensor.wait_ge(ev.h, evA[h - 2])
            for i in range(15):
                tensor.matmul(PS[h % 2][0:64, 0:64], Bt[:, i * 64:(i + 1) * 64],
                              xdw[:, i * 64:(i + 1) * 64], start=(i == 0), stop=False)
            pe.inc(tensor.matmul(PS[h % 2][0:64, 0:64], Bt[:, 15 * 64:16 * 64],
                                 xdw[:, 15 * 64:16 * 64], start=False, stop=True))
            peA[h] = pe.n

        @block.vector
        def _(vector, h=h):
            vector.wait_ge(pe.h, peA[h])
            ev.inc(vector.tensor_copy(L[:, h * 64:(h + 1) * 64], PS[h % 2][0:64, 0:64]))
            evA[h] = ev.n
    ev_passA = ev.n

    @block.gpsimd
    def _(gpsimd):
        gpsimd.wait_ge(ev.h, ev_passA)
        ccd.dinc(gpsimd.dma_start(out=dram["cc2_in"][0:64, :], in_=L[:, :]))
        ccd.dinc(gpsimd.dma_start(out=dram["cc2_in"][64:65, 0:32],
                                  in_=caT[0:32, T - 1:T]))
        ccd.dinc(gpsimd.dma_start(out=dram["cc2_in"][64:65, 32:T],
                                  in_=zrow[:, 32:T]))
        gpsimd.wait_ge(ccd.h, ccd.n)
        gpsimd.collective_compute(
            "AllGather", ALU.bypass,
            replica_groups=groups or [[0, 1, 2, 3], [4, 5, 6, 7]],
            ins=[dram["cc2_in"].ap().opt()],
            outs=[dram["cc2_out"].ap().opt()],
        ).then_inc(cc.h, 1)
        cc.n += 1
        gpsimd.wait_ge(cc.h, cc.n)
        src = dram["cc2_out"].ap().rearrange("(r x) t -> r x t", x=65)
        ccd.dinc(gpsimd.dma_start(out=TOT[:, :], in_=src[:, 64, 0:32]))
    ccd_tot = ccd.n

    dv_cb0 = dv.n
    @block.vector
    def _(vector):
        vector.wait_ge(ccd.h, ccd_tot)
        dv.inc(vector.memset(P[:, :], 0.0))
    dv_pz = dv.n

    pe_cb0 = pe.n
    @block.tensor
    def _(tensor):
        tensor.wait_ge(ccd.h, ccd_tot)
        pe.inc(tensor.matmul(PS[4][0:4, 32:64], m2[:, :], TOT[:, :],
                             start=True, stop=True))
    pe_w4 = pe.n

    @block.scalar
    def _(scalar):
        scalar.wait_ge(pe.h, pe_w4)
        ac.inc(scalar.activation(out=W4[:, :], in_=PS[4][0:4, 32:64], func=AF.Exp))
    ac_w4 = ac.n

    @block.vector
    def _(vector):
        vector.wait_ge(ac.h, ac_w4)
        dv.inc(vector.tensor_scalar_mul(W4[:, :], W4[:, :], m1[0:4, 0:1]))
    dv_w4 = dv.n

    @block.tensor
    def _(tensor):
        tensor.wait_ge(dv.h, dv_w4)
        pe.inc(tensor.transpose(PS[5][0:32, 0:4], W4[:, :], idnf4[:, :]))
    pe_w4t = pe.n

    @block.vector
    def _(vector):
        vector.wait_ge(pe.h, pe_w4t)
        dv.inc(vector.tensor_copy(W4T[:, :], PS[5][0:32, 0:4]))
    dv_w4t = dv.n

    ccd_lg = {}
    dv_stt = {}
    pe_wc = {}
    for r in range(4):
        @block.gpsimd
        def _(gpsimd, r=r):
            if r >= 1:
                gpsimd.wait_ge(dv.h, dv_stt.get(r - 1, dv_w4t))
            src = dram["cc2_out"].ap().rearrange("(r x) t -> r x t", x=65)
            ccd.dinc(gpsimd.dma_start(out=Lg[:, :], in_=src[r, 0:64, :]))
            ccd_lg[r] = ccd.n

        @block.tensor
        def _(tensor, r=r):
            tensor.wait_ge(dv.h, dv_w4t)
            if r >= 1:
                tensor.wait_ge(dv.h, dv_stt.get(r - 1, 0))
            for h in range(31):
                tensor.matmul(PS[(r % 2)][0:64, 256 + h:256 + h + 1],
                              bc0(sel32[:, h:h + 1], 64), W4T[0:32, r:r + 1],
                              start=True, stop=True)
            pe.inc(tensor.matmul(PS[(r % 2)][0:64, 256 + 31:256 + 32],
                                 bc0(sel32[:, 31:32], 64), W4T[0:32, r:r + 1],
                                 start=True, stop=True))
            pe_wc[r] = pe.n

        @block.vector
        def _(vector, r=r):
            vector.wait_ge(pe.h, pe_wc[r])
            dv.inc(vector.tensor_copy(wcols[:, :], PS[(r % 2)][0:64, 256:288]))
            dv.flush(vector)
            vector.wait_ge(ccd.h, ccd_lg[r])
            for h in range(31):
                vector.scalar_tensor_tensor(
                    P[:, h * 64:(h + 1) * 64], Lg[:, h * 64:(h + 1) * 64],
                    wcols[:, h:h + 1], P[:, h * 64:(h + 1) * 64],
                    ALU.mult, ALU.add)
            dv.inc(vector.scalar_tensor_tensor(
                P[:, 31 * 64:32 * 64], Lg[:, 31 * 64:32 * 64],
                wcols[:, 31:32], P[:, 31 * 64:32 * 64],
                ALU.mult, ALU.add))
            dv_stt[r] = dv.n
    dv_P0 = dv.n

    nsel32 = ck.sb("p2_nsel", [32, 32], F32)
    ESB = ck.sb("p2_esb", [128, 16 * 32], F32)
    TCe = ck.sb("p2_tce", [32, NSC], F32)

    dv_pb0 = dv.n
    @block.vector
    def _(vector):
        vector.wait_ge(ld2.h, ld_all)
        vector.tensor_copy(TCe[:, :], caT[0:32, QSUB - 1:T:QSUB])
        dv.inc(vector.tensor_scalar_mul(nsel32[:, :], sel32[:, :], -1.0))
    dv_nsel = dv.n

    pe_esb = {}
    dv_esb = {}
    for c in range(NSC):
        @block.vector
        def _(vector, c=c):
            if c == 0:
                vector.wait_ge(dv.h, dv_nsel)
            dv.inc(vector.tensor_scalar_mul(seldg[:, :], sel32[:, :],
                                            TCe[0:32, c:c + 1]))
            dv_esb[(c, 'd')] = dv.n

        @block.tensor
        def _(tensor, c=c):
            tensor.wait_ge(dv.h, dv_esb[(c, 'd')])
            if c >= 1:
                tensor.wait_ge(dv.h, dv_esb[(c - 1, 'a')])
            pe.inc(tensor.matmul(PS[5][:, 0:32], bc0(ones32c[:, 0:1], 128),
                                 seldg[:, :], start=True, stop=True))
            pe_esb[c] = pe.n

        @block.vector
        def _(vector, c=c):
            vector.wait_ge(pe.h, pe_esb[c])
            vector.tensor_add(ESB[:, (2 * c) * 32:(2 * c + 1) * 32],
                              nca_t[:, (2 * c) * 32:(2 * c + 1) * 32], PS[5][:, 0:32])
            dv.inc(vector.tensor_add(ESB[:, (2 * c + 1) * 32:(2 * c + 2) * 32],
                                     nca_t[:, (2 * c + 1) * 32:(2 * c + 2) * 32],
                                     PS[5][:, 0:32]))
            dv_esb[(c, 'a')] = dv.n

    @block.scalar
    def _(scalar):
        scalar.wait_ge(dv.h, dv_esb[(NSC - 1, 'a')])
        ac.inc(scalar.activation(out=ESB[:, :], in_=ESB[:, :], func=AF.Exp))
    ac_esb = ac.n

    Tn = NSC * 32
    pe_p1, pe_p2 = {}, {}
    dv_d1a, dv_d1b, dv_d2 = {}, {}, {}
    ac_e = {}

    def emit_p2_tail(tensor, tp):
        cp, hp = divmod(tp, 32)
        pp = tp % 2
        l0p = QSUB * cp
        t0p, t1p = 2 * cp, 2 * cp + 1
        tensor.wait_ge(dv.h, dv_d1b[tp])
        if tp >= 2:
            tensor.wait_ge(dv.h, dv_d2[tp - 2])
        tensor.matmul(PS[4 + pp][:, 0:64], SD0[:, pp * 256:pp * 256 + 128],
                      Xs[:, t0p * DIN + hp * 64:t0p * DIN + hp * 64 + 64],
                      start=True, stop=True)
        tensor.matmul(PS[4 + pp][:, 64:128], SD0[:, pp * 256 + 128:pp * 256 + 256],
                      Xs[:, t0p * DIN + hp * 64:t0p * DIN + hp * 64 + 64],
                      start=True, stop=False)
        tensor.matmul(PS[4 + pp][:, 64:128], SD1[:, pp * 128:pp * 128 + 128],
                      Xs[:, t1p * DIN + hp * 64:t1p * DIN + hp * 64 + 64],
                      start=False, stop=True)
        tensor.matmul(PS[4 + pp][:, 128:192], Cn[:, l0p:l0p + 128],
                      P16[:, :], start=True, stop=True)
        tensor.matmul(PS[4 + pp][:, 192:256], Cn[:, l0p + 128:l0p + 256],
                      P16[:, :], start=True, stop=True)
        tensor.matmul(PS[4 + pp][0:64, 256:320], Bt[:, t0p * 64:(t0p + 1) * 64],
                      Xd[:, pp * 128:pp * 128 + 64], start=True, stop=False)
        pe.inc(tensor.matmul(PS[4 + pp][0:64, 256:320], Bt[:, t1p * 64:(t1p + 1) * 64],
                             Xd[:, pp * 128 + 64:pp * 128 + 128], start=False, stop=True))
        pe_p2[tp] = pe.n

    for t in range(Tn + 1):
        c, h = divmod(t, 32)
        p = t % 2
        if t < Tn:
            l0 = QSUB * c
            tt0, tt1 = 2 * c, 2 * c + 1

            @block.tensor
            def _(tensor, t=t, c=c, h=h, p=p, l0=l0, tt0=tt0, tt1=tt1):
                if t == 0:
                    tensor.wait_ge(ac.h, ac_esb)
                    tensor.wait_ge(dv.h, dv_P0)
                if t >= 2:
                    tensor.wait_ge(dv.h, dv_d1b[t - 2])
                    tensor.wait_ge(ac.h, ac_e[t - 2])
                tensor.matmul(PS[2 + p][:, 0:256], bc0(sel32[:, h:h + 1], 128),
                              caT[0:32, l0:l0 + 256], start=True, stop=True)
                if c >= 1:
                    tensor.matmul(PS[2 + p][:, 256:257], bc0(nsel32[:, h:h + 1], 128),
                                  caT[0:32, l0 - 1:l0], start=True, stop=True)
                tensor.matmul(PS[2 + p][0:64, 260:261], bc0(sel32[:, h:h + 1], 64),
                              TC[0:32, c:c + 1], start=True, stop=True)
                tensor.matmul(PS[0 + p][:, 0:256], Bn[:, l0:l0 + 128],
                              Cn[:, l0:l0 + 256], start=True, stop=True)
                pe.inc(tensor.matmul(PS[0 + p][:, 256:384], Bn[:, l0 + 128:l0 + 256],
                                     Cn[:, l0 + 128:l0 + 256], start=True, stop=True))
                pe_p1[t] = pe.n
                if t >= 1:
                    emit_p2_tail(tensor, t - 1)

            @block.vector
            def _(vector, t=t, p=p):
                vector.wait_ge(pe.h, pe_p1[t])
                vector.tensor_add(arg0[:, p * 256:(p + 1) * 256],
                                  PS[2 + p][:, 0:256], umaskw[:, :])
                if t >= 32:
                    vector.tensor_copy(noffc[:, p:p + 1], PS[2 + p][:, 256:257])
                dv.inc(vector.tensor_add(arg1[:, p * 128:(p + 1) * 128],
                                         PS[2 + p][:, 128:256], umaskw[:, 0:128]))
                dv_d1a[t] = dv.n

            @block.scalar
            def _(scalar, t=t, c=c, h=h, p=p, tt0=tt0, tt1=tt1):
                scalar.wait_ge(dv.h, dv_d1a[t])
                scalar.activation(out=arg0[:, p * 256:(p + 1) * 256],
                                  in_=arg0[:, p * 256:(p + 1) * 256], func=AF.Exp,
                                  bias=nca_t[:, tt0 * 32 + h:tt0 * 32 + h + 1])
                scalar.activation(out=arg1[:, p * 128:(p + 1) * 128],
                                  in_=arg1[:, p * 128:(p + 1) * 128], func=AF.Exp,
                                  bias=nca_t[:, tt1 * 32 + h:tt1 * 32 + h + 1])
                e2bias = noffc[:, p:p + 1] if c >= 1 else zcol[:, 0:1]
                scalar.activation(out=E2c[:, p * 2:p * 2 + 1],
                                  in_=ca_t[:, tt0 * 32 + h:tt0 * 32 + h + 1],
                                  func=AF.Exp, bias=e2bias)
                ac.inc(scalar.activation(out=E2c[:, p * 2 + 1:p * 2 + 2],
                                         in_=ca_t[:, tt1 * 32 + h:tt1 * 32 + h + 1],
                                         func=AF.Exp, bias=e2bias))
                ac_e[t] = ac.n

            @block.vector
            def _(vector, t=t, c=c, h=h, p=p, tt0=tt0, tt1=tt1):
                vector.wait_ge(ac.h, ac_e[t])
                if t >= 1:
                    vector.wait_ge(pe.h, pe_p2[t - 1])
                if t >= 32:
                    vector.wait_ge(dv.h, dv_d2[t - 32])
                vector.tensor_mul(SD0[:, p * 256:(p + 1) * 256],
                                  arg0[:, p * 256:(p + 1) * 256], PS[0 + p][:, 0:256])
                vector.tensor_mul(SD1[:, p * 128:(p + 1) * 128],
                                  arg1[:, p * 128:(p + 1) * 128], PS[0 + p][:, 256:384])
                vector.tensor_scalar_mul(Xd[:, p * 128:p * 128 + 64],
                                         Xs[:, tt0 * DIN + h * 64:tt0 * DIN + h * 64 + 64],
                                         ESB[:, tt0 * 32 + h:tt0 * 32 + h + 1])
                vector.tensor_scalar_mul(Xd[:, p * 128 + 64:p * 128 + 128],
                                         Xs[:, tt1 * DIN + h * 64:tt1 * DIN + h * 64 + 64],
                                         ESB[:, tt1 * 32 + h:tt1 * 32 + h + 1])
                vector.tensor_copy(P16[:, :], P[:, h * 64:(h + 1) * 64])
                dv.inc(vector.tensor_copy(tcc[:, p:p + 1], PS[2 + p][0:64, 260:261]))
                dv_d1b[t] = dv.n

        if t == Tn:
            @block.tensor
            def _(tensor):
                emit_p2_tail(tensor, Tn - 1)

        if t >= 1:
            tp = t - 1
            cp, hp = divmod(tp, 32)
            pp = tp % 2

            @block.vector
            def _(vector, tp=tp, cp=cp, hp=hp, pp=pp):
                vector.wait_ge(pe.h, pe_p2[tp])
                if hp == 0 and cp >= 2:
                    vector.wait_ge(stys[cp % 2].h, 16 * 2 * (cp // 2))
                ypar = cp % 2
                for lt in range(2):
                    vector.tensor_scalar_mul(
                        ytmp[:, lt * 64:(lt + 1) * 64],
                        PS[4 + pp][:, 128 + lt * 64:128 + (lt + 1) * 64],
                        E2c[:, pp * 2 + lt:pp * 2 + lt + 1])
                for lt in range(2):
                    vector.tensor_add(
                        Ych[:, (ypar * 2 + lt) * DIN + hp * 64:(ypar * 2 + lt) * DIN + hp * 64 + 64],
                        ytmp[:, lt * 64:(lt + 1) * 64],
                        PS[4 + pp][:, lt * 64:(lt + 1) * 64])
                dv.inc(vector.scalar_tensor_tensor(
                    P[:, hp * 64:(hp + 1) * 64], P[:, hp * 64:(hp + 1) * 64],
                    tcc[:, pp:pp + 1], PS[4 + pp][0:64, 256:320],
                    ALU.mult, ALU.add))
                dv_d2[tp] = dv.n

        if t % 32 == 0 and t >= 32:
            cs = t // 32 - 1

            @block.sync
            def _(sync, cs=cs):
                sync.wait_ge(dv.h, dv_d2[cs * 32 + 31])
                ypar = cs % 2
                for tt in range(2):
                    stys[ypar].dinc(sync.dma_start(
                        out=dram["Y"][(2 * cs + tt) * 128:(2 * cs + tt + 1) * 128, :],
                        in_=Ych[:, (ypar * 2 + tt) * DIN:(ypar * 2 + tt + 1) * DIN]))
    @block.sync
    def _(sync):
        sync.wait_ge(dv.h, dv_d2[Tn - 1])
        ypar = (NSC - 1) % 2
        for tt in range(2):
            stys[ypar].dinc(sync.dma_start(
                out=dram["Y"][(2 * (NSC - 1) + tt) * 128:(2 * (NSC - 1) + tt + 1) * 128, :],
                in_=Ych[:, (ypar * 2 + tt) * DIN:(ypar * 2 + tt + 1) * DIN]))

    @block.gpsimd
    def _(gpsimd):
        for sm in stys:
            gpsimd.wait_ge(sm.h, sm.n)
    return {}


def _p2_make_tail(ck, dram):
    pass



def host_inputs_phase34(I, core):
    b, q = divmod(core, 4)
    W2h = np.ascontiguousarray(np.asarray(I["L0_attn_W_in2"]).T)
    W3h = np.ascontiguousarray(np.asarray(I["L0_attn_W_in3"]).T)
    g = np.asarray(I["L0_attn_ln_g"]).astype(np.float32)
    bt = np.asarray(I["L0_attn_ln_b"]).astype(np.float32)
    gbc = np.ascontiguousarray(np.broadcast_to(g[None, :], (128, DIN)))
    bbc = np.ascontiguousarray(np.broadcast_to(bt[None, :], (128, DIN)))
    WgT = np.ascontiguousarray(np.asarray(I["L0_ffn_Wg"]).T)
    W1 = np.asarray(I["L0_ffn_W1"])
    W1il = np.concatenate([
        np.concatenate([W1[:, :, k * 512:(k + 1) * 512],
                        W1[:, :, 2048 + k * 512:2048 + (k + 1) * 512]], axis=2)
        for k in range(4)], axis=2)
    W2e = np.asarray(I["L0_ffn_W2"])
    m8 = np.zeros((8, 1), np.float32)
    m8[:core, 0] = 1.0
    trius = np.triu(np.ones((128, 128), np.float32), 1)
    return {"w_in2T": W2h.astype(np.float32), "w_in3T": W3h.astype(np.float32),
            "ln_g_bc": gbc, "ln_b_bc": bbc, "WgT": WgT.astype(np.float32),
            "w1il": np.ascontiguousarray(W1il).astype(np.float32),
            "w2e": np.ascontiguousarray(W2e).astype(np.float32),
            "m8": m8, "trius": trius}


def p34_dram_decls(ck, kind="ExternalOutput", p1kind="ExternalInput", ngather=8):
    mk = ck.dout if kind == "ExternalOutput" else ck.dint
    d = {}
    if p1kind == "ExternalInput":
        d["u_loc"] = ck.din("u_loc", [17 * 128, D], F32)
        d["idn128"] = ck.din("idn128", [128, 128], BF16)
        d["xn0T"] = ck.din("xn0T", [D, T], BF16)
        d["Y"] = ck.din("Y", [T, DIN], F32)
    d["w_in2T"] = ck.din("w_in2T", [D, DIN], F32)
    d["w_in3T"] = ck.din("w_in3T", [DIN, D], F32)
    d["ln_g_bc"] = ck.din("ln_g_bc", [128, DIN], F32)
    d["ln_b_bc"] = ck.din("ln_b_bc", [128, DIN], F32)
    d["WgT"] = ck.din("WgT", [D, 8], F32)
    d["w1il"] = ck.din("w1il", [8, D, 4096], F32)
    d["w2e"] = ck.din("w2e", [8, DIN, D], F32)
    d["m8"] = ck.din("m8", [8, 1], F32)
    d["trius"] = ck.din("trius", [128, 128], F32)
    d["x1"] = mk("x1", [T, D], F32)
    d["cc4_in"] = ck.dint("cc4_in", [1, 8], F32)
    d["cc4_out"] = ck.dint("cc4_out", [ngather, 8], F32)
    return d


def emit_phase34(ck, block, dram, groups=None, cap=5120.0, nexperts=8, dbg=False):
    nc = ck.nc
    pe = ck.sem("g_pe")
    dv = ck.sem("g_dv")
    ac = ck.sem("g_ac")
    ev = ck.sem("g_ev")
    ld3 = ck.sem("p3_ld")
    uld = [ck.sem(f"p3_uld{j}") for j in range(2)]
    yld = [ck.sem(f"p3_yld{j}") for j in range(2)]
    wld3 = ck.sem("p3_wld")
    w1ld = [ck.sem(f"p4_w1ld{j}") for j in range(2)]
    cc4 = ck.sem("p4_cc")
    ccd4 = ck.sem("p4_ccd")
    st3 = [ck.sem(f"p3_st{j}") for j in range(2)]

    xn0T = ck.sb("p34_xn0T", [128, 8 * T], BF16)
    abf = ck.sb("p34_abf", [128, 60 * 1024], BF16)
    af32 = ck.sb("p34_af32", [128, 12 * 1024], F32)
    w2T = abf[:, 0:16 * 1024]
    w3T = abf[:, 16 * 1024:32 * 1024]
    hnb = abf[:, 32 * 1024:34 * 1024]
    hnT = abf[:, 34 * 1024:36 * 1024]
    gbc = af32[:, 0:2048]
    bbc = af32[:, 2048:4096]
    hbuf = af32[:, 4096:6144]
    ybuf = af32[:, 6144:8192]
    x1st = af32[:, 8192:9216]
    ubuf = af32[:, 9216:11264]
    w1b = abf[:, 0:32 * 1024]
    w2b = abf[:, 32 * 1024:48 * 1024]
    hgT = abf[:, 48 * 1024:56 * 1024]
    yTb = abf[:, 56 * 1024:60 * 1024]
    stage = af32[:, 0:2048]
    stat = ck.sb("p34_stat", [128, 8], F32)
    idn = ck.sb("p34_idn", [128, 128], BF16)
    PS = ck.PS
    PB = ck.PB

    NTl = 16

    @block.sync
    def _(sync):
        for j in range(8):
            ld3.dinc(sync.dma_start(out=xn0T[:, j * T:(j + 1) * T],
                                    in_=dram["xn0T"][j * 128:(j + 1) * 128, :]))
        ld3.dinc(sync.dma_start(out=gbc[:, :], in_=dram["ln_g_bc"][:, :]))
        ld3.dinc(sync.dma_start(out=bbc[:, :], in_=dram["ln_b_bc"][:, :]))
    ld3_a = ld3.n

    @block.gpsimd
    def _(gpsimd):
        wld3.dinc(gpsimd.dma_start(out=idn[:, :], in_=dram["idn128"][:, :]))
        for j in range(8):
            wld3.dinc(gpsimd.dma_start(out=w2T[:, j * DIN:(j + 1) * DIN],
                             in_=dram["w_in2T"][j * 128:(j + 1) * 128, :]))
        for j in range(16):
            wld3.dinc(gpsimd.dma_start(out=w3T[:, j * D:(j + 1) * D],
                             in_=dram["w_in3T"][j * 128:(j + 1) * 128, :]))
    wld3_a = wld3.n

    dv_p3 = {}
    @block.sync
    def _(sync):
        sl = 0
        uld[0].dinc(sync.dma_start(out=ubuf[:, 0:D],
                     in_=dram["u_loc"][HALO:HALO + 128, :]))
        uld[1].dinc(sync.dma_start(out=ubuf[:, D:2 * D],
                     in_=dram["u_loc"][HALO + 128:HALO + 256, :]))
        yld[0].dinc(sync.dma_start(out=ybuf[:, 0:DIN],
                     in_=dram["Y"][0:128, :]))

    import types
    ns = types.SimpleNamespace(**{k: v for k, v in locals().items() if k != 'ns'})
    return _p34_body(ns)



def p5_dram_decls(ck, kind="ExternalOutput", xin="ExternalInput"):
    d = {}
    if xin == "ExternalInput":
        d["x1"] = ck.din("x1", [T, D], F32)
        d["idn128"] = ck.din("idn128", [128, 128], BF16)
    mk = ck.dout if kind == "ExternalOutput" else ck.dint
    d["xn1T"] = mk("xn1T", [D, T], BF16)
    d["cc5_in"] = ck.dint("cc5_in", [D, T], BF16)
    d["cc5_out"] = ck.dint("cc5_out", [4 * D, T], BF16)
    return d


def emit_phase5(ck, block, dram, groups=None):
    nc = ck.nc
    pe = ck.sem("g_pe")
    dv = ck.sem("g_dv")
    ac = ck.sem("g_ac")
    ev = ck.sem("g_ev")
    ld5 = [ck.sem(f"p5_ld{j}") for j in range(3)]
    il5 = ck.sem("p5_il")
    st5 = ck.sem("p5_st")
    cc5 = ck.sem("p5_cc")
    ccd5 = ck.sem("p5_ccd")

    x_t = ck.sb("p5_x", [128, 3 * D], F32)
    mu = ck.sb("p5_mu", [128, 16], F32)
    nmu = ck.sb("p5_nmu", [128, 16], F32)
    rstd = ck.sb("p5_rstd", [128, 16], F32)
    sq = ck.sb("p5_sq", [128, D], F32)
    var = ck.sb("p5_var", [128, 16], F32)
    xn_t = ck.sb("p5_xnt", [128, 2 * D], BF16)
    idn = ck.sb("p5_idn", [128, 128], BF16)
    xn1T = ck.sb("p5_xn1T", [128, 8 * T], BF16)
    PS = ck.PS
    PB = ck.PB
    NT5 = 16

    dv0 = dv.n
    dvc_s1, dvc_s3, dvc_s5 = {}, {}, {}
    acc_s2, acc_s4 = {}, {}
    _dv = dv0
    for i in range(NT5 + 2):
        if i <= NT5 - 1:
            _dv += 2; dvc_s1[i] = _dv
        if 1 <= i <= NT5:
            _dv += 3; dvc_s3[i - 1] = _dv
        if 2 <= i <= NT5 + 1:
            _dv += 1; dvc_s5[i - 2] = _dv
    _ac = ac.n
    for i in range(NT5 + 1):
        if i <= NT5 - 1:
            _ac += 1; acc_s2[i] = _ac
        if 1 <= i <= NT5:
            _ac += 1; acc_s4[i - 1] = _ac

    @block.sync
    def _(sync):
        for i in range(NT5):
            if i >= 3:
                sync.wait_ge(dv.h, dvc_s5[i - 3])
            ld5[i % 3].dinc(sync.dma_start(out=x_t[:, (i % 3) * D:(i % 3 + 1) * D],
                            in_=dram["x1"][i * 128:(i + 1) * 128, :]))

    @block.gpsimd
    def _(gpsimd):
        il5.dinc(gpsimd.dma_start(out=idn[:, :], in_=dram["idn128"][:, :]))

    pe0 = pe.n
    pe_tr = lambda i: pe0 + 8 * (i + 1)

    def s1(vector, i):
        vector.wait_ge(ld5[i % 3].h, 16 * (i // 3 + 1))
        ut = x_t[:, (i % 3) * D:(i % 3 + 1) * D]
        dv.inc(vector.reduce_sum(mu[:, i:i + 1], ut, axis=AX.X))
        dv.flush(vector)
        dv.inc(vector.tensor_scalar_mul(nmu[:, i:i + 1], mu[:, i:i + 1], -1.0 / D))
        assert dv.n == dvc_s1[i]

    def s3(vector, i):
        vector.wait_ge(ac.h, acc_s2[i])
        dv.inc(vector.reduce_sum(var[:, i:i + 1], sq[:, :], axis=AX.X))
        dv.flush(vector)
        dv.inc(vector.tensor_scalar(var[:, i:i + 1], var[:, i:i + 1], 1.0 / D, 1e-5,
                                    ALU.mult, ALU.add))
        dv.flush(vector)
        dv.inc(vector.reciprocal(rstd[:, i:i + 1], var[:, i:i + 1]))
        assert dv.n == dvc_s3[i]

    def s5(vector, i):
        vector.wait_ge(ac.h, acc_s4[i])
        if i >= 2:
            vector.wait_ge(pe.h, pe_tr(i - 2))
        ut = x_t[:, (i % 3) * D:(i % 3 + 1) * D]
        dv.inc(vector.tensor_scalar(xn_t[:, (i % 2) * D:(i % 2 + 1) * D], ut,
                                    nmu[:, i:i + 1], rstd[:, i:i + 1],
                                    ALU.add, ALU.mult))
        assert dv.n == dvc_s5[i]

    ev_tr0 = ev.n

    def evtr(vector, j):
        for jd in range(8):
            k = j * 8 + jd
            vector.wait_ge(pe.h, pe0 + k + 1)
            ev.inc(vector.tensor_copy(
                xn1T[:, jd * T + j * 128: jd * T + (j + 1) * 128],
                PB[k % 2][:, 0:128]))

    @block.vector
    def _(vector):
        for i in range(NT5 + 3):
            if i <= NT5 - 1:
                s1(vector, i)
            if 1 <= i <= NT5:
                s3(vector, i - 1)
            if 2 <= i <= NT5 + 1:
                s5(vector, i - 2)
            if 3 <= i <= NT5 + 2:
                evtr(vector, i - 3)

    @block.scalar
    def _(scalar):
        for i in range(NT5 + 1):
            if i <= NT5 - 1:
                scalar.wait_ge(dv.h, dvc_s1[i])
                if i >= 1:
                    scalar.wait_ge(dv.h, dvc_s3[i - 1])
                ut = x_t[:, (i % 3) * D:(i % 3 + 1) * D]
                ac.inc(scalar.activation(out=sq[:, :], in_=ut, func=AF.Square,
                                         bias=nmu[:, i:i + 1]))
                assert ac.n == acc_s2[i]
            if 1 <= i <= NT5:
                j = i - 1
                scalar.wait_ge(dv.h, dvc_s3[j])
                ac.inc(scalar.activation(out=rstd[:, j:j + 1], in_=rstd[:, j:j + 1],
                                         func=AF.Sqrt))
                assert ac.n == acc_s4[j]

    @block.tensor
    def _(tensor):
        tensor.wait_ge(il5.h, 16)
        for i in range(NT5):
            tensor.wait_ge(dv.h, dvc_s5[i])
            for jd in range(8):
                k = i * 8 + jd
                if k >= 2:
                    tensor.wait_ge(ev.h, ev_tr0 + k - 1)
                pe.inc(tensor.transpose(
                    PB[k % 2][:, 0:128],
                    xn_t[:, (i % 2) * D + jd * 128:(i % 2) * D + (jd + 1) * 128],
                    idn[:, :]))

    ev_end = ev.n

    @block.sync
    def _(sync):
        for jd in range(8):
            sync.wait_ge(ev.h, ev_tr0 + (NT5 - 1) * 8 + jd + 1)
            st5.dinc(sync.dma_start(out=dram["xn1T"][jd * 128:(jd + 1) * 128, :],
                     in_=xn1T[:, jd * T:(jd + 1) * T]))

    @block.gpsimd
    def _(gpsimd):
        gpsimd.wait_ge(ev.h, ev_end)
        for jd in range(8):
            ccd5.dinc(gpsimd.dma_start(out=dram["cc5_in"][jd * 128:(jd + 1) * 128, :],
                      in_=xn1T[:, jd * T:(jd + 1) * T]))
        gpsimd.wait_ge(ccd5.h, ccd5.n)
        gpsimd.collective_compute(
            "AllGather", ALU.bypass,
            replica_groups=groups or [[0, 1, 2, 3], [4, 5, 6, 7]],
            ins=[dram["cc5_in"].ap().opt()],
            outs=[dram["cc5_out"].ap().opt()],
        ).then_inc(cc5.h, 1)
        cc5.n += 1
        gpsimd.wait_ge(cc5.h, cc5.n)
        gpsimd.wait_ge(st5.h, st5.n)
    return {}


"""Body of phase 3+4 (P3 per-tile chain, MoE routing, experts)."""



def _p34_body(ns):
    cap = getattr(ns, 'cap', 5120.0)
    nexperts = getattr(ns, 'nexperts', 8)
    dbg = getattr(ns, 'dbg', False)
    ck, block, dram, groups = ns.ck, ns.block, ns.dram, ns.groups
    nc = ns.nc
    pe, dv, ac, ev = ns.pe, ns.dv, ns.ac, ns.ev
    ld3, uld, yld, wld3, w1ld = ns.ld3, ns.uld, ns.yld, ns.wld3, ns.w1ld
    cc4, ccd4 = ns.cc4, ns.ccd4
    xn0T, w2T, w3T = ns.xn0T, ns.w2T, ns.w3T
    gbc, bbc, ubuf, ybuf, hbuf, hnb, hnT, stat, idn = (
        ns.gbc, ns.bbc, ns.ubuf, ns.ybuf, ns.hbuf, ns.hnb, ns.hnT, ns.stat, ns.idn)
    PS, PB = ns.PS, ns.PB
    NTl = ns.NTl
    ld3_a, wld3_a = ns.ld3_a, ns.wld3_a
    dv_p3 = ns.dv_p3

    u2b2 = ck.sb("p34_u2b2", [128, 2 * 512], F32)
    w1b, w2b, hgT, stage = ns.w1b, ns.w2b, ns.hgT, ns.stage
    yTb = ns.yTb
    x1st = ns.x1st
    logitsT = ns.af32[0:8, 4096:6144]
    logits_t = ck.sb("p34_lg", [128, 16 * 8], F32)
    w_all = ck.sb("p34_w", [128, 16 * 8], F32)
    H2 = ck.sb("p34_h2", [128, 16 * 8], F32)
    eq1 = ck.sb("p34_eq1", [128, 8], F32)
    crows = ck.sb("p34_crow", [1, 16 * 8], F32)
    brow = ck.sb("p34_brow", [1, 8], F32)
    wgT = ck.sb("p34_wgT", [128, 8 * 8], BF16)
    m8 = ck.sb("p34_m8", [8, 1], F32)
    trius = ck.sb("p34_trius", [128, 128], F32)
    idnf = ck.sb("p34_idnf", [128, 128], F32)
    ones1 = ck.sb("p34_ones1", [128, 1], F32)
    onesr = ns.af32[0:1, 6144:6272]
    cntg = ns.af32[0:8, 6400:6408]
    gl = ck.sem("p34_gl")
    stx1 = [ck.sem(f"p34_stx{j}") for j in range(2)]

    @block.gpsimd
    def _(gpsimd):
        gl.dinc(gpsimd.dma_start(
            out=wgT[:, :].rearrange("p (kt e) -> p kt e", kt=8),
            in_=dram["WgT"].ap().rearrange("(kt kp) e -> kp kt e", kp=128)))
        gl.dinc(gpsimd.dma_start(out=m8[:, :], in_=dram["m8"][:, :]))
        gl.dinc(gpsimd.dma_start(out=trius[:, :], in_=dram["trius"][:, :]))
    gl_r = gl.n

    pe_u2, ac_u2, dv_h = {}, {}, {}
    dv_nmu, dv_var, ac_sq, ac_rs, dv_hn = {}, {}, {}, {}, {}
    pe_tr3, ev_tr3, pe_w3 = {}, {}, {}

    for i in range(NTl):
        sl = i % 2

        if i >= 1:
            @block.sync
            def _(sync, i=i, sl=sl):
                sync.wait_ge(dv.h, dv_h[(i - 1, 3)])
                if i >= 2:
                    sync.wait_ge(dv.h, dv_p3[i - 2])
                    uld[sl].dinc(sync.dma_start(out=ubuf[:, sl * D:(sl + 1) * D],
                                 in_=dram["u_loc"][3 + i * 128:3 + (i + 1) * 128, :]))
                yld[0].dinc(sync.dma_start(out=ybuf[:, 0:DIN],
                             in_=dram["Y"][i * 128:(i + 1) * 128, :]))

        for nchunk in range(4):
            @block.tensor
            def _(tensor, i=i, nchunk=nchunk):
                if i == 0 and nchunk == 0:
                    tensor.wait_ge(ld3.h, ld3_a)
                    tensor.wait_ge(wld3.h, wld3_a)
                pp = nchunk % 2
                key = (i, nchunk - 2) if nchunk >= 2 else (i - 1, nchunk + 2)
                if key in ac_u2:
                    tensor.wait_ge(ac.h, ac_u2[key])
                for kb in range(7):
                    tensor.matmul(PS[pp][:, 0:512],
                                  xn0T[:, kb * T + i * 128:kb * T + (i + 1) * 128],
                                  w2T[:, kb * DIN + nchunk * 512:kb * DIN + (nchunk + 1) * 512],
                                  start=(kb == 0), stop=False)
                pe.inc(tensor.matmul(PS[pp][:, 0:512],
                                     xn0T[:, 7 * T + i * 128:7 * T + (i + 1) * 128],
                                     w2T[:, 7 * DIN + nchunk * 512:7 * DIN + (nchunk + 1) * 512],
                                     start=False, stop=True))
                pe_u2[(i, nchunk)] = pe.n

            @block.scalar
            def _(scalar, i=i, nchunk=nchunk):
                scalar.wait_ge(pe.h, pe_u2[(i, nchunk)])
                for key in [(i, nchunk - 2), (i - 1, nchunk + 2)]:
                    if key in dv_h:
                        scalar.wait_ge(dv.h, dv_h[key])
                        break
                ac.inc(scalar.activation(
                    out=u2b2[:, (nchunk % 2) * 512:(nchunk % 2 + 1) * 512],
                    in_=PS[nchunk % 2][:, 0:512], func=AF.Gelu_apprx_tanh))
                ac_u2[(i, nchunk)] = ac.n

            @block.vector
            def _(vector, i=i, sl=sl, nchunk=nchunk):
                vector.wait_ge(ac.h, ac_u2[(i, nchunk)])
                if nchunk == 0:
                    vector.wait_ge(yld[0].h, 16 * (i + 1))
                dv.inc(vector.tensor_mul(
                    hbuf[:, nchunk * 512:(nchunk + 1) * 512],
                    u2b2[:, (nchunk % 2) * 512:(nchunk % 2 + 1) * 512],
                    ybuf[:, nchunk * 512:(nchunk + 1) * 512]))
                dv_h[(i, nchunk)] = dv.n

        @block.vector
        def _(vector, i=i):
            if i >= 1:
                vector.wait_ge(ac.h, ac_rs[i - 1])
            dv.inc(vector.reduce_sum(stat[:, 0:1], hbuf[:, :], axis=AX.X))
            dv.flush(vector)
            dv.inc(vector.tensor_scalar_mul(stat[:, 1:2], stat[:, 0:1], -1.0 / DIN))
            dv_nmu[i] = dv.n

        dv_sqr = {}
        for nchunk in range(4):
            @block.scalar
            def _(scalar, i=i, nchunk=nchunk):
                if nchunk == 0:
                    scalar.wait_ge(dv.h, dv_nmu[i])
                if nchunk >= 2:
                    scalar.wait_ge(dv.h, dv_sqr[nchunk - 2])
                ac.inc(scalar.activation(
                    out=u2b2[:, (nchunk % 2) * 512:(nchunk % 2 + 1) * 512],
                    in_=hbuf[:, nchunk * 512:(nchunk + 1) * 512],
                    func=AF.Square, bias=stat[:, 1:2]))
                ac_sq[(i, nchunk)] = ac.n

            @block.vector
            def _(vector, i=i, nchunk=nchunk):
                vector.wait_ge(ac.h, ac_sq[(i, nchunk)])
                dv.inc(vector.reduce_sum(stat[:, 4 + nchunk:5 + nchunk],
                                         u2b2[:, (nchunk % 2) * 512:(nchunk % 2 + 1) * 512],
                                         axis=AX.X))
                dv_sqr[nchunk] = dv.n

        @block.vector
        def _(vector, i=i):
            dv.flush(vector)
            vector.tensor_add(stat[:, 4:5], stat[:, 4:5], stat[:, 5:6])
            dv.inc(vector.tensor_add(stat[:, 6:7], stat[:, 6:7], stat[:, 7:8]))
            dv.flush(vector)
            dv.inc(vector.tensor_add(stat[:, 4:5], stat[:, 4:5], stat[:, 6:7]))
            dv.flush(vector)
            dv.inc(vector.tensor_scalar(stat[:, 4:5], stat[:, 4:5], 1.0 / DIN, 1e-5,
                                        ALU.mult, ALU.add))
            dv.flush(vector)
            dv.inc(vector.reciprocal(stat[:, 2:3], stat[:, 4:5]))
            dv_var[i] = dv.n

        @block.scalar
        def _(scalar, i=i):
            scalar.wait_ge(dv.h, dv_var[i])
            ac.inc(scalar.activation(out=stat[:, 2:3], in_=stat[:, 2:3], func=AF.Sqrt))
            ac_rs[i] = ac.n

        @block.vector
        def _(vector, i=i):
            vector.wait_ge(ac.h, ac_rs[i])
            for nchunk in range(4):
                vector.tensor_scalar(u2b2[:, (nchunk % 2) * 512:(nchunk % 2 + 1) * 512],
                                     hbuf[:, nchunk * 512:(nchunk + 1) * 512],
                                     stat[:, 1:2], stat[:, 2:3], ALU.add, ALU.mult)
                vector.tensor_mul(u2b2[:, (nchunk % 2) * 512:(nchunk % 2 + 1) * 512],
                                  u2b2[:, (nchunk % 2) * 512:(nchunk % 2 + 1) * 512],
                                  gbc[:, nchunk * 512:(nchunk + 1) * 512])
                dv.inc(vector.tensor_add(hnb[:, nchunk * 512:(nchunk + 1) * 512],
                                         u2b2[:, (nchunk % 2) * 512:(nchunk % 2 + 1) * 512],
                                         bbc[:, nchunk * 512:(nchunk + 1) * 512]))
            dv_hn[i] = dv.n

        for j in range(16):
            @block.tensor
            def _(tensor, i=i, j=j):
                k = i * 16 + j
                if j == 0:
                    tensor.wait_ge(dv.h, dv_hn[i])
                    if i == 0:
                        tensor.wait_ge(wld3.h, wld3_a)
                if k >= 2:
                    tensor.wait_ge(ev.h, ev_tr3[k - 2])
                pe.inc(tensor.transpose(PB[k % 2][:, 0:128],
                                        hnb[:, j * 128:(j + 1) * 128], idn[:, :]))
                pe_tr3[k] = pe.n

            @block.vector
            def _(vector, i=i, j=j):
                k = i * 16 + j
                vector.wait_ge(pe.h, pe_tr3[k])
                ev.inc(vector.tensor_copy(hnT[:, j * 128:(j + 1) * 128],
                                          PB[k % 2][:, 0:128]))
                ev_tr3[k] = ev.n

        @block.tensor
        def _(tensor, i=i):
            tensor.wait_ge(ev.h, ev_tr3[i * 16 + 15])
            for nchunk in range(2):
                if i >= 1:
                    tensor.wait_ge(dv.h, dv_p3.get(i - 1, 0))
                for kb in range(15):
                    tensor.matmul(PS[2 + nchunk][:, 0:512],
                                  hnT[:, kb * 128:(kb + 1) * 128],
                                  w3T[:, kb * D + nchunk * 512:kb * D + (nchunk + 1) * 512],
                                  start=(kb == 0), stop=False)
                pe.inc(tensor.matmul(PS[2 + nchunk][:, 0:512],
                                     hnT[:, 15 * 128:16 * 128],
                                     w3T[:, 15 * D + nchunk * 512:15 * D + (nchunk + 1) * 512],
                                     start=False, stop=True))
                pe_w3[(i, nchunk)] = pe.n

        @block.vector
        def _(vector, i=i, sl=sl):
            vector.wait_ge(pe.h, pe_w3[(i, 1)])
            vector.wait_ge(uld[sl].h, 16 * (i // 2 + 1))
            if i >= 1:
                vector.wait_ge(stx1[0].h, 16 * i)
            xs = x1st[:, 0:D]
            vector.tensor_add(xs[:, 0:512], PS[2][:, 0:512],
                              ubuf[:, sl * D:sl * D + 512])
            dv.inc(vector.tensor_add(xs[:, 512:1024], PS[3][:, 0:512],
                                     ubuf[:, sl * D + 512:(sl + 1) * D]))
            dv_p3[i] = dv.n

        @block.sync
        def _(sync, i=i):
            sync.wait_ge(dv.h, dv_p3[i])
            stx1[0].dinc(sync.dma_start(
                out=dram["x1"][i * 128:(i + 1) * 128, :],
                in_=x1st[:, 0:D]))

    pe_lg = {}
    dv_lgT = {}

    @block.vector
    def _(vector):
        vector.memset(ones1[:, :], 1.0)
        vector.memset(cntg[:, :], 0.0)
        vector.memset(onesr[:, :], 1.0)
        vector.memset(crows[:, :], 0.0)
        dv.inc(vector.tensor_copy(idnf[:, :], idn[:, :]))
    dv_msets = dv.n

    for tc in range(4):
        @block.tensor
        def _(tensor, tc=tc):
            if tc == 0:
                tensor.wait_ge(gl.h, gl_r)
                tensor.wait_ge(dv.h, dv_p3[NTl - 1])
            if tc >= 2:
                tensor.wait_ge(dv.h, dv_lgT[tc - 2])
            for kb in range(7):
                tensor.matmul(PS[4 + tc % 2][0:8, 0:512], wgT[:, kb * 8:(kb + 1) * 8],
                              xn0T[:, kb * T + tc * 512:kb * T + (tc + 1) * 512],
                              start=(kb == 0), stop=False)
            pe.inc(tensor.matmul(PS[4 + tc % 2][0:8, 0:512], wgT[:, 7 * 8:8 * 8],
                                 xn0T[:, 7 * T + tc * 512:7 * T + (tc + 1) * 512],
                                 start=False, stop=True))
            pe_lg[tc] = pe.n

        @block.vector
        def _(vector, tc=tc):
            vector.wait_ge(pe.h, pe_lg[tc])
            dv.inc(vector.tensor_copy(logitsT[:, tc * 512:(tc + 1) * 512],
                                      PS[4 + tc % 2][0:8, 0:512]))
            dv_lgT[tc] = dv.n
    dv_lgT_all = dv.n

    pe_lgt, ev_lgt = {}, {}
    for i in range(16):
        @block.tensor
        def _(tensor, i=i):
            if i == 0:
                tensor.wait_ge(dv.h, dv_lgT_all)
                tensor.wait_ge(dv.h, dv_msets)
            if i >= 2:
                tensor.wait_ge(ev.h, ev_lgt[i - 2])
            pe.inc(tensor.transpose(PS[2 + (i % 2)][0:128, 0:8],
                                    logitsT[:, i * 128:(i + 1) * 128], idnf[0:8, 0:8]))
            pe_lgt[i] = pe.n

        @block.vector
        def _(vector, i=i):
            vector.wait_ge(pe.h, pe_lgt[i])
            ev.inc(vector.tensor_copy(logits_t[:, i * 8:(i + 1) * 8],
                                      PS[2 + (i % 2)][0:128, 0:8]))
            ev_lgt[i] = ev.n

    dv_rt, ac_sg = {}, {}
    for i in range(16):
        @block.vector
        def _(vector, i=i):
            vector.wait_ge(ev.h, ev_lgt[i])
            lg = logits_t[:, i * 8:(i + 1) * 8]
            dv.inc(vector.reduce_max(stat[:, 0:1], lg, axis=AX.X))
            dv.flush(vector)
            dv.inc(vector.tensor_scalar(eq1[:, :], lg, stat[:, 0:1], None, ALU.is_equal))
            dv.flush(vector)
            dv.inc(vector.scalar_tensor_tensor(u2b2[:, 0:8], eq1[:, :], -1e30, lg,
                                               ALU.mult, ALU.add))
            dv.flush(vector)
            dv.inc(vector.reduce_max(stat[:, 1:2], u2b2[:, 0:8], axis=AX.X))
            dv.flush(vector)
            dv.inc(vector.tensor_sub(stat[:, 2:3], stat[:, 0:1], stat[:, 1:2]))
            dv_rt[(i, 'd')] = dv.n

        @block.scalar
        def _(scalar, i=i):
            scalar.wait_ge(dv.h, dv_rt[(i, 'd')])
            ac.inc(scalar.activation(out=stat[:, 3:4], in_=stat[:, 2:3], func=AF.Sigmoid))
            ac_sg[i] = ac.n

        @block.vector
        def _(vector, i=i):
            vector.wait_ge(ac.h, ac_sg[i])
            lg = logits_t[:, i * 8:(i + 1) * 8]
            vector.tensor_scalar(stat[:, 4:5], stat[:, 3:4], -1.0, 1.0,
                                 ALU.mult, ALU.add)
            dv.inc(vector.tensor_scalar(H2[:, i * 8:(i + 1) * 8], lg, stat[:, 1:2],
                                        None, ALU.is_ge))
            dv.flush(vector)
            dv.inc(vector.tensor_sub(u2b2[:, 0:8], H2[:, i * 8:(i + 1) * 8], eq1[:, :]))
            dv.flush(vector)
            vector.tensor_scalar_mul(u2b2[:, 0:8], u2b2[:, 0:8], stat[:, 4:5])
            dv.inc(vector.tensor_scalar_mul(u2b2[:, 8:16], eq1[:, :], stat[:, 3:4]))
            dv.flush(vector)
            dv.inc(vector.tensor_add(w_all[:, i * 8:(i + 1) * 8], u2b2[:, 0:8],
                                     u2b2[:, 8:16]))
            dv_rt[(i, 'w')] = dv.n

    pe_cnt, ev_cr = {}, {}
    csrow = ns.af32[0:1, 6528:6656]

    @block.tensor
    def _(tensor):
        for i in range(16):
            tensor.wait_ge(dv.h, dv_rt[(i, 'w')])
            pe.inc(tensor.matmul(PS[0][0:1, i * 8:(i + 1) * 8], ones1[:, 0:1],
                                 H2[:, i * 8:(i + 1) * 8], start=True, stop=True))
            pe_cnt[i] = pe.n

    @block.vector
    def _(vector):
        vector.wait_ge(pe.h, pe_cnt[15])
        dv.inc(vector.tensor_copy(csrow[:, :], PS[0][0:1, 0:128]))
        dv.flush(vector)
        for i in range(1, 16):
            dv.inc(vector.tensor_add(crows[:, i * 8:(i + 1) * 8],
                                     crows[:, (i - 1) * 8:i * 8],
                                     csrow[:, (i - 1) * 8:i * 8]))
            dv.flush(vector)
        dv.inc(vector.tensor_add(brow[:, :], crows[:, 15 * 8:16 * 8],
                                 csrow[:, 15 * 8:16 * 8]))
        dv.flush(vector)
    ev_cnt_all = ev.n
    dv_cnt_all = dv.n

    @block.gpsimd
    def _(gpsimd):
        gpsimd.wait_ge(dv.h, dv_cnt_all)
        ccd4.dinc(gpsimd.dma_start(out=dram["cc4_in"][:, :], in_=brow[:, :]))
        gpsimd.wait_ge(ccd4.h, ccd4.n)
        gpsimd.collective_compute(
            "AllGather", ALU.bypass,
            replica_groups=groups or [list(range(8))],
            ins=[dram["cc4_in"].ap().opt()],
            outs=[dram["cc4_out"].ap().opt()],
        ).then_inc(cc4.h, 1)
        cc4.n += 1
        gpsimd.wait_ge(cc4.h, cc4.n)
        ccd4.dinc(gpsimd.dma_start(out=cntg[0:dram["cc4_out"].shape[0], :],
                                   in_=dram["cc4_out"][:, :]))
    ccd4_all = ccd4.n

    @block.tensor
    def _(tensor):
        tensor.wait_ge(ccd4.h, ccd4_all)
        tensor.wait_ge(gl.h, gl_r)
        pe.inc(tensor.matmul(PS[1][0:1, 0:8], m8[:, :], cntg[:, :],
                             start=True, stop=True))
    pe_base = pe.n

    @block.vector
    def _(vector):
        vector.wait_ge(pe.h, pe_base)
        dv.inc(vector.tensor_copy(brow[:, :], PS[1][0:1, 0:8]))
    dv_base = dv.n

    pe_pref, dv_w = {}, {}
    for i in range(16):
        @block.tensor
        def _(tensor, i=i):
            tensor.wait_ge(dv.h, dv_base)
            if i >= 2:
                tensor.wait_ge(dv.h, dv_w.get(i - 2, 0))
            tensor.matmul(PS[2 + (i % 2)][:, 0:8], trius[:, :],
                          H2[:, i * 8:(i + 1) * 8], start=True, stop=False)
            tensor.matmul(PS[2 + (i % 2)][:, 0:8], onesr[:, :],
                          crows[:, i * 8:(i + 1) * 8], start=False, stop=False)
            pe.inc(tensor.matmul(PS[2 + (i % 2)][:, 0:8], onesr[:, :],
                                 brow[:, :], start=False, stop=True))
            pe_pref[i] = pe.n

        @block.vector
        def _(vector, i=i):
            vector.wait_ge(pe.h, pe_pref[i])
            dv.inc(vector.tensor_scalar(u2b2[:, 0:8], PS[2 + (i % 2)][:, 0:8], cap,
                                        None, ALU.is_lt))
            dv.flush(vector)
            dv.inc(vector.tensor_mul(w_all[:, i * 8:(i + 1) * 8],
                                     w_all[:, i * 8:(i + 1) * 8], u2b2[:, 0:8]))
            dv_w[i] = dv.n
    dv_w_all = dv.n

    accd = ck.sem("p34_accd")
    sg3 = [ck.sem(f"p34_sg3{j}") for j in range(2)]
    pe_pair, ac_hg, dv_hg = {}, {}, {}
    pe_y, pe_yd, ac_y = {}, {}, {}
    pe_tr4, pe_tr4d, pe_trk, dv_c4 = {}, {}, {}, {}
    acc_n = {}
    pe_done = {}
    for e in range(nexperts):
        @block.gpsimd
        def _(gpsimd, e=e):
            if e >= 1:
                gpsimd.wait_ge(pe.h, pe_done[e - 1])
            src1 = dram["w1il"][e].rearrange("(kt kp) n -> kp kt n", kp=128)
            w1ld[0].dinc(gpsimd.dma_start(
                out=w1b[:, :].rearrange("p (kt n) -> p kt n", kt=8), in_=src1))
            src2 = dram["w2e"][e].rearrange("(kt kp) n -> kp kt n", kp=128)
            w1ld[0].dinc(gpsimd.dma_start(
                out=w2b[:, :].rearrange("p (kt n) -> p kt n", kt=16), in_=src2))

        for tc in range(4):
            for k in range(16):
                @block.tensor
                def _(tensor, e=e, tc=tc, k=k):
                    if tc == 0 and k == 0:
                        tensor.wait_ge(w1ld[0].h, 32 * (e + 1))
                        if e == 0:
                            tensor.wait_ge(dv.h, dv_w_all)
                    p = k % 2
                    ji = (k // 4) * 8 + (k % 4)
                    jg = ji + 4
                    prevs = [(e, tc, k - 2), (e, tc - 1, 14 + k), (e - 1, 3, 14 + k)]
                    for pk in prevs:
                        if pk in ac_hg:
                            tensor.wait_ge(ac.h, ac_hg[pk])
                            tensor.wait_ge(dv.h, dv_hg[pk])
                            break
                    mm = None
                    for jj, bank in ((ji, 0 + 2 * p), (jg, 1 + 2 * p)):
                        for kb in range(7):
                            tensor.matmul(PS[bank][:, 0:512],
                                          w1b[:, kb * 4096 + jj * 128:kb * 4096 + (jj + 1) * 128],
                                          xn0T[:, kb * T + tc * 512:kb * T + (tc + 1) * 512],
                                          start=(kb == 0), stop=False)
                        mm = tensor.matmul(PS[bank][:, 0:512],
                                           w1b[:, 7 * 4096 + jj * 128:7 * 4096 + (jj + 1) * 128],
                                           xn0T[:, 7 * T + tc * 512:7 * T + (tc + 1) * 512],
                                           start=False, stop=True)
                    pe.inc(mm)
                    pe_pair[(e, tc, k)] = pe.n

                @block.scalar
                def _(scalar, e=e, tc=tc, k=k):
                    p = k % 2
                    scalar.wait_ge(pe.h, pe_pair[(e, tc, k)])
                    if (e, tc, k - 2) in dv_hg:
                        scalar.wait_ge(dv.h, dv_hg[(e, tc, k - 2)])
                    ac.inc(scalar.activation(out=u2b2[:, p * 512:(p + 1) * 512],
                                             in_=PS[0 + 2 * p][:, 0:512], func=AF.Gelu_apprx_tanh))
                    ac_hg[(e, tc, k)] = ac.n

                @block.vector
                def _(vector, e=e, tc=tc, k=k):
                    p = k % 2
                    vector.wait_ge(ac.h, ac_hg[(e, tc, k)])
                    if (e, tc) != (0, 0) and k == 0:
                        prev = (e, tc - 1) if tc >= 1 else (e - 1, 3)
                        vector.wait_ge(pe.h, pe_y[prev])
                    dv.inc(vector.tensor_mul(hgT[:, k * 512:(k + 1) * 512],
                                             u2b2[:, p * 512:(p + 1) * 512],
                                             PS[1 + 2 * p][:, 0:512]))
                    dv_hg[(e, tc, k)] = dv.n

            for dd in range(8):
                @block.tensor
                def _(tensor, e=e, tc=tc, dd=dd):
                    if dd == 0:
                        tensor.wait_ge(dv.h, dv_hg[(e, tc, 15)])
                    prevs = [(e, tc, dd - 2), (e, tc - 1, dd + 6), (e - 1, 3, dd + 6)]
                    for pk in prevs:
                        if pk in ac_y:
                            tensor.wait_ge(ac.h, ac_y[pk])
                            break
                    for kb in range(15):
                        tensor.matmul(PS[4 + dd % 2][:, 0:512],
                                      w2b[:, kb * D + dd * 128:kb * D + (dd + 1) * 128],
                                      hgT[:, kb * 512:(kb + 1) * 512],
                                      start=(kb == 0), stop=False)
                    pe.inc(tensor.matmul(PS[4 + dd % 2][:, 0:512],
                                         w2b[:, 15 * D + dd * 128:15 * D + (dd + 1) * 128],
                                         hgT[:, 15 * 512:16 * 512],
                                         start=False, stop=True))
                    pe_yd[(e, tc, dd)] = pe.n
                    if dd == 7:
                        pe_y[(e, tc)] = pe.n

                @block.scalar
                def _(scalar, e=e, tc=tc, dd=dd):
                    scalar.wait_ge(pe.h, pe_yd[(e, tc, dd)])
                    prevs = [(e, tc, dd - 2), (e, tc - 1, dd + 6), (e - 1, 3, dd + 6)]
                    for pk in prevs:
                        if pk in pe_tr4d:
                            scalar.wait_ge(pe.h, pe_tr4d[pk])
                            break
                    ac.inc(scalar.copy(yTb[:, dd * 512:(dd + 1) * 512],
                                       PS[4 + dd % 2][:, 0:512]))
                    ac_y[(e, tc, dd)] = ac.n

            for tb in range(4):
                for dd in range(8):
                    @block.tensor
                    def _(tensor, e=e, tc=tc, tb=tb, dd=dd):
                        kk = tb * 8 + dd
                        if tb == 0 and dd == 0:
                            tensor.wait_ge(ac.h, ac_y[(e, tc, 7)])
                        prevs = [(e, tc, kk - 2), (e, tc - 1, kk + 30), (e - 1, 3, kk + 30)]
                        for pk in prevs:
                            if pk in dv_c4:
                                tensor.wait_ge(dv.h, dv_c4[pk])
                                break
                        pe.inc(tensor.transpose(
                            PB[kk % 2][:, 0:128],
                            yTb[:, dd * 512 + tb * 128:dd * 512 + (tb + 1) * 128],
                            idn[:, :]))
                        pe_trk[(e, tc, kk)] = pe.n
                        if tb == 3 and dd == 7:
                            for d2 in range(8):
                                pe_tr4d[(e, tc, d2)] = pe.n

                    @block.vector
                    def _(vector, e=e, tc=tc, tb=tb, dd=dd):
                        kk = tb * 8 + dd
                        it = tc * 4 + tb
                        vector.wait_ge(pe.h, pe_trk[(e, tc, kk)])
                        if dd == 0:
                            prevs = [(e, tc, tb - 2), (e, tc - 1, tb + 2), (e - 1, 3, tb + 2)]
                            for pk in prevs:
                                if pk in acc_n:
                                    vector.wait_ge(accd.h, acc_n[pk])
                                    break
                        dv.inc(vector.tensor_scalar_mul(
                            stage[:, (tb % 2) * D + dd * 128:(tb % 2) * D + (dd + 1) * 128],
                            PB[kk % 2][:, 0:128],
                            w_all[:, it * 8 + e:it * 8 + e + 1]))
                        dv_c4[(e, tc, kk)] = dv.n

                @block.gpsimd
                def _(gpsimd, e=e, tc=tc, tb=tb):
                    it = tc * 4 + tb
                    gpsimd.wait_ge(dv.h, dv_c4[(e, tc, tb * 8 + 7)])
                    if e == 0 and tc == 0 and tb == 0:
                        gpsimd.wait_ge(stx1[0].h, stx1[0].n)
                    if tc == 0 and tb == 0 and e >= 1:
                        gpsimd.wait_ge(accd.h, acc_n[(e - 1, 3, 3)])
                    accd.dinc(gpsimd.dma_start(
                        out=dram["x1"][it * 128:(it + 1) * 128, :],
                        in_=stage[:, (tb % 2) * D:(tb % 2 + 1) * D],
                        accum_op=ALU.add))
                    acc_n[(e, tc, tb)] = accd.n

            if tc == 3:
                pe_done[e] = None

        pe_done[e] = pe_y[(e, 3)]

    if dbg:
        dbgo = ck.dout("dbg_w", [128, 16 * 8], F32)
        dbgo2 = ck.dout("dbg_lg", [128, 16 * 8], F32)
        dbgo3 = ck.dout("dbg_cr", [1, 16 * 8], F32)
        dbgo4 = ck.dout("dbg_br", [1, 8], F32)
        dbgo5 = ck.dout("dbg_h", [128, DIN], F32)
        dbgo6 = ck.dout("dbg_hn", [128, DIN], F32)
        dbgo7 = ck.dout("dbg_hnT", [128, DIN], BF16)
        dbgo8 = ck.dout("dbg_w3t", [128, 2 * D], BF16)
        sdb = ck.sem("p34_sdb")

        @block.gpsimd
        def _(gpsimd):
            gpsimd.wait_ge(dv.h, dv_w_all)
            sdb.dinc(gpsimd.dma_start(out=dbgo5[:, :], in_=hbuf[:, :]))
            sdb.dinc(gpsimd.dma_start(out=dbgo6[:, :], in_=hnb[:, :]))
            sdb.dinc(gpsimd.dma_start(out=dbgo7[:, :], in_=hnT[:, :]))
            sdb.dinc(gpsimd.dma_start(out=dbgo8[:, :], in_=w3T[:, 0:2 * D]))
            sdb.dinc(gpsimd.dma_start(out=dbgo[:, :], in_=w_all[:, :]))
            sdb.dinc(gpsimd.dma_start(out=dbgo2[:, :], in_=logits_t[:, :]))
            sdb.dinc(gpsimd.dma_start(out=dbgo3[:, :], in_=crows[:, :]))
            sdb.dinc(gpsimd.dma_start(out=dbgo4[:, :], in_=brow[:, :]))
            gpsimd.wait_ge(sdb.h, sdb.n)

    @block.gpsimd
    def _(gpsimd):
        gpsimd.wait_ge(accd.h, accd.n)
        for sm in stx1:
            gpsimd.wait_ge(sm.h, sm.n)
    return {}


"""Phase 6: mLSTM attention. One (batch, head) per core: b = core//4, h = core%4."""

from concourse.aot_env import allow_ici_all_to_all

DH = 256
NLS = 16
NST = 64
VW = 260


def host_inputs_p6(I, core):
    b, h = divmod(core, 4)
    wq = np.asarray(I["L1_attn_wq"]); wk = np.asarray(I["L1_attn_wk"]); wv = np.asarray(I["L1_attn_wv"])
    wqkv = np.zeros((1024, 768), np.float32)
    for j in range(64):
        blk = 64 * h + j
        rr = 256 * h + 4 * j
        wqkv[rr:rr + 4, 4 * j:4 * j + 4] = wq[blk].T
        wqkv[rr:rr + 4, 256 + 4 * j:256 + 4 * j + 4] = wk[blk].T
        wqkv[rr:rr + 4, 512 + 4 * j:512 + 4 * j + 4] = wv[blk].T
    wi3 = np.ascontiguousarray(np.asarray(I["L1_attn_wi"])[h].reshape(3, 1024))
    wf3 = np.ascontiguousarray(np.asarray(I["L1_attn_wf"])[h].reshape(3, 1024))
    bi_s = np.full((1, 1), np.asarray(I["L1_attn_bi"])[h], np.float32)
    bf_s = np.full((1, 1), np.asarray(I["L1_attn_bf"])[h], np.float32)
    gam = np.asarray(I["L1_attn_ln_w"])[h * 256:(h + 1) * 256].astype(np.float32)
    gamma_bc = np.ascontiguousarray(np.broadcast_to(gam[None, :], (128, 256)))
    um = np.zeros((4, 128, 512), np.float32)
    for v in range(4):
        for p in range(128):
            um[v, p, :v * 128 + p] = -3e38
    idn = np.eye(128, dtype=np.float32)
    import ml_dtypes
    return {"wqkv": wqkv, "wi3": wi3, "wf3": wf3, "bi_s": bi_s, "bf_s": bf_s,
            "gamma_bc": gamma_bc, "umaskd": um,
            "idn128": idn.astype(ml_dtypes.bfloat16)}


def p6_dram_decls(ck, kind="ExternalOutput", xgin="ExternalInput", ngroup=4):
    d = {}
    if xgin == "ExternalInput":
        d["xg"] = ck.din("xg", [ngroup * D, T], BF16)
        d["idn128"] = ck.din("idn128", [128, 128], BF16)
    d["wqkv"] = ck.din("wqkv", [D, 768], F32)
    d["wi3"] = ck.din("wi3", [3, 1024], F32)
    d["wf3"] = ck.din("wf3", [3, 1024], F32)
    d["bi_s"] = ck.din("bi_s", [1, 1], F32)
    d["bf_s"] = ck.din("bf_s", [1, 1], F32)
    d["gamma_bc"] = ck.din("gamma_bc", [128, 256], F32)
    d["umaskd"] = ck.din("umaskd", [4, 128, 512], F32)
    mk = ck.dout if kind == "ExternalOutput" else ck.dint
    d["a1s"] = mk("a1s", [ngroup * T, DH], F32)
    d["a1l"] = mk("a1l", [T, D], F32)
    d["cc6_in"] = ck.dint("cc6_in", [ngroup * T, DH], F32)
    d["cc6_out"] = ck.dint("cc6_out", [ngroup * ngroup * T, DH], F32)
    return d


def emit_phase6(ck, block, dram, groups=None, ngroup=4, myq_masks=None):
    nc = ck.nc
    pe = ck.sem("g_pe")
    dv = ck.sem("g_dv")
    ac = ck.sem("g_ac")
    ev = ck.sem("g_ev")
    xl6 = [ck.sem(f"p6_xl{j}") for j in range(2)]
    wl6 = ck.sem("p6_wl")
    st6 = [ck.sem(f"p6_st{j}") for j in range(2)]
    cc6 = ck.sem("p6_cc")
    ccd6 = ck.sem("p6_ccd")
    SEQ = ngroup * T

    qT = ck.sb("p6_qT", [128, 2 * SEQ], BF16)
    kT = ck.sb("p6_kT", [128, 2 * SEQ], BF16)
    vs = ck.sb("p6_vs", [128, (SEQ // 128) * VW], BF16)
    R0 = ck.sb("p6_R0", [1, SEQ], F32)
    ar6 = ck.sb("p6_ar6", [128, SEQ], F32)
    R1 = ar6[0:1, 0:SEQ]
    R2 = ar6[32:33, 0:SEQ]
    wi3 = ar6[0:3, 0:2048]
    wgate = ck.sb("p6_wg", [128, 2], BF16)
    bib = ck.sb("p6_bib", [1, 2], F32)
    gamb = ck.sb("p6_gam", [128, 256], F32)
    umaskd = ck.sb("p6_um", [128, 4 * 512], F32)
    idn = ck.sb("p6_idn", [128, 128], BF16)
    idnf = ck.sb("p6_idnf", [128, 128], F32)
    onesr = ck.sb("p6_onesr", [1, 128], F32)
    rm16 = ck.sb("p6_rm16", [1, 16], F32)
    r_col = ck.sb("p6_rcol", [128, NST], F32)
    m_col = ck.sb("p6_mcol", [128, NST], F32)
    em_col = ck.sb("p6_emcol", [128, NST], F32)
    er_all = ck.sb("p6_erall", [128, NLS * NST], F32)
    growb = ck.sb("p6_growb", [128, 512], F32)
    diagb = ck.sb("p6_diagb", [128, 512], F32)
    q2 = ck.sb("p6_q2", [128, 2 * 512], BF16)
    SD = ck.sb("p6_sd", [128, 2 * 512], BF16)
    argd = ck.sb("p6_argd", [128, 512], F32)
    a1st = ck.sb("p6_a1st", [128, 2 * 256], F32)
    est = ck.sb("p6_est", [128, 16], F32)
    ones3 = ck.sb("p6_ones3", [3, 1], F32)
    wgc = ck.sb("p6_wgc", [128, 16], BF16)
    bcol = ck.sb("p6_bcol", [128, 2], F32)
    g_wqkv = ck.nc.sbuf_tensor("p6_wqkv", [128, 8 * 768], BF16)
    wqkv = g_wqkv.__enter__()
    g_xgc = ck.nc.sbuf_tensor("p6_xgc", [128, 8 * 512], BF16)
    xgc = g_xgc.__enter__()
    PS = ck.PS
    PB = ck.PB

    @block.gpsimd
    def _(gpsimd):
        wl6.dinc(gpsimd.dma_start(out=idn[:, :], in_=dram["idn128"][:, :]))
        for kt in range(8):
            wl6.dinc(gpsimd.dma_start(out=wqkv[:, kt * 768:(kt + 1) * 768],
                     in_=dram["wqkv"][kt * 128:(kt + 1) * 128, :]))
        wl6.dinc(gpsimd.dma_start(out=wi3[0:3, 0:1024], in_=dram["wi3"][:, :]))
        wl6.dinc(gpsimd.dma_start(out=wi3[0:3, 1024:2048], in_=dram["wf3"][:, :]))
        wl6.dinc(gpsimd.dma_start(out=bib[:, 0:1], in_=dram["bi_s"][:, :]))
        wl6.dinc(gpsimd.dma_start(out=bib[:, 1:2], in_=dram["bf_s"][:, :]))
        wl6.dinc(gpsimd.dma_start(out=gamb[:, :], in_=dram["gamma_bc"][:, :]))
        wl6.dinc(gpsimd.dma_start(
            out=umaskd[:, :].rearrange("p (v x) -> p v x", v=4),
            in_=dram["umaskd"].ap().rearrange("v p x -> p v x")))
    wl6_a = wl6.n

    dv0 = dv.n
    @block.vector
    def _(vector):
        vector.wait_ge(wl6.h, wl6_a)
        vector.memset(onesr[:, :], 1.0)
        vector.memset(vs[:, :].rearrange("p (t w) -> p t w", w=VW)[:, :, 256:260], 0.0)
        vector.memset(vs[:, :].rearrange("p (t w) -> p t w", w=VW)[:, :, 256:257], 1.0)
        vector.tensor_copy(idnf[:, :], idn[:, :])
        dv.inc(vector.tensor_scalar(gamb[:, :], gamb[:, :], 1.0, None, ALU.add))
    dv_init = dv.n

    dv_o3 = dv.n
    @block.vector
    def _(vector):
        dv.inc(vector.memset(ones3[:, :], 1.0))
    dv_o3 = dv.n

    pe_gsum_d, dv_gsum_d = {}, {}
    for seg in range(4):
        @block.tensor
        def _(tensor, seg=seg):
            if seg == 0:
                tensor.wait_ge(dv.h, dv_o3)
                tensor.wait_ge(wl6.h, wl6_a)
            if seg >= 2:
                tensor.wait_ge(dv.h, dv_gsum_d[seg - 2])
            pe.inc(tensor.matmul(PS[4 + seg % 2][0:1, 0:512], ones3[:, :],
                                 wi3[0:3, seg * 512:(seg + 1) * 512],
                                 start=True, stop=True))
            pe_gsum_d[seg] = pe.n

        @block.vector
        def _(vector, seg=seg):
            vector.wait_ge(pe.h, pe_gsum_d[seg])
            dv.inc(vector.tensor_copy(wi3[0:1, seg * 512:(seg + 1) * 512],
                                      PS[4 + seg % 2][0:1, 0:512]))
            dv_gsum_d[seg] = dv.n
    dv_gsum = dv.n

    pe_gt, dv_gt = {}, {}
    for k in range(16):
        g, jd = divmod(k, 8)

        @block.tensor
        def _(tensor, g=g, jd=jd, k=k):
            if k == 0:
                tensor.wait_ge(dv.h, dv_gsum)
                tensor.wait_ge(dv.h, dv_init)
            if k >= 2:
                tensor.wait_ge(dv.h, dv_gt[k - 2])
            pe.inc(tensor.transpose(PS[5][0:128, k % 2:k % 2 + 1],
                                    wi3[0:1, g * 1024 + jd * 128:g * 1024 + (jd + 1) * 128],
                                    idnf[0:1, 0:1]))
            pe_gt[k] = pe.n

        @block.vector
        def _(vector, g=g, jd=jd, k=k):
            vector.wait_ge(pe.h, pe_gt[k])
            dv.inc(vector.tensor_copy(wgc[:, g * 8 + jd:g * 8 + jd + 1],
                                      PS[5][0:128, k % 2:k % 2 + 1]))
            dv_gt[k] = dv.n
    dv_wgc = dv.n

    pe_qk, ac_qk, dv_qk = {}, {}, {}
    ig_row = R1
    fg_row = R0[0:1, 0:SEQ]
    for c in range(16):
        r = c // 4

        @block.gpsimd
        def _(gpsimd, c=c, r=r):
            if c >= 1:
                gpsimd.wait_ge(pe.h, pe_qk[(c - 1, 'g')])
            for jd in range(8):
                xl6[c % 2].dinc(gpsimd.dma_start(
                    out=xgc[:, jd * 512:(jd + 1) * 512] if c % 2 == 0 else
                        xgc[:, jd * 512:(jd + 1) * 512],
                    in_=dram["xg"][r * D + jd * 128:r * D + (jd + 1) * 128,
                                   (c % 4) * 512:(c % 4 + 1) * 512]))

        @block.tensor
        def _(tensor, c=c):
            tensor.wait_ge(xl6[c % 2].h, 16 * 8 * (c // 2 + 1))
            if c == 0:
                tensor.wait_ge(dv.h, dv_wgc)
            else:
                tensor.wait_ge(dv.h, dv_qk[(c - 1, 'gev')])
            for qk in range(2):
                for mt in range(2):
                    bank = qk * 2 + mt
                    for kt in range(8):
                        mm = tensor.matmul(
                            PS[bank][:, 0:512],
                            wqkv[:, kt * 768 + qk * 256 + mt * 128:kt * 768 + qk * 256 + (mt + 1) * 128],
                            xgc[:, kt * 512:(kt + 1) * 512],
                            start=(kt == 0), stop=(kt == 7))
            pe.inc(mm)
            pe_qk[(c, 'qk')] = pe.n
            for st in range(4):
                for kt in range(8):
                    mm = tensor.matmul(
                        PS[4][:, st * 320:st * 320 + 256] if False else PS[4][:, st * 128:st * 128 + 256] if False else PS[4 + st % 2][:, (st // 2) * 256:(st // 2) * 256 + 256],
                        xgc[:, kt * 512 + st * 128:kt * 512 + st * 128 + 128],
                        wqkv[:, kt * 768 + 512:kt * 768 + 768],
                        start=(kt == 0), stop=(kt == 7))
            pe.inc(mm)
            pe_qk[(c, 'v')] = pe.n
            pe_qk[(c, 'all')] = pe.n

        @block.vector
        def _(vector, c=c):
            vector.wait_ge(pe.h, pe_qk[(c, 'qk')])
            for qk in range(2):
                for mt in range(2):
                    bank = qk * 2 + mt
                    dst = qT if qk == 0 else kT
                    dv.inc(vector.tensor_copy(
                        dst[:, mt * SEQ + c * 512:mt * SEQ + (c + 1) * 512],
                        PS[bank][:, 0:512]))
            vector.wait_ge(pe.h, pe_qk[(c, 'v')])
            for st in range(4):
                dv.inc(vector.tensor_copy(
                    vs[:, (c * 4 + st) * VW:(c * 4 + st) * VW + 256],
                    PS[4 + st % 2][:, (st // 2) * 256:(st // 2) * 256 + 256]))
            dv_qk[(c, 'ev')] = dv.n

        @block.tensor
        def _(tensor, c=c):
            tensor.wait_ge(dv.h, dv_qk[(c, 'ev')])
            for g in range(2):
                for kt in range(8):
                    mm = tensor.matmul(PS[g][0:1, 0:512],
                                       wgc[:, g * 8 + kt:g * 8 + kt + 1],
                                       xgc[:, kt * 512:(kt + 1) * 512],
                                       start=(kt == 0), stop=(kt == 7))
            pe.inc(mm)
            pe_qk[(c, 'g')] = pe.n

        @block.vector
        def _(vector, c=c):
            vector.wait_ge(pe.h, pe_qk[(c, 'g')])
            vector.tensor_scalar(ig_row[0:1, c * 512:(c + 1) * 512],
                                 PS[0][0:1, 0:512], bib[0:1, 0:1], None, ALU.add)
            dv.inc(vector.tensor_scalar(fg_row[0:1, c * 512:(c + 1) * 512],
                                        PS[1][0:1, 0:512], bib[0:1, 1:2], None,
                                        ALU.add))
            dv_qk[(c, 'gev')] = dv.n
    pe_qkv_all = pe.n
    dv_qkv_all = dv.n

    if getattr(ck, 'p6_stage', 99) <= 1:
        g_xgc.__exit__(None, None, None)
        g_wqkv.__exit__(None, None, None)

        @block.sync
        def _(sync):
            sync.wait_ge(dv.h, dv.n)
            sync.wait_ge(pe.h, pe.n)
        return {}

    R3 = ar6[64:65, 0:SEQ]
    growr = argd[0:1, 0:512]
    LN16 = 2.772588722239781
    g_xgc.__exit__(None, None, None)
    g_wqkv.__exit__(None, None, None)

    NCH = 32
    CW = SEQ // NCH
    rtmp = ck.sb("p6_rtmp", [1, 4 * CW], F32)
    rtmp2 = ck.sb("p6_rtmp2", [1, 2 * CW + 8], F32)
    tF = rtmp2[0:1, 0:CW]
    z5 = rtmp2[0:1, CW:2 * CW]
    cF = rtmp2[0:1, 2 * CW:2 * CW + 1]
    cM = rtmp2[0:1, 2 * CW + 1:2 * CW + 2]
    ninit = rtmp2[0:1, 2 * CW + 2:2 * CW + 3]

    dv_pre = dv.n
    @block.vector
    def _(vector):
        vector.wait_ge(dv.h, dv_qkv_all)
        vector.memset(z5[:, :], 0.0)
        vector.memset(cF[:, :], 0.0)
        dv.inc(vector.memset(cM[:, :], -1e30))
    dv_pre = dv.n

    ac_rc, dv_rc = {}, {}
    for c in range(NCH):
        p = c % 2
        t1 = rtmp[0:1, (2 * p) * CW:(2 * p + 1) * CW]
        t2 = rtmp[0:1, (2 * p + 1) * CW:(2 * p + 2) * CW]
        fgc = fg_row[0:1, c * CW:(c + 1) * CW]
        igc = ig_row[0:1, c * CW:(c + 1) * CW]
        mc = R3[0:1, c * CW:(c + 1) * CW]

        @block.scalar
        def _(scalar, c=c, t1=t1, t2=t2, fgc=fgc):
            if c == 0:
                scalar.wait_ge(dv.h, dv_qkv_all)
            if c >= 2:
                scalar.wait_ge(dv.h, dv_rc[c - 2])
            scalar.activation(out=t1, in_=fgc, func=AF.Abs)
            scalar.activation(out=t2, in_=t1, func=AF.Exp, scale=-1.0)
            scalar.activation(out=t1, in_=t2, func=AF.Ln, bias=1.0)
            ac.inc(scalar.activation(out=t2, in_=fgc, func=AF.Relu, scale=-1.0))
            ac_rc[c] = ac.n

        @block.vector
        def _(vector, c=c, t1=t1, t2=t2, fgc=fgc, igc=igc, mc=mc):
            vector.wait_ge(ac.h, ac_rc[c])
            if c == 0:
                vector.wait_ge(dv.h, dv_pre)
            dv.inc(vector.tensor_add(t1, t1, t2)); dv.flush(vector)
            dv.inc(vector.tensor_scalar_mul(fgc, t1, -1.0)); dv.flush(vector)
            dv.inc(vector.tensor_tensor_scan(tF, fgc, z5, 0.0, ALU.add, ALU.add))
            dv.flush(vector)
            dv.inc(vector.tensor_scalar(tF, tF, cF, None, ALU.add)); dv.flush(vector)
            dv.inc(vector.tensor_copy(cF, tF[0:1, CW - 1:CW])); dv.flush(vector)
            dv.inc(vector.tensor_sub(igc, igc, tF)); dv.flush(vector)
            dv.inc(vector.tensor_tensor_scan(t2, igc, z5, -1e30, ALU.max, ALU.add))
            dv.flush(vector)
            dv.inc(vector.tensor_scalar(t2, t2, cM, None, ALU.max)); dv.flush(vector)
            dv.inc(vector.tensor_copy(cM, t2[0:1, CW - 1:CW])); dv.flush(vector)
            dv.inc(vector.tensor_add(t1, tF, t2)); dv.flush(vector)
            dv.inc(vector.tensor_copy(mc, t1)); dv.flush(vector)
            dv.inc(vector.tensor_copy(fgc, t2))
            dv_rc[c] = dv.n

    @block.vector
    def _(vector):
        dv.flush(vector)
        dv.inc(vector.tensor_copy(rm16[:, :], fg_row[0:1, 0:SEQ:512]))
        dv.flush(vector)
        dv.inc(vector.tensor_scalar_mul(R0[:, :], fg_row, -1.0))
        dv.flush(vector)
    dv_rows = dv.n

    if getattr(ck, 'p6_stage', 99) <= 2:
        @block.sync
        def _(sync):
            sync.wait_ge(dv.h, dv.n)
            sync.wait_ge(pe.h, pe.n)
            sync.wait_ge(ac.h, ac.n)
        return {}

    rscr = ck.dint("p6_rscr", [2, SEQ], F32)
    r2d = ck.sb("p6_r2d", [64, 128], F32)
    trs = ck.sem("p6_trs")
    pe_tc, dv_tc = {}, {}

    @block.sync
    def _(sync):
        sync.wait_ge(dv.h, dv_rows)
        trs.dinc(sync.dma_start(out=rscr[0:1, :], in_=ig_row))
        trs.dinc(sync.dma_start(out=rscr[1:2, :], in_=R3[0:1, :]))
        sync.wait_ge(trs.h, trs.n)

    for w in range(2):
        @block.sync
        def _(sync, w=w):
            if w == 1:
                sync.wait_ge(pe.h, pe_tc[0])
            trs.dinc(sync.dma_start(
                out=r2d[:, :],
                in_=rscr.ap()[w:w + 1, :].rearrange("o (st p) -> (o st) p", p=128)))

        @block.tensor
        def _(tensor, w=w):
            tensor.wait_ge(trs.h, 32 + 16 * (w + 1))
            if w == 1:
                tensor.wait_ge(dv.h, dv_tc[0])
            pe.inc(tensor.transpose(PS[5][0:128, 0:64], r2d[:, :],
                                    idnf[0:64, 0:64]))
            pe_tc[w] = pe.n

        @block.vector
        def _(vector, w=w):
            vector.wait_ge(pe.h, pe_tc[w])
            dst = r_col if w == 0 else m_col
            dv.inc(vector.tensor_copy(dst[:, :], PS[5][0:128, 0:64]))
            dv_tc[w] = dv.n
    dv_tabs = dv.n
    tab_n = trs.n

    if getattr(ck, 'p6_stage', 99) <= 2.5:
        @block.sync
        def _(sync):
            sync.wait_ge(dv.h, dv.n)
            sync.wait_ge(pe.h, pe.n)
            sync.wait_ge(ac.h, ac.n)
            sync.wait_ge(trs.h, tab_n)
        return {}

    ac_em = ac.n
    @block.scalar
    def _(scalar):
        scalar.wait_ge(dv.h, dv_tabs)
        ac.inc(scalar.activation(out=em_col[:, :], in_=m_col[:, :], func=AF.Exp,
                                 scale=-1.0))
    ac_em = ac.n

    pe_er, dv_er, ac_er = {}, {}, {}
    for L in range(NLS):
        @block.tensor
        def _(tensor, L=L):
            if L == 0:
                tensor.wait_ge(dv.h, dv_tabs)
            if L >= 2:
                tensor.wait_ge(ac.h, ac_er[L - 2])
            pe.inc(tensor.matmul(PS[4][0:128, L % 2:L % 2 + 1], onesr[:, :],
                                 R0[0:1, L * 512:L * 512 + 1], start=True, stop=True))
            pe_er[L] = pe.n

        @block.vector
        def _(vector, L=L):
            vector.wait_ge(pe.h, pe_er[L])
            dv.inc(vector.tensor_copy(bcol[:, L % 2:L % 2 + 1],
                                      PS[4][0:128, L % 2:L % 2 + 1]))
            dv_er[L] = dv.n

        @block.vector
        def _(vector, L=L):
            vector.wait_ge(dv.h, dv_er[L])
            sl = er_all[:, L * NST:(L + 1) * NST]
            vector.tensor_scalar(sl, r_col[:, :], bcol[:, L % 2:L % 2 + 1], None,
                                 ALU.add)
            vector.tensor_relu(est[:, 0:NST] if False else m_col[:, :] if False else est[:, 0:16], sl[:, 0:16]) if False else None
            vector.tensor_relu(argd[:, 0:NST], sl)
            dv.inc(vector.tensor_sub(sl, sl, argd[:, 0:NST]))
            dv_er[(L, 'c')] = dv.n

        @block.scalar
        def _(scalar, L=L):
            scalar.wait_ge(dv.h, dv_er[(L, 'c')])
            ac.inc(scalar.activation(out=er_all[:, L * NST:(L + 1) * NST],
                                     in_=er_all[:, L * NST:(L + 1) * NST],
                                     func=AF.Exp))
            ac_er[L] = ac.n
    ac_er_all = ac.n

    if getattr(ck, 'p6_stage', 99) <= 3:
        @block.sync
        def _(sync):
            sync.wait_ge(dv.h, dv.n)
            sync.wait_ge(pe.h, pe.n)
            sync.wait_ge(ac.h, ac.n)
        return {}

    if getattr(ck, 'p6_dbg', False):
        ck.p6_dsd = ck.dout("dbg_sd", [128, 512], BF16)
        ck.p6_dar = ck.dout("dbg_ar", [128, 512], F32)
    pe_m1, pe_m2, dv_sd, ac_pre, dv_pre = {}, {}, {}, {}, {}
    dv_epi = {}
    for L in range(NLS):
        @block.tensor
        def _(tensor, L=L):
            if L == 0:
                tensor.wait_ge(ac.h, ac_er_all)
                tensor.wait_ge(dv.h, dv_qkv_all)
            if L >= 1:
                tensor.wait_ge(dv.h, dv_epi[L - 1])
            pe.inc(tensor.matmul(PS[4][:, 0:512], onesr[:, :],
                                 R0[0:1, L * 512:(L + 1) * 512], start=True, stop=True))
            pe_m1[(L, 'pro')] = pe.n

        @block.scalar
        def _(scalar, L=L):
            scalar.wait_ge(dv.h, dv_rows)
            ac.inc(scalar.activation(out=growr[:, :], in_=R0[0:1, L * 512:(L + 1) * 512],
                                     func=AF.Exp, bias=rm16[0:1, L:L + 1]))
            ac_pre[L] = ac.n

        @block.tensor
        def _(tensor, L=L):
            tensor.wait_ge(ac.h, ac_pre[L])
            pe.inc(tensor.matmul(PS[5][:, 0:512], onesr[:, :], growr[:, :],
                                 start=True, stop=True))
            pe_m1[(L, 'pro2')] = pe.n

        @block.vector
        def _(vector, L=L):
            vector.wait_ge(pe.h, pe_m1[(L, 'pro2')])
            vector.tensor_scalar(diagb[:, :], PS[4][:, 0:512], 1.0, -LN16,
                                 ALU.mult, ALU.add)
            dv.inc(vector.tensor_scalar_mul(growb[:, :], PS[5][:, 0:512], 1.0 / 16.0))
            dv.flush(vector)
            for kt in range(2):
                dv.inc(vector.tensor_mul(q2[:, kt * 512:(kt + 1) * 512],
                                         qT[:, kt * SEQ + L * 512:kt * SEQ + (L + 1) * 512],
                                         growb[:, :]))
            dv_pre[L] = dv.n

        njs = 4 * L + 4

        def emit_m2(tensor, L, j, final=False):
            pp = j % 2
            tensor.wait_ge(dv.h, dv_sd[(L, j)])
            for lt in range(4):
                mm = tensor.matmul(
                    PS[lt][:, 0:257],
                    SD[:, pp * 512 + lt * 128:pp * 512 + (lt + 1) * 128],
                    vs[:, j * VW:j * VW + 257],
                    start=(j == 0), stop=final)
            pe.inc(mm)
            pe_m2[(L, j)] = pe.n

        for j in range(njs):
            diag = j >= 4 * L
            pp = j % 2

            @block.tensor
            def _(tensor, L=L, j=j, diag=diag, pp=pp, njs=njs, emit_m2=emit_m2):
                if j == 0:
                    tensor.wait_ge(dv.h, dv_pre[L])
                if (L, j - 2) in dv_sd:
                    tensor.wait_ge(dv.h, dv_sd[(L, j - 2)])
                for kt in range(2):
                    mm = tensor.matmul(
                        PS[4 + pp][:, 0:512],
                        kT[:, kt * SEQ + j * 128:kt * SEQ + (j + 1) * 128],
                        q2[:, kt * 512:(kt + 1) * 512] if not diag else
                        qT[:, kt * SEQ + L * 512:kt * SEQ + (L + 1) * 512],
                        start=(kt == 0), stop=(kt == 1))
                pe.inc(mm)
                pe_m1[(L, j)] = pe.n
                if j >= 1:
                    emit_m2(tensor, L, j - 1)

            if diag:
                @block.vector
                def _(vector, L=L, j=j, pp=pp):
                    v = j - 4 * L
                    dv.inc(vector.tensor_add(argd[:, :], diagb[:, :],
                                             umaskd[:, v * 512:(v + 1) * 512]))
                    dv_sd[(L, j, 'arg')] = dv.n

                if getattr(ck, 'p6_dbg', False) and L == 0 and j == 0:
                    sdb0 = ck.sem("p6_sdb0")

                    @block.sync
                    def _(sync, L=L, j=j):
                        sync.wait_ge(dv.h, dv_sd[(L, j, 'arg')])
                        sdb0.dinc(sync.dma_start(out=ck.p6_dar[:, :], in_=argd[:, :]))

                @block.scalar
                def _(scalar, L=L, j=j):
                    scalar.wait_ge(dv.h, dv_sd[(L, j, 'arg')])
                    ac.inc(scalar.activation(out=argd[:, :], in_=argd[:, :],
                                             func=AF.Exp, bias=r_col[:, j:j + 1]))
                    ac_pre[(L, j)] = ac.n

                @block.vector
                def _(vector, L=L, j=j, pp=pp):
                    vector.wait_ge(ac.h, ac_pre[(L, j)])
                    vector.wait_ge(pe.h, pe_m1[(L, j)])
                    if (L, j - 2) in pe_m2:
                        vector.wait_ge(pe.h, pe_m2[(L, j - 2)])
                    dv.inc(vector.tensor_mul(SD[:, pp * 512:(pp + 1) * 512],
                                             argd[:, :], PS[4 + pp][:, 0:512]))
                    dv_sd[(L, j)] = dv.n

                if getattr(ck, 'p6_dbg', False) and L == 0 and j == 0:
                    sdb1 = ck.sem("p6_sdb1")

                    @block.sync
                    def _(sync, L=L, j=j, pp=pp):
                        sync.wait_ge(dv.h, dv_sd[(L, j)])
                        sdb1.dinc(sync.dma_start(out=ck.p6_dsd[:, :],
                                                 in_=SD[:, pp * 512:(pp + 1) * 512]))
            else:
                @block.vector
                def _(vector, L=L, j=j, pp=pp):
                    vector.wait_ge(pe.h, pe_m1[(L, j)])
                    if (L, j - 2) in pe_m2:
                        vector.wait_ge(pe.h, pe_m2[(L, j - 2)])
                    dv.inc(vector.tensor_scalar_mul(
                        SD[:, pp * 512:(pp + 1) * 512], PS[4 + pp][:, 0:512],
                        er_all[:, L * NST + j:L * NST + j + 1]))
                    dv_sd[(L, j)] = dv.n

            if j == njs - 1:
                @block.tensor
                def _(tensor, L=L, j=j, emit_m2=emit_m2):
                    emit_m2(tensor, L, j, final=True)

        for lt in range(4):
            @block.vector
            def _(vector, L=L, lt=lt):
                vector.wait_ge(pe.h, pe_m2[(L, njs - 1)])
                ct = L * 4 + lt
                dv.inc(vector.tensor_copy(est[:, 0:1], PS[lt][:, 256:257]))
                dv.flush(vector)
                dv_epi[(L, lt, 'rs')] = dv.n

            @block.scalar
            def _(scalar, L=L, lt=lt):
                scalar.wait_ge(dv.h, dv_epi[(L, lt, 'rs')])
                ac.inc(scalar.activation(out=est[:, 1:2], in_=est[:, 0:1], func=AF.Abs))
                ac_pre[(L, lt, 'abs')] = ac.n

            @block.vector
            def _(vector, L=L, lt=lt):
                ct = L * 4 + lt
                vector.wait_ge(ac.h, ac_pre[(L, lt, 'abs')])
                dv.inc(vector.tensor_tensor(est[:, 2:3], est[:, 1:2],
                                            em_col[:, ct:ct + 1], ALU.max))
                dv.flush(vector)
                dv.inc(vector.tensor_scalar(est[:, 2:3], est[:, 2:3], 1e-8, None,
                                            ALU.add))
                dv.flush(vector)
                dv.inc(vector.reciprocal(est[:, 3:4], est[:, 2:3]))
                dv.flush(vector)
                sl = lt % 2
                dv.inc(vector.tensor_scalar_mul(a1st[:, sl * 256:(sl + 1) * 256],
                                                PS[lt][:, 0:256], est[:, 3:4]))
                if getattr(ck, 'p6_dbg', False) and L == 0 and lt == 0:
                    vector.tensor_copy(diagb[:, 0:256], PS[lt][:, 0:256])
                    vector.tensor_copy(diagb[:, 256:258], est[:, 0:2])
                dv.inc(vector.reduce_sum(est[:, 4:5], a1st[:, sl * 256:(sl + 1) * 256],
                                         axis=AX.X))
                dv.flush(vector)
                dv.inc(vector.tensor_scalar_mul(est[:, 5:6], est[:, 4:5], -1.0 / 256))
                dv_epi[(L, lt, 'mu')] = dv.n

            @block.scalar
            def _(scalar, L=L, lt=lt):
                scalar.wait_ge(dv.h, dv_epi[(L, lt, 'mu')])
                sl = lt % 2
                ac.inc(scalar.activation(out=est[:, 8:8 + 128] if False else argd[:, 0:256],
                                         in_=a1st[:, sl * 256:(sl + 1) * 256],
                                         func=AF.Square, bias=est[:, 5:6]))
                ac_pre[(L, lt, 'sq')] = ac.n

            @block.vector
            def _(vector, L=L, lt=lt):
                vector.wait_ge(ac.h, ac_pre[(L, lt, 'sq')])
                dv.inc(vector.reduce_sum(est[:, 6:7], argd[:, 0:256], axis=AX.X))
                dv.flush(vector)
                dv.inc(vector.tensor_scalar(est[:, 6:7], est[:, 6:7], 1.0 / 256, 1e-5,
                                            ALU.mult, ALU.add))
                dv.flush(vector)
                dv.inc(vector.reciprocal(est[:, 7:8], est[:, 6:7]))
                dv_epi[(L, lt, 'v')] = dv.n

            @block.scalar
            def _(scalar, L=L, lt=lt):
                scalar.wait_ge(dv.h, dv_epi[(L, lt, 'v')])
                ac.inc(scalar.activation(out=est[:, 7:8], in_=est[:, 7:8], func=AF.Sqrt))
                ac_pre[(L, lt, 'rs2')] = ac.n

            @block.vector
            def _(vector, L=L, lt=lt):
                vector.wait_ge(ac.h, ac_pre[(L, lt, 'rs2')])
                sl = lt % 2
                ct = L * 4 + lt
                if ct >= 2:
                    vector.wait_ge(st6[sl].h, 16 * 2 * (ct // 2))
                vector.tensor_scalar(a1st[:, sl * 256:(sl + 1) * 256],
                                     a1st[:, sl * 256:(sl + 1) * 256],
                                     est[:, 5:6], est[:, 7:8], ALU.add, ALU.mult)
                dv.inc(vector.tensor_mul(a1st[:, sl * 256:(sl + 1) * 256],
                                         a1st[:, sl * 256:(sl + 1) * 256], gamb[:, :]))
                dv_epi[(L, lt)] = dv.n

            @block.sync
            def _(sync, L=L, lt=lt):
                sync.wait_ge(dv.h, dv_epi[(L, lt)])
                ct = L * 4 + lt
                sl = lt % 2
                st6[sl].dinc(sync.dma_start(
                    out=dram["a1s"][ct * 128:(ct + 1) * 128, :],
                    in_=a1st[:, sl * 256:(sl + 1) * 256]))
                st6[sl].dinc(sync.dma_start(
                    out=dram["cc6_in"][ct * 128:(ct + 1) * 128, :],
                    in_=a1st[:, sl * 256:(sl + 1) * 256]))
        dv_epi[L] = dv.n

    if getattr(ck, 'p6_dbg', False):
        dq = ck.dout("dbg_qT", [128, 2 * SEQ], BF16)
        dk = ck.dout("dbg_kT", [128, 2 * SEQ], BF16)
        dvs = ck.dout("dbg_vs", [128, 8 * VW], BF16)
        dr = ck.dout("dbg_r", [1, SEQ], F32)
        dm = ck.dout("dbg_m", [1, SEQ], F32)
        dnr = ck.dout("dbg_nrm", [1, SEQ], F32)
        der = ck.dout("dbg_er", [128, NLS * NST], F32)
        do2 = ck.dout("dbg_o2", [128, 512], F32)

        sdb6 = ck.sem("p6_sdb")

        @block.gpsimd
        def _(gpsimd):
            for sm in st6:
                gpsimd.wait_ge(sm.h, sm.n)
            sdb6.dinc(gpsimd.dma_start(out=dq[:, :], in_=qT[:, :]))
            sdb6.dinc(gpsimd.dma_start(out=dk[:, :], in_=kT[:, :]))
            sdb6.dinc(gpsimd.dma_start(out=dvs[:, :], in_=vs[:, 0:8 * VW]))
            sdb6.dinc(gpsimd.dma_start(out=dr[:, :], in_=ig_row))
            sdb6.dinc(gpsimd.dma_start(out=dm[:, :], in_=R3[0:1, :]))
            sdb6.dinc(gpsimd.dma_start(out=dnr[:, :], in_=R0[:, :]))
            sdb6.dinc(gpsimd.dma_start(out=der[:, :], in_=er_all[:, :]))
            sdb6.dinc(gpsimd.dma_start(out=do2[:, :], in_=diagb[:, 0:512]))
            gpsimd.wait_ge(sdb6.h, sdb6.n)

    import dataclasses as _dc
    import concourse.bass as bass

    if getattr(ck, 'p6_no_a1l', False):
        @block.gpsimd
        def _(gpsimd):
            for sm in st6:
                gpsimd.wait_ge(sm.h, sm.n)
        return {}

    @block.gpsimd
    def _(gpsimd):
        for sm in st6:
            gpsimd.wait_ge(sm.h, sm.n)
        gpsimd.collective_compute(
            "AllGather", ALU.bypass,
            replica_groups=groups or [[0, 1, 2, 3], [4, 5, 6, 7]],
            ins=[dram["cc6_in"].ap().opt()],
            outs=[dram["cc6_out"].ap().opt()],
        ).then_inc(cc6.h, 1)
        cc6.n += 1
        gpsimd.wait_ge(cc6.h, cc6.n)
        pid = gpsimd.partition_id()
        qv = gpsimd.scalar_reg_alu(ALU.mod, pid, ngroup)
        for r in range(4):
            src = dram["cc6_out"].ap()[bass.ds(qv * T, T), :]
            src = _dc.replace(src, offset=src.offset + r * ngroup * T * DH)
            ccd6.dinc(gpsimd.dma_start(
                out=dram["a1l"][:, r * 256:(r + 1) * 256],
                in_=src))
        gpsimd.wait_ge(ccd6.h, ccd6.n)
    return {}


"""Phase 7: L1 MoE (dense, top-2-of-softmax scaling) + shared expert + final sum."""


NE = 8


def host_inputs_p7(I, core):
    d = {}
    d["gwT"] = np.ascontiguousarray(np.asarray(I["L1_ffn_gate_w"]).T).astype(np.float32)
    d["gb_bc"] = np.ascontiguousarray(
        np.broadcast_to(np.asarray(I["L1_ffn_gate_b"])[None, :], (128, 8))).astype(np.float32)
    w1 = np.asarray(I["L1_ffn_w1"]); w3 = np.asarray(I["L1_ffn_w3"]); w2 = np.asarray(I["L1_ffn_w2"])
    d["w1T"] = np.ascontiguousarray(np.transpose(w1, (0, 2, 1)))
    d["w3T"] = np.ascontiguousarray(np.transpose(w3, (0, 2, 1)))
    d["w2T"] = np.ascontiguousarray(np.transpose(w2, (0, 2, 1)))
    d["b1c"] = np.ascontiguousarray(np.asarray(I["L1_ffn_b1"]).reshape(8, 8, 128).transpose(0, 2, 1)).astype(np.float32)
    d["b3c"] = np.ascontiguousarray(np.asarray(I["L1_ffn_b3"]).reshape(8, 8, 128).transpose(0, 2, 1)).astype(np.float32)
    d["b2bc"] = np.ascontiguousarray(
        np.broadcast_to(np.asarray(I["L1_ffn_b2"])[:, None, :], (8, 128, 1024))).astype(np.float32)
    d["ws1T"] = np.ascontiguousarray(np.asarray(I["L1_ffn_ws1"]).T).astype(np.float32)
    d["ws3T"] = np.ascontiguousarray(np.asarray(I["L1_ffn_ws3"]).T).astype(np.float32)
    d["ws2T"] = np.ascontiguousarray(np.asarray(I["L1_ffn_ws2"]).T).astype(np.float32)
    d["bs1c"] = np.ascontiguousarray(np.asarray(I["L1_ffn_bs1"]).reshape(16, 128).T).astype(np.float32)
    d["bs3c"] = np.ascontiguousarray(np.asarray(I["L1_ffn_bs3"]).reshape(16, 128).T).astype(np.float32)
    d["bs2bc"] = np.ascontiguousarray(
        np.broadcast_to(np.asarray(I["L1_ffn_bs2"])[None, :], (128, 1024))).astype(np.float32)
    idn = np.eye(128, dtype=np.float32)
    import ml_dtypes
    d["idn128"] = idn.astype(ml_dtypes.bfloat16)
    return d


def p7_dram_decls(ck, xin="ExternalInput"):
    d = {}
    if xin == "ExternalInput":
        d["xn1T"] = ck.din("xn1T", [D, T], BF16)
        d["a1l"] = ck.din("a1l", [T, D], F32)
        d["x1"] = ck.din("x1", [T, D], F32)
        d["idn128"] = ck.din("idn128", [128, 128], BF16)
    d["gwT"] = ck.din("gwT", [D, 8], F32)
    d["gb_bc"] = ck.din("gb_bc", [128, 8], F32)
    d["w1T"] = ck.din("w1T", [8, D, 1024], F32)
    d["w3T"] = ck.din("w3T", [8, D, 1024], F32)
    d["w2T"] = ck.din("w2T", [8, 1024, D], F32)
    d["b1c"] = ck.din("b1c", [8, 128, 8], F32)
    d["b3c"] = ck.din("b3c", [8, 128, 8], F32)
    d["b2bc"] = ck.din("b2bc", [8, 128, 1024], F32)
    d["ws1T"] = ck.din("ws1T", [D, 2048], F32)
    d["ws3T"] = ck.din("ws3T", [D, 2048], F32)
    d["ws2T"] = ck.din("ws2T", [2048, D], F32)
    d["bs1c"] = ck.din("bs1c", [128, 16], F32)
    d["bs3c"] = ck.din("bs3c", [128, 16], F32)
    d["bs2bc"] = ck.din("bs2bc", [128, 1024], F32)
    d["out"] = ck.dout("out", [T, D], F32)
    return d


def emit_phase7(ck, block, dram):
    nc = ck.nc
    pe = ck.sem("g_pe")
    dv = ck.sem("g_dv")
    ac = ck.sem("g_ac")
    ev = ck.sem("g_ev")
    gl7 = ck.sem("p7_gl")
    wl7 = [ck.sem(f"p7_wl{j}") for j in range(2)]
    bl7 = [ck.sem(f"p7_bl{j}") for j in range(2)]
    acc7 = ck.sem("p7_acc")
    stb = [ck.sem(f"p7_stb{j}") for j in range(2)]

    xn1T = ck.sb("p7_xn1T", [128, 8 * T], BF16)
    wbf = ck.sb("p7_wbf", [128, 48 * 1024], BF16)
    w1b = wbf[:, 0:8 * 1024]
    w3b = wbf[:, 8 * 1024:16 * 1024]
    w2b = wbf[:, 16 * 1024:24 * 1024]
    ws1b = wbf[:, 0:16 * 1024]
    ws3b = wbf[:, 16 * 1024:32 * 1024]
    ws2b = wbf[:, 32 * 1024:48 * 1024]
    hgT = ck.sb("p7_hgT", [128, 16 * 512], BF16)
    yTb = ck.sb("p7_yT", [128, 8 * 512], BF16)
    stage = ck.sb("p7_stage", [128, 2 * D], F32)
    t1b = ck.sb("p7_t1", [128, 2 * 512], F32)
    t2b = ck.sb("p7_t2", [128, 2 * 512], F32)
    xab = ck.sb("p7_xab", [128, 2 * D], F32)
    gwT = ck.sb("p7_gwT", [128, 8 * 8], BF16)
    gbb = ck.sb("p7_gbb", [128, 8], F32)
    lgT = ck.sb("p7_lgT", [8, T], F32)
    lg = ck.sb("p7_lg", [128, 16 * 8], F32)
    w_all = ck.sb("p7_w", [128, 16 * 8], F32)
    stat = ck.sb("p7_stat", [128, 8], F32)
    eq = ck.sb("p7_eq", [128, 8], F32)
    b1s = ck.sb("p7_b1s", [128, 2 * 8], F32)
    b2s = ck.sb("p7_b2s", [128, 2 * 1024], F32)
    idn = ck.sb("p7_idn", [128, 128], BF16)
    idnf = ck.sb("p7_idnf", [128, 128], F32)
    PS = ck.PS
    PB = ck.PB

    @block.gpsimd
    def _(gpsimd):
        gl7.dinc(gpsimd.dma_start(out=idn[:, :], in_=dram["idn128"][:, :]))
        for j in range(8):
            gl7.dinc(gpsimd.dma_start(out=xn1T[:, j * T:(j + 1) * T],
                     in_=dram["xn1T"][j * 128:(j + 1) * 128, :]))
        gl7.dinc(gpsimd.dma_start(
            out=gwT[:, :].rearrange("p (kt e) -> p kt e", kt=8),
            in_=dram["gwT"].ap().rearrange("(kt kp) e -> kp kt e", kp=128)))
        gl7.dinc(gpsimd.dma_start(out=gbb[:, :], in_=dram["gb_bc"][:, :]))
    gl7_a = gl7.n

    xl7 = [ck.sem(f"p7_xl{j}") for j in range(2)]
    dv_b = {}
    for i in range(16):
        sl = i % 2

        @block.sync
        def _(sync, i=i, sl=sl):
            if i >= 2:
                sync.wait_ge(dv.h, dv_b[i - 2])
            xl7[sl].dinc(sync.dma_start(out=xab[:, sl * D:(sl + 1) * D],
                         in_=dram["x1"][i * 128:(i + 1) * 128, :]))
            xl7[sl].dinc(sync.dma_start(out=stage[:, sl * D:(sl + 1) * D],
                         in_=dram["a1l"][i * 128:(i + 1) * 128, :]))

        @block.vector
        def _(vector, i=i, sl=sl):
            vector.wait_ge(xl7[sl].h, 32 * (i // 2 + 1))
            if i >= 2:
                vector.wait_ge(stb[sl].h, 16 * (i // 2))
            dv.inc(vector.tensor_add(stage[:, sl * D:(sl + 1) * D],
                                     stage[:, sl * D:(sl + 1) * D],
                                     xab[:, sl * D:(sl + 1) * D]))
            dv_b[i] = dv.n

        @block.sync
        def _(sync, i=i, sl=sl):
            sync.wait_ge(dv.h, dv_b[i])
            stb[sl].dinc(sync.dma_start(out=dram["out"][i * 128:(i + 1) * 128, :],
                         in_=stage[:, sl * D:(sl + 1) * D]))
    st_base0 = (stb[0].n, stb[1].n)

    pe_lg, dv_lgT = {}, {}

    @block.vector
    def _(vector):
        vector.wait_ge(gl7.h, gl7_a)
        dv.inc(vector.tensor_copy(idnf[:, :], idn[:, :]))
    dv_ms = dv.n

    for tc in range(4):
        @block.tensor
        def _(tensor, tc=tc):
            if tc == 0:
                tensor.wait_ge(gl7.h, gl7_a)
            if tc >= 2:
                tensor.wait_ge(dv.h, dv_lgT[tc - 2])
            for kb in range(7):
                tensor.matmul(PS[4 + tc % 2][0:8, 0:512], gwT[:, kb * 8:(kb + 1) * 8],
                              xn1T[:, kb * T + tc * 512:kb * T + (tc + 1) * 512],
                              start=(kb == 0), stop=False)
            pe.inc(tensor.matmul(PS[4 + tc % 2][0:8, 0:512], gwT[:, 7 * 8:8 * 8],
                                 xn1T[:, 7 * T + tc * 512:7 * T + (tc + 1) * 512],
                                 start=False, stop=True))
            pe_lg[tc] = pe.n

        @block.vector
        def _(vector, tc=tc):
            vector.wait_ge(pe.h, pe_lg[tc])
            dv.inc(vector.tensor_copy(lgT[:, tc * 512:(tc + 1) * 512],
                                      PS[4 + tc % 2][0:8, 0:512]))
            dv_lgT[tc] = dv.n

    pe_lt, ev_lt = {}, {}
    for i in range(16):
        @block.tensor
        def _(tensor, i=i):
            if i == 0:
                tensor.wait_ge(dv.h, dv_lgT[3])
                tensor.wait_ge(dv.h, dv_ms)
            if i >= 2:
                tensor.wait_ge(ev.h, ev_lt[i - 2])
            pe.inc(tensor.transpose(PS[4 + (i % 2)][0:128, 0:8],
                                    lgT[:, i * 128:(i + 1) * 128], idnf[0:8, 0:8]))
            pe_lt[i] = pe.n

        @block.vector
        def _(vector, i=i):
            vector.wait_ge(pe.h, pe_lt[i])
            ev.inc(vector.tensor_add(lg[:, i * 8:(i + 1) * 8],
                                     PS[4 + (i % 2)][0:128, 0:8], gbb[:, :]))
            ev_lt[i] = ev.n

    dv_rt, ac_rt = {}, {}
    for i in range(16):
        @block.vector
        def _(vector, i=i):
            vector.wait_ge(ev.h, ev_lt[i])
            sl_ = lg[:, i * 8:(i + 1) * 8]
            dv.inc(vector.reduce_max(stat[:, 0:1], sl_, axis=AX.X))
            dv.flush(vector)
            dv.inc(vector.tensor_scalar_mul(stat[:, 1:2], stat[:, 0:1], -1.0))
            dv_rt[(i, 'm')] = dv.n

        @block.scalar
        def _(scalar, i=i):
            scalar.wait_ge(dv.h, dv_rt[(i, 'm')])
            ac.inc(scalar.activation(out=eq[:, :], in_=lg[:, i * 8:(i + 1) * 8],
                                     func=AF.Exp, bias=stat[:, 1:2]))
            ac_rt[i] = ac.n

        @block.vector
        def _(vector, i=i):
            vector.wait_ge(ac.h, ac_rt[i])
            dv.inc(vector.reduce_sum(stat[:, 2:3], eq[:, :], axis=AX.X))
            dv.flush(vector)
            dv.inc(vector.reciprocal(stat[:, 3:4], stat[:, 2:3]))
            dv.flush(vector)
            sc = w_all[:, i * 8:(i + 1) * 8]
            dv.inc(vector.tensor_scalar_mul(sc, eq[:, :], stat[:, 3:4]))
            dv.flush(vector)
            dv.inc(vector.reduce_max(stat[:, 4:5], sc, axis=AX.X))
            dv.flush(vector)
            dv.inc(vector.tensor_scalar(eq[:, :], sc, stat[:, 4:5], None, ALU.is_equal))
            dv.flush(vector)
            dv.inc(vector.scalar_tensor_tensor(eq[:, :], eq[:, :], -1e30, sc,
                                               ALU.mult, ALU.add))
            dv.flush(vector)
            dv.inc(vector.reduce_max(stat[:, 5:6], eq[:, :], axis=AX.X))
            dv.flush(vector)
            dv.inc(vector.tensor_scalar(eq[:, :], sc, stat[:, 5:6], None, ALU.is_ge))
            dv.flush(vector)
            dv.inc(vector.tensor_mul(sc, sc, eq[:, :]))
            dv_rt[(i, 'w')] = dv.n
    dv_w_all = dv.n

    pe_h1, pe_h3, ac_h1, dv_hg, pe_y, ac_y = {}, {}, {}, {}, {}, {}
    pe_trk, dv_c = {}, {}
    acc_n = {}
    pe_done = {}

    def expert_weights(gpsimd, e):
        if e >= 1:
            gpsimd.wait_ge(dv.h, dv_c[(e - 1, 3, 31)])
        for nm, buf in (("w1T", w1b), ("w3T", w3b), ("w2T", w2b)):
            src = dram[nm][e].rearrange("(kt kp) n -> kp kt n", kp=128)
            wl7[0].dinc(gpsimd.dma_start(
                out=buf[:, :].rearrange("p (kt n) -> p kt n", kt=8), in_=src))
        bl7[0].dinc(gpsimd.dma_start(out=b1s[:, 0:8], in_=dram["b1c"][e]))
        bl7[0].dinc(gpsimd.dma_start(out=b1s[:, 8:16], in_=dram["b3c"][e]))
        bl7[0].dinc(gpsimd.dma_start(out=b2s[:, 0:1024], in_=dram["b2bc"][e]))

    for e in range(NE):
        @block.gpsimd
        def _(gpsimd, e=e):
            expert_weights(gpsimd, e)

        for tc in range(4):
            for it in range(8):
                pp = it % 2

                @block.tensor
                def _(tensor, e=e, tc=tc, it=it, pp=pp):
                    if tc == 0 and it == 0:
                        tensor.wait_ge(wl7[0].h, 48 * (e + 1))
                        tensor.wait_ge(bl7[0].h, 48 * (e + 1))
                        if e == 0:
                            tensor.wait_ge(dv.h, dv_w_all)
                    prevs = [(e, tc, it - 2), (e, tc - 1, it + 6), (e - 1, 3, it + 6)]
                    for pk in prevs:
                        if pk in dv_hg:
                            tensor.wait_ge(dv.h, dv_hg[pk])
                            break
                    for kb in range(8):
                        tensor.matmul(PS[0 + pp][:, 0:512],
                                      w1b[:, kb * 1024 + it * 128:kb * 1024 + (it + 1) * 128],
                                      xn1T[:, kb * T + tc * 512:kb * T + (tc + 1) * 512],
                                      start=(kb == 0), stop=(kb == 7))
                    for kb in range(8):
                        mm = tensor.matmul(PS[2 + pp][:, 0:512],
                                           w3b[:, kb * 1024 + it * 128:kb * 1024 + (it + 1) * 128],
                                           xn1T[:, kb * T + tc * 512:kb * T + (tc + 1) * 512],
                                           start=(kb == 0), stop=(kb == 7))
                    pe.inc(mm)
                    pe_h1[(e, tc, it)] = pe.n

                @block.scalar
                def _(scalar, e=e, tc=tc, it=it, pp=pp):
                    scalar.wait_ge(pe.h, pe_h1[(e, tc, it)])
                    if (e, tc, it - 2) in dv_hg:
                        scalar.wait_ge(dv.h, dv_hg[(e, tc, it - 2)])
                    ac.inc(scalar.activation(out=t1b[:, pp * 512:(pp + 1) * 512],
                                             in_=PS[0 + pp][:, 0:512], func=AF.Silu,
                                             bias=b1s[:, it:it + 1]))
                    ac_h1[(e, tc, it)] = ac.n

                @block.vector
                def _(vector, e=e, tc=tc, it=it, pp=pp):
                    vector.wait_ge(ac.h, ac_h1[(e, tc, it)])
                    if it == 0 and (e, tc) != (0, 0):
                        prev = (e, tc - 1) if tc >= 1 else (e - 1, 3)
                        vector.wait_ge(pe.h, pe_y[prev])
                    vector.tensor_scalar(t2b[:, pp * 512:(pp + 1) * 512],
                                         PS[2 + pp][:, 0:512], b1s[:, 8 + it:9 + it],
                                         None, ALU.add)
                    dv.inc(vector.tensor_mul(hgT[:, it * 512:(it + 1) * 512],
                                             t1b[:, pp * 512:(pp + 1) * 512],
                                             t2b[:, pp * 512:(pp + 1) * 512]))
                    dv_hg[(e, tc, it)] = dv.n

            for dd in range(8):
                @block.tensor
                def _(tensor, e=e, tc=tc, dd=dd):
                    if dd == 0:
                        tensor.wait_ge(dv.h, dv_hg[(e, tc, 7)])
                    prevs = [(e, tc, dd - 2), (e, tc - 1, dd + 6), (e - 1, 3, dd + 6)]
                    for pk in prevs:
                        if pk in ac_y:
                            tensor.wait_ge(ac.h, ac_y[pk])
                            break
                    for kb in range(8):
                        mm = tensor.matmul(PS[4 + dd % 2][:, 0:512],
                                           w2b[:, kb * 1024 + dd * 128:kb * 1024 + (dd + 1) * 128],
                                           hgT[:, kb * 512:(kb + 1) * 512],
                                           start=(kb == 0), stop=(kb == 7))
                    pe.inc(mm)
                    pe_y[(e, tc, dd)] = pe.n
                    if dd == 7:
                        pe_y[(e, tc)] = pe.n

                @block.scalar
                def _(scalar, e=e, tc=tc, dd=dd):
                    scalar.wait_ge(pe.h, pe_y[(e, tc, dd)])
                    prevs = [(e, tc, dd - 2, 't'), (e, tc - 1, dd + 6, 't'), (e - 1, 3, dd + 6, 't')]
                    for pk in prevs:
                        if pk in pe_trk:
                            scalar.wait_ge(pe.h, pe_trk[pk])
                            break
                    ac.inc(scalar.copy(yTb[:, dd * 512:(dd + 1) * 512],
                                       PS[4 + dd % 2][:, 0:512]))
                    ac_y[(e, tc, dd)] = ac.n

            for tb in range(4):
                for dd in range(8):
                    kk = tb * 8 + dd

                    @block.tensor
                    def _(tensor, e=e, tc=tc, tb=tb, dd=dd, kk=kk):
                        if tb == 0 and dd == 0:
                            tensor.wait_ge(ac.h, ac_y[(e, tc, 7)])
                        prevs = [(e, tc, kk - 2), (e, tc - 1, kk + 30), (e - 1, 3, kk + 30)]
                        for pk in prevs:
                            if pk in dv_c:
                                tensor.wait_ge(dv.h, dv_c[pk])
                                break
                        pe.inc(tensor.transpose(
                            PB[kk % 2][:, 0:128],
                            yTb[:, dd * 512 + tb * 128:dd * 512 + (tb + 1) * 128],
                            idn[:, :]))
                        pe_trk[(e, tc, kk)] = pe.n
                        if tb == 3 and dd == 7:
                            for d2 in range(8):
                                pe_trk[(e, tc, d2, 't')] = pe.n

                    @block.vector
                    def _(vector, e=e, tc=tc, tb=tb, dd=dd, kk=kk):
                        it_ = tc * 4 + tb
                        vector.wait_ge(pe.h, pe_trk[(e, tc, kk)])
                        if e == 0 and tc == 0 and dd == 0 and tb < 2:
                            vector.wait_ge(stb[tb].h, st_base0[tb])
                        if dd == 0:
                            prevs = [(e, tc, tb - 2), (e, tc - 1, tb + 2), (e - 1, 3, tb + 2)]
                            for pk in prevs:
                                if pk in acc_n:
                                    vector.wait_ge(acc7.h, acc_n[pk])
                                    break
                        sslice = stage[:, (tb % 2) * D + dd * 128:(tb % 2) * D + (dd + 1) * 128]
                        vector.tensor_add(sslice, PB[kk % 2][:, 0:128],
                                          b2s[:, dd * 128:(dd + 1) * 128])
                        dv.inc(vector.tensor_scalar_mul(
                            sslice, sslice, w_all[:, it_ * 8 + e:it_ * 8 + e + 1]))
                        dv_c[(e, tc, kk)] = dv.n

                @block.gpsimd
                def _(gpsimd, e=e, tc=tc, tb=tb):
                    it_ = tc * 4 + tb
                    gpsimd.wait_ge(dv.h, dv_c[(e, tc, tb * 8 + 7)])
                    if e == 0 and tc == 0 and tb == 0:
                        gpsimd.wait_ge(stb[0].h, stb[0].n)
                        gpsimd.wait_ge(stb[1].h, stb[1].n)
                    if tc == 0 and tb == 0 and e >= 1:
                        gpsimd.wait_ge(acc7.h, acc_n[(e - 1, 3, 3)])
                    acc7.dinc(gpsimd.dma_start(
                        out=dram["out"][it_ * 128:(it_ + 1) * 128, :],
                        in_=stage[:, (tb % 2) * D:(tb % 2 + 1) * D],
                        accum_op=ALU.add))
                    acc_n[(e, tc, tb)] = acc7.n

        pe_done[e] = pe_y[(e, 3)]

    bs13 = ck.sb("p7_bs13", [128, 32], F32)
    b2s2 = b2s[:, 1024:2048]
    E = NE

    @block.gpsimd
    def _(gpsimd):
        gpsimd.wait_ge(dv.h, dv_c[(NE - 1, 3, 31)])
        for nm, buf in (("ws1T", ws1b), ("ws3T", ws3b)):
            wl7[1].dinc(gpsimd.dma_start(
                out=buf[:, :].rearrange("p (kt n) -> p kt n", kt=8),
                in_=dram[nm].ap().rearrange("(kt kp) n -> kp kt n", kp=128)))
        wl7[1].dinc(gpsimd.dma_start(
            out=ws2b[:, :].rearrange("p (kt n) -> p kt n", kt=16),
            in_=dram["ws2T"].ap().rearrange("(kt kp) n -> kp kt n", kp=128)))
        bl7[1].dinc(gpsimd.dma_start(out=bs13[:, 0:16], in_=dram["bs1c"][:, :]))
        bl7[1].dinc(gpsimd.dma_start(out=bs13[:, 16:32], in_=dram["bs3c"][:, :]))
        bl7[1].dinc(gpsimd.dma_start(out=b2s2, in_=dram["bs2bc"][:, :]))

    for tc in range(4):
        for it in range(16):
            pp = it % 2

            @block.tensor
            def _(tensor, tc=tc, it=it, pp=pp):
                if tc == 0 and it == 0:
                    tensor.wait_ge(wl7[1].h, 48)
                    tensor.wait_ge(bl7[1].h, 48)
                prevs = [(E, tc, it - 2), (E, tc - 1, it + 14), (NE - 1, 3, 7)]
                for pk in prevs:
                    if pk in dv_hg:
                        tensor.wait_ge(dv.h, dv_hg[pk])
                        break
                for kb in range(8):
                    tensor.matmul(PS[0 + pp][:, 0:512],
                                  ws1b[:, kb * 2048 + it * 128:kb * 2048 + (it + 1) * 128],
                                  xn1T[:, kb * T + tc * 512:kb * T + (tc + 1) * 512],
                                  start=(kb == 0), stop=(kb == 7))
                for kb in range(8):
                    mm = tensor.matmul(PS[2 + pp][:, 0:512],
                                       ws3b[:, kb * 2048 + it * 128:kb * 2048 + (it + 1) * 128],
                                       xn1T[:, kb * T + tc * 512:kb * T + (tc + 1) * 512],
                                       start=(kb == 0), stop=(kb == 7))
                pe.inc(mm)
                pe_h1[(E, tc, it)] = pe.n

            @block.scalar
            def _(scalar, tc=tc, it=it, pp=pp):
                scalar.wait_ge(pe.h, pe_h1[(E, tc, it)])
                if (E, tc, it - 2) in dv_hg:
                    scalar.wait_ge(dv.h, dv_hg[(E, tc, it - 2)])
                ac.inc(scalar.activation(out=t1b[:, pp * 512:(pp + 1) * 512],
                                         in_=PS[0 + pp][:, 0:512], func=AF.Silu,
                                         bias=bs13[:, it:it + 1]))
                ac_h1[(E, tc, it)] = ac.n

            @block.vector
            def _(vector, tc=tc, it=it, pp=pp):
                vector.wait_ge(ac.h, ac_h1[(E, tc, it)])
                if it == 0:
                    prev = (E, tc - 1) if tc >= 1 else (NE - 1, 3)
                    vector.wait_ge(pe.h, pe_y[prev])
                vector.tensor_scalar(t2b[:, pp * 512:(pp + 1) * 512],
                                     PS[2 + pp][:, 0:512], bs13[:, 16 + it:17 + it],
                                     None, ALU.add)
                dv.inc(vector.tensor_mul(hgT[:, it * 512:(it + 1) * 512],
                                         t1b[:, pp * 512:(pp + 1) * 512],
                                         t2b[:, pp * 512:(pp + 1) * 512]))
                dv_hg[(E, tc, it)] = dv.n

        for dd in range(8):
            @block.tensor
            def _(tensor, tc=tc, dd=dd):
                if dd == 0:
                    tensor.wait_ge(dv.h, dv_hg[(E, tc, 15)])
                prevs = [(E, tc, dd - 2), (E, tc - 1, dd + 6), (NE - 1, 3, dd + 6)]
                for pk in prevs:
                    if pk in ac_y:
                        tensor.wait_ge(ac.h, ac_y[pk])
                        break
                for kb in range(16):
                    mm = tensor.matmul(PS[4 + dd % 2][:, 0:512],
                                       ws2b[:, kb * 1024 + dd * 128:kb * 1024 + (dd + 1) * 128],
                                       hgT[:, kb * 512:(kb + 1) * 512],
                                       start=(kb == 0), stop=(kb == 15))
                pe.inc(mm)
                pe_y[(E, tc, dd)] = pe.n
                if dd == 7:
                    pe_y[(E, tc)] = pe.n

            @block.scalar
            def _(scalar, tc=tc, dd=dd):
                scalar.wait_ge(pe.h, pe_y[(E, tc, dd)])
                prevs = [(E, tc, dd - 2, 't'), (E, tc - 1, dd + 6, 't'), (NE - 1, 3, dd + 6, 't')]
                for pk in prevs:
                    if pk in pe_trk:
                        scalar.wait_ge(pe.h, pe_trk[pk])
                        break
                ac.inc(scalar.copy(yTb[:, dd * 512:(dd + 1) * 512],
                                   PS[4 + dd % 2][:, 0:512]))
                ac_y[(E, tc, dd)] = ac.n

        for tb in range(4):
            for dd in range(8):
                kk = tb * 8 + dd

                @block.tensor
                def _(tensor, tc=tc, tb=tb, dd=dd, kk=kk):
                    if tb == 0 and dd == 0:
                        tensor.wait_ge(ac.h, ac_y[(E, tc, 7)])
                    prevs = [(E, tc, kk - 2), (E, tc - 1, kk + 30), (NE - 1, 3, kk + 30)]
                    for pk in prevs:
                        if pk in dv_c:
                            tensor.wait_ge(dv.h, dv_c[pk])
                            break
                    pe.inc(tensor.transpose(
                        PB[kk % 2][:, 0:128],
                        yTb[:, dd * 512 + tb * 128:dd * 512 + (tb + 1) * 128],
                        idn[:, :]))
                    pe_trk[(E, tc, kk)] = pe.n
                    if tb == 3 and dd == 7:
                        for d2 in range(8):
                            pe_trk[(E, tc, d2, 't')] = pe.n

                @block.vector
                def _(vector, tc=tc, tb=tb, dd=dd, kk=kk):
                    vector.wait_ge(pe.h, pe_trk[(E, tc, kk)])
                    if dd == 0:
                        prevs = [(E, tc, tb - 2), (E, tc - 1, tb + 2), (NE - 1, 3, tb + 2)]
                        for pk in prevs:
                            if pk in acc_n:
                                vector.wait_ge(acc7.h, acc_n[pk])
                                break
                    sslice = stage[:, (tb % 2) * D + dd * 128:(tb % 2) * D + (dd + 1) * 128]
                    dv.inc(vector.tensor_add(sslice, PB[kk % 2][:, 0:128],
                                             b2s2[:, dd * 128:(dd + 1) * 128]))
                    dv_c[(E, tc, kk)] = dv.n

            @block.gpsimd
            def _(gpsimd, tc=tc, tb=tb):
                it_ = tc * 4 + tb
                gpsimd.wait_ge(dv.h, dv_c[(E, tc, tb * 8 + 7)])
                if tc == 0 and tb == 0:
                    gpsimd.wait_ge(acc7.h, acc_n[(NE - 1, 3, 3)])
                acc7.dinc(gpsimd.dma_start(
                    out=dram["out"][it_ * 128:(it_ + 1) * 128, :],
                    in_=stage[:, (tb % 2) * D:(tb % 2 + 1) * D],
                    accum_op=ALU.add))
                acc_n[(E, tc, tb)] = acc7.n

    @block.sync
    def _(sync):
        sync.wait_ge(acc7.h, acc7.n)
    return {}


from concourse.bass_utils import run_bass_kernel_spmd

_PHASES = {}


def _phase(name, builder):
    if name not in _PHASES:
        _PHASES[name] = builder()
    return _PHASES[name]


def _build_p1():
    ck = CoreKernel()
    d = p1_dram_decls(ck, kind="ExternalOutput")
    with ck.nc.Block() as block:
        emit_phase1(ck, block, d)
    ck.close()
    return ck


def _build_p2():
    ck = CoreKernel()
    d = p2_dram_decls(ck)
    with ck.nc.Block() as block:
        emit_phase2(ck, block, d)
    ck.close()
    return ck


def _build_p34():
    ck = CoreKernel()
    d = p34_dram_decls(ck)
    with ck.nc.Block() as block:
        emit_phase34(ck, block, d)
    ck.close()
    return ck


def _build_p5():
    ck = CoreKernel()
    d = p5_dram_decls(ck)
    with ck.nc.Block() as block:
        emit_phase5(ck, block, d, groups=[[0, 1, 2, 3], [4, 5, 6, 7]])
    ck.close()
    return ck


def _build_p6():
    ck = CoreKernel()
    ck.p6_no_a1l = True
    d = p6_dram_decls(ck)
    with ck.nc.Block() as block:
        emit_phase6(ck, block, d, groups=[[0, 1, 2, 3], [4, 5, 6, 7]])
    ck.close()
    return ck


def _build_p7():
    ck = CoreKernel()
    d = p7_dram_decls(ck)
    with ck.nc.Block() as block:
        emit_phase7(ck, block, d)
    ck.close()
    return ck


def kernel(**I):
    cores = list(range(8))

    ck1 = _phase('p1', _build_p1)
    ims1 = [host_inputs_phase1(I, c) for c in cores]
    r1 = run_bass_kernel_spmd(ck1.nc, ims1, core_ids=cores).results

    ck2 = _phase('p2', _build_p2)
    ims2 = []
    for c in cores:
        im = host_inputs_phase2(I, c)
        if "idn128b" in im:
            del im["idn128b"]
        for k in ["X_t", "BC_t", "B_n", "C_n", "caT", "ca_t"]:
            im[k] = np.asarray(r1[c][k])
        ims2.append(im)
    r2 = run_bass_kernel_spmd(ck2.nc, ims2, core_ids=cores).results

    ck3 = _phase('p34', _build_p34)
    ims3 = []
    for c in cores:
        im = host_inputs_phase34(I, c)
        p1h = host_inputs_phase1(I, c)
        im["u_loc"] = p1h["u_loc"]
        im["idn128"] = p1h["idn128"]
        im["xn0T"] = np.asarray(r1[c]["xn0T"])
        im["Y"] = np.asarray(r2[c]["Y"])
        ims3.append(im)
    r3 = run_bass_kernel_spmd(ck3.nc, ims3, core_ids=cores).results
    x1 = [np.asarray(r3[c]["x1"], np.float32) for c in cores]

    import ml_dtypes
    idn = np.eye(128, dtype=np.float32).astype(ml_dtypes.bfloat16)
    ck5 = _phase('p5', _build_p5)
    ims5 = [{"x1": x1[c], "idn128": idn} for c in cores]
    r5 = run_bass_kernel_spmd(ck5.nc, ims5, core_ids=cores).results
    xn1T = [np.asarray(r5[c]["xn1T"]) for c in cores]

    ck6 = _phase('p6', _build_p6)
    ims6 = []
    for c in cores:
        b = c // 4
        im = host_inputs_p6(I, c)
        im["xg"] = np.concatenate([xn1T[4 * b + q] for q in range(4)], axis=0)
        ims6.append(im)
    r6 = run_bass_kernel_spmd(ck6.nc, ims6, core_ids=cores).results
    a1s = [np.asarray(r6[c]["a1s"], np.float32) for c in cores]
    a1l = []
    for c in cores:
        b, q = divmod(c, 4)
        a1l.append(np.hstack([a1s[4 * b + hh][q * T:(q + 1) * T]
                              for hh in range(4)]))

    ck7 = _phase('p7', _build_p7)
    ims7 = []
    for c in cores:
        im = host_inputs_p7(I, c)
        im["xn1T"] = xn1T[c]
        im["a1l"] = a1l[c]
        im["x1"] = x1[c]
        ims7.append(im)
    r7 = run_bass_kernel_spmd(ck7.nc, ims7, core_ids=cores).results

    out = np.empty((2, 8192, 1024), np.float32)
    for c in cores:
        b, q = divmod(c, 4)
        out[b, q * T:(q + 1) * T] = np.asarray(r7[c]["out"], np.float32)
    return out


# version 04 (final)
# speedup vs baseline: 1.0737x; 1.0737x over previous
import numpy as np
import concourse.bass as bass
import concourse.mybir as mybir
from contextlib import ExitStack

F32 = mybir.dt.float32
BF16 = mybir.dt.bfloat16
AF = mybir.ActivationFunctionType
ALU = mybir.AluOpType
AX = mybir.AxisListType

T = 2048
HALO = 3
TH = T + HALO
D = 1024
DIN = 2048
CH = 2176
NH0 = 32
QSUB = 256
S1 = 8192
DH1 = 256
CAP = 5120


class Sem:
    def __init__(self, ck, name):
        self.h = ck.stack.enter_context(ck.nc.semaphore(name))
        self.n = 0

    def inc(self, inst, k=1):
        inst.then_inc(self.h, k)
        self.n += k
        return self.n

    def dinc(self, inst):
        return self.inc(inst, 16)

    def flush(self, eng):
        eng.wait_ge(self.h, self.n)


class CoreKernel:
    def __init__(self):
        self.nc = bass.Bass("TRN2", target_bir_lowering=False)
        self.stack = ExitStack()
        self.sems = {}
        self._sb = {}
        self.PS = [self.ps(f"psf{i}", [128, 512], F32) for i in range(6)]
        self.PB = [self.ps(f"psb{i}", [128, 512], BF16) for i in range(2)]

    def sem(self, name):
        if name not in self.sems:
            self.sems[name] = Sem(self, name)
        return self.sems[name]

    def push_scope(self):
        if not hasattr(self, '_scopes'):
            self._scopes = []
        self._scopes.append(ExitStack())

    def pop_scope(self):
        self._scopes.pop().close()

    def sb(self, name, shape, dtype):
        st = self._scopes[-1] if getattr(self, '_scopes', None) else self.stack
        t = st.enter_context(self.nc.sbuf_tensor(name, list(shape), dtype))
        self._sb[name] = t
        return t

    def ps(self, name, shape, dtype=F32):
        return self.stack.enter_context(self.nc.psum_tensor(name, list(shape), dtype))

    def din(self, name, shape, dtype=F32):
        return self.nc.dram_tensor(name, list(shape), dtype, kind="ExternalInput")

    def dout(self, name, shape, dtype=F32):
        return self.nc.dram_tensor(name, list(shape), dtype, kind="ExternalOutput")

    def dint(self, name, shape, dtype=F32):
        return self.nc.dram_tensor(name, list(shape), dtype, kind="Internal")

    def close(self):
        self.stack.close()


def chunks(total, step):
    out = []
    o = 0
    while o < total:
        out.append((o, min(step, total - o)))
        o += step
    return out



def host_inputs_phase1(I, core):
    b, q = divmod(core, 4)
    u = np.asarray(I["L0_attn_u"])
    s0 = q * T
    rows = np.zeros((17 * 128, D), np.float32)
    lo = max(0, s0 - HALO)
    off = HALO - (s0 - lo)
    rows[off:off + (s0 + T - lo)] = u[b, lo:s0 + T]
    w1 = np.asarray(I["L0_attn_W_in1"])
    w_in1T = np.ascontiguousarray(w1[DIN:].T)
    cw = np.asarray(I["L0_attn_conv_w"])
    conv_wT = np.ascontiguousarray(np.transpose(cw, (2, 1, 0)))
    conv_b_t = np.ascontiguousarray(np.asarray(I["L0_attn_conv_b"]).reshape(17, 128).T)
    a_col = np.ascontiguousarray(
        np.broadcast_to(np.asarray(I["L0_attn_prelu_a"]).reshape(1, 1), (128, 1))
    ).astype(np.float32)
    alog_col = np.zeros((32, 1), np.float32)
    alog_col[:, 0] = np.asarray(I["L0_attn_A_log"])
    idn = np.eye(128, dtype=np.float32)
    import ml_dtypes
    return {
        "u_loc": rows,
        "w_in1T": w_in1T.astype(np.float32),
        "conv_wT": conv_wT.astype(np.float32),
        "conv_b_t": conv_b_t.astype(np.float32),
        "prelu_a_col": a_col,
        "alog_col": alog_col,
        "idn128": idn.astype(ml_dtypes.bfloat16),
    }


def p1_dram_decls(ck, kind="ExternalOutput"):
    mk = ck.dout if kind == "ExternalOutput" else ck.dint
    d = {}
    d["u_loc"] = ck.din("u_loc", [17 * 128, D], F32)
    d["w_in1T"] = ck.din("w_in1T", [D, 2208], F32)
    d["conv_wT"] = ck.din("conv_wT", [4, CH, CH], F32)
    d["conv_b_t"] = ck.din("conv_b_t", [128, 17], F32)
    d["prelu_a_col"] = ck.din("prelu_a_col", [128, 1], F32)
    d["alog_col"] = ck.din("alog_col", [32, 1], F32)
    d["idn128"] = ck.din("idn128", [128, 128], BF16)
    d["xn0T"] = mk("xn0T", [D, T], BF16)
    d["X_t"] = mk("X_t", [T, DIN], BF16)
    d["B_n"] = mk("B_n", [64, T], BF16)
    d["BC_t"] = mk("BC_t", [T, 128], BF16)
    d["C_n"] = mk("C_n", [64, T], BF16)
    d["caT"] = mk("caT", [NH0, T], F32)
    d["ca_t"] = mk("ca_t", [T, NH0], F32)
    return d


def emit_phase1(ck, block, dram):
    nc = ck.nc
    ld = ck.sem("g_ld")
    wld = ck.sem("g_wld")
    pe = ck.sem("g_pe")
    dv = ck.sem("g_dv")
    ac = ck.sem("g_ac")
    ev = ck.sem("g_ev")
    st = ck.sem("g_st")
    stxs = [ck.sem(f"g_stxs{j}") for j in range(4)]
    stcs = [ck.sem(f"g_stcs{j}") for j in range(2)]
    sto = ck.sem("g_sto")

    NT = 17

    u_t = ck.sb("p1_u", [128, 3 * D], F32)
    mu = ck.sb("p1_mu", [128, NT], F32)
    nmu = ck.sb("p1_nmu", [128, NT], F32)
    rstd = ck.sb("p1_rstd", [128, NT], F32)
    sq = ck.sb("p1_sq", [128, D], F32)
    var = ck.sb("p1_var", [128, NT], F32)
    xn_t = ck.sb("p1_xnt", [128, 2 * D], BF16)
    idn = ck.sb("idn", [128, 128], BF16)
    idnf = ck.sb("idnf", [128, 128], F32)
    xn0T = ck.sb("p1_xn0T", [128, 8 * 2176], BF16)
    wshare = ck.sb("p1_wsh", [128, 8 * 2208], BF16)
    w1 = wshare
    cwbuf = [wshare[:, 0:68 * 128], wshare[:, 68 * 128:136 * 128]]
    zxbc = ck.sb("p1_zxbc", [128, 17 * TH], BF16)
    dtb = ck.sb("p1_dtb", [32, TH], F32)
    caT = ck.sb("p1_caT", [32, T], F32)
    cbias = ck.sb("p1_cb", [128, 17], F32)
    acol = ck.sb("p1_acol", [128, 1], F32)
    cvo = ck.sb("p1_cvo", [128, 2 * T], BF16)
    cvf = ck.sb("p1_cvf", [128, 512], F32)
    cvr = ck.sb("p1_cvr", [128, 512], F32)
    alog = ck.sb("p1_alog", [32, 1], F32)
    aneg = ck.sb("p1_aneg", [32, 1], F32)
    catt = ck.sb("p1_catt", [128, 64], F32)
    xtev = ck.sb("p1_xtev", [128, 4 * 128], BF16)

    PS = ck.PS
    PB = ck.PB

    dv0 = ck.sems["g_dv"].n
    ac0 = ck.sems["g_ac"].n
    dvc_s1, dvc_s3, dvc_s5 = {}, {}, {}
    acc_s2, acc_s4 = {}, {}
    _dv = dv0
    for i in range(NT + 2):
        if i <= NT - 1:
            _dv += 2; dvc_s1[i] = _dv
        if 1 <= i <= NT:
            _dv += 3; dvc_s3[i - 1] = _dv
        if 2 <= i <= NT + 1:
            _dv += 1; dvc_s5[i - 2] = _dv
    _ac = ac0
    for i in range(NT + 1):
        if i <= NT - 1:
            _ac += 1; acc_s2[i] = _ac
        if 1 <= i <= NT:
            _ac += 1; acc_s4[i - 1] = _ac

    lds = [ck.sem(f"g_lds{j}") for j in range(3)]
    ldm = ck.sem("g_ldm")
    @block.sync
    def _(sync):
        for i in range(NT):
            if i >= 3:
                sync.wait_ge(dv.h, dvc_s5[i - 3])
            lds[i % 3].dinc(sync.dma_start(out=u_t[:, (i % 3) * D:(i % 3 + 1) * D],
                           in_=dram["u_loc"][i * 128:(i + 1) * 128, :]))
        ldm.dinc(sync.dma_start(out=cbias[:, :], in_=dram["conv_b_t"][:, :]))
        ldm.dinc(sync.dma_start(out=acol[:, :], in_=dram["prelu_a_col"][:, :]))
        ldm.dinc(sync.dma_start(out=alog[:, :], in_=dram["alog_col"][:, :]))
    ld_u = [16 * (i // 3 + 1) for i in range(NT)]
    ld_misc = ldm.n

    wldi = ck.sem("g_wldi")
    @block.gpsimd
    def _(gpsimd):
        wldi.dinc(gpsimd.dma_start(out=idn[:, :], in_=dram["idn128"][:, :]))
        for kb in range(8):
            wld.dinc(gpsimd.dma_start(out=w1[:, kb * 2208:(kb + 1) * 2208],
                             in_=dram["w_in1T"][kb * 128:(kb + 1) * 128, :]))
    wld_w1 = wld.n

    pe0 = ck.sems["g_pe"].n
    pe_tr_xn = lambda i: pe0 + 8 * (i + 1)

    def ln_s1(vector, i):
        vector.wait_ge(lds[i % 3].h, ld_u[i])
        ut = u_t[:, (i % 3) * D:(i % 3 + 1) * D]
        dv.inc(vector.reduce_sum(mu[:, i:i + 1], ut, axis=AX.X))
        dv.flush(vector)
        dv.inc(vector.tensor_scalar_mul(nmu[:, i:i + 1], mu[:, i:i + 1], -1.0 / D))
        assert dv.n == dvc_s1[i]

    def ln_s3(vector, i):
        vector.wait_ge(ac.h, acc_s2[i])
        dv.inc(vector.reduce_sum(var[:, i:i + 1], sq[:, :], axis=AX.X))
        dv.flush(vector)
        dv.inc(vector.tensor_scalar(var[:, i:i + 1], var[:, i:i + 1], 1.0 / D, 1e-5,
                                    ALU.mult, ALU.add))
        dv.flush(vector)
        dv.inc(vector.reciprocal(rstd[:, i:i + 1], var[:, i:i + 1]))
        assert dv.n == dvc_s3[i]

    def ln_s5(vector, i):
        vector.wait_ge(ac.h, acc_s4[i])
        if i >= 2:
            vector.wait_ge(pe.h, pe_tr_xn(i - 2))
        ut = u_t[:, (i % 3) * D:(i % 3 + 1) * D]
        dv.inc(vector.tensor_scalar(xn_t[:, (i % 2) * D:(i % 2 + 1) * D], ut,
                                    nmu[:, i:i + 1], rstd[:, i:i + 1],
                                    ALU.add, ALU.mult))
        assert dv.n == dvc_s5[i]

    ev_tr0 = ck.sems["g_ev"].n

    def ln_evtr(vector, j):
        for jd in range(8):
            k = j * 8 + jd
            vector.wait_ge(pe.h, pe0 + k + 1)
            ev.inc(vector.tensor_copy(
                xn0T[:, jd * 2176 + j * 128: jd * 2176 + (j + 1) * 128],
                PB[k % 2][:, 0:128]))

    @block.vector
    def _(vector):
        for i in range(NT + 3):
            if i <= NT - 1:
                ln_s1(vector, i)
            if 1 <= i <= NT:
                ln_s3(vector, i - 1)
            if 2 <= i <= NT + 1:
                ln_s5(vector, i - 2)
            if 3 <= i <= NT + 2:
                ln_evtr(vector, i - 3)

    @block.scalar
    def _(scalar):
        for i in range(NT + 1):
            if i <= NT - 1:
                scalar.wait_ge(dv.h, dvc_s1[i])
                if i >= 1:
                    scalar.wait_ge(dv.h, dvc_s3[i - 1])
                ut = u_t[:, (i % 3) * D:(i % 3 + 1) * D]
                ac.inc(scalar.activation(out=sq[:, :], in_=ut, func=AF.Square,
                                         bias=nmu[:, i:i + 1]))
                assert ac.n == acc_s2[i]
            if 1 <= i <= NT:
                j = i - 1
                scalar.wait_ge(dv.h, dvc_s3[j])
                ac.inc(scalar.activation(out=rstd[:, j:j + 1], in_=rstd[:, j:j + 1],
                                         func=AF.Sqrt))
                assert ac.n == acc_s4[j]

    @block.tensor
    def _(tensor):
        tensor.wait_ge(wldi.h, 16)
        for i in range(NT):
            tensor.wait_ge(dv.h, dvc_s5[i])
            for jd in range(8):
                k = i * 8 + jd
                if k >= 2:
                    tensor.wait_ge(ev.h, ev_tr0 + k - 1)
                pe.inc(tensor.transpose(
                    PB[k % 2][:, 0:128],
                    xn_t[:, (i % 2) * D + jd * 128:(i % 2) * D + (jd + 1) * 128],
                    idn[:, :]))
    assert pe.n == pe0 + NT * 8

    ev_tr_end = ev.n

    @block.sync
    def _(sync):
        for jd in range(8):
            sync.wait_ge(ev.h, ev_tr0 + (NT - 1) * 8 + jd + 1)
            sto.dinc(sync.dma_start(out=dram["xn0T"][jd * 128:(jd + 1) * 128, :],
                           in_=xn0T[:, jd * 2176 + HALO: jd * 2176 + HALO + T]))

    tcs = chunks(TH, 512)
    pe_zx0 = pe.n
    ev_zx0 = ev.n
    @block.tensor
    def _(tensor):
        tensor.wait_ge(wld.h, wld_w1)
        tensor.wait_ge(ev.h, ev_tr_end)
        cnt = 0
        for m in range(18):
            mc0 = m * 128
            mw = 128 if m < 17 else 32
            for (t0, tw) in tcs:
                pidx = cnt % 2
                if cnt >= 2:
                    tensor.wait_ge(ev.h, ev_zx0 + cnt - 1)
                for kb in range(7):
                    tensor.matmul(PS[pidx][0:mw, 0:tw],
                                  w1[:, kb * 2208 + mc0: kb * 2208 + mc0 + mw],
                                  xn0T[:, kb * 2176 + t0: kb * 2176 + t0 + tw],
                                  start=(kb == 0), stop=False)
                pe.inc(tensor.matmul(PS[pidx][0:mw, 0:tw],
                                     w1[:, 7 * 2208 + mc0: 7 * 2208 + mc0 + mw],
                                     xn0T[:, 7 * 2176 + t0: 7 * 2176 + t0 + tw],
                                     start=False, stop=True))
                cnt += 1
    pe_zx_end = pe.n
    nzx = 18 * len(tcs)

    @block.scalar
    def _(scalar):
        cnt = 0
        for m in range(18):
            mw = 128 if m < 17 else 32
            for (t0, tw) in tcs:
                pidx = cnt % 2
                scalar.wait_ge(pe.h, pe_zx0 + cnt + 1)
                if m < 17:
                    ev.inc(scalar.copy(zxbc[:, m * TH + t0: m * TH + t0 + tw],
                                       PS[pidx][0:mw, 0:tw]))
                else:
                    ev.inc(scalar.copy(dtb[:, t0:t0 + tw], PS[pidx][0:mw, 0:tw]))
                cnt += 1
    ev_zx_end = ev.n

    wld_cv0 = wld.n
    ac_cv0 = ac.n
    dv_cv0 = dv.n
    ev_cv0 = ev.n
    pe_cv0 = pe.n

    pe_conv_chunk = {}
    pe_tr_k = {}
    dv_cv_done = {m: dv_cv0 + 4 * (m + 1) for m in range(17)}
    pe_after_convmm = {}
    _pe = pe_cv0
    for m in range(17):
        for ti in range(4):
            _pe += 1
            pe_conv_chunk[(m, ti)] = _pe
        pe_after_convmm[m] = _pe
        ms = [m - 1] if m >= 1 else []
        if m == 16:
            ms.append(16)
        for mm_ in ms:
            for i in range(16):
                pe_tr_k[mm_ * 16 + i] = _pe + i + 1
            _pe += 16

    cws = [ck.sem("g_cws0"), ck.sem("g_cws1")]
    @block.tensor
    def _(tensor):
        tensor.wait_ge(ev.h, ev_zx_end)
        cnt = 0
        for m in range(17):
            buf = m % 2
            tensor.wait_ge(cws[buf].h, 16 * (m // 2 + 1))
            for ti in range(4):
                t0 = ti * 512
                pidx = 2 + (cnt % 2)
                if cnt >= 2:
                    tensor.wait_ge(ac.h, ac_cv0 + cnt - 1)
                mm = None
                first = True
                for k in range(4):
                    for kb in range(17):
                        mm = tensor.matmul(
                            PS[pidx][:, 0:512],
                            cwbuf[buf][:, (k * 17 + kb) * 128:(k * 17 + kb + 1) * 128],
                            zxbc[:, kb * TH + t0 + k: kb * TH + t0 + k + 512],
                            start=first, stop=(k == 3 and kb == 16))
                        first = False
                pe.inc(mm)
                cnt += 1
            ms2 = [m - 1] if m >= 1 else []
            if m == 16:
                ms2.append(16)
            for mtr in ms2:
                tensor.wait_ge(dv.h, dv_cv_done[mtr])
                for i in range(16):
                    k = mtr * 16 + i
                    if k >= 2:
                        tensor.wait_ge(ev.h, ev_cv0 + k - 1)
                    pe.inc(tensor.transpose(
                        PB[k % 2][:, 0:128],
                        cvo[:, (mtr % 2) * T + i * 128:(mtr % 2) * T + (i + 1) * 128],
                        idn[:, :]))
        assert pe.n == _pe, (pe.n, _pe)

    @block.gpsimd
    def _(gpsimd):
        for m in range(17):
            buf = m % 2
            if m < 2:
                gpsimd.wait_ge(pe.h, pe_zx_end)
            else:
                gpsimd.wait_ge(pe.h, pe_after_convmm[m - 2])
            src = dram["conv_wT"][:, :, m * 128:(m + 1) * 128]
            src = src.rearrange("k (ih il) o -> il k ih o", il=128)
            dst = cwbuf[buf].rearrange("p (k ih o) -> p k ih o", k=4, ih=17)
            cws[buf].dinc(gpsimd.dma_start(out=dst, in_=src))

    @block.scalar
    def _(scalar):
        cnt = 0
        for m in range(17):
            for ti in range(4):
                pidx = 2 + (cnt % 2)
                scalar.wait_ge(pe.h, pe_conv_chunk[(m, ti)])
                if cnt >= 1:
                    scalar.wait_ge(dv.h, dv_cv0 + cnt)
                ac.inc(scalar.add(cvf[:, :], PS[pidx][:, 0:512], cbias[:, m:m + 1]))
                cnt += 1

    @block.vector
    def _(vector):
        cnt = 0
        for m in range(17):
            buf = m % 2
            for ti in range(4):
                vector.wait_ge(ac.h, ac_cv0 + cnt + 1)
                if m >= 2 and ti == 0:
                    vector.wait_ge(pe.h, pe_tr_k[(m - 2) * 16 + 15])
                vector.tensor_relu(cvr[:, :], cvf[:, :])
                vector.tensor_sub(cvf[:, :], cvf[:, :], cvr[:, :])
                dv.inc(vector.scalar_tensor_tensor(
                    cvo[:, buf * T + ti * 512: buf * T + ti * 512 + 512],
                    cvf[:, :], acol[0:128, 0:1], cvr[:, :],
                    ALU.mult, ALU.add))
                cnt += 1
            ms = [m - 1] if m >= 1 else []
            if m == 16:
                ms.append(16)
            for mtr in ms:
                for i in range(16):
                    k = mtr * 16 + i
                    vector.wait_ge(pe.h, pe_tr_k[k])
                    if k >= 4:
                        vector.wait_ge(stxs[k % 4].h, 16 * (k // 4))
                    ev.inc(vector.tensor_copy(
                        xtev[:, (k % 4) * 128:(k % 4 + 1) * 128],
                        PB[k % 2][:, 0:128]))

    @block.sync
    def _(sync):
        for k in range(17 * 16):
            m, i = divmod(k, 16)
            sync.wait_ge(ev.h, ev_cv0 + k + 1)
            dst = (dram["X_t"][i * 128:(i + 1) * 128, m * 128:(m + 1) * 128] if m < 16
                   else dram["BC_t"][i * 128:(i + 1) * 128, :])
            stxs[k % 4].dinc(sync.dma_start(out=dst,
                           in_=xtev[:, (k % 4) * 128:(k % 4 + 1) * 128]))

    @block.gpsimd
    def _(gpsimd):
        gpsimd.wait_ge(dv.h, dv_cv_done[16])
        st.dinc(gpsimd.dma_start(out=dram["B_n"][:, :], in_=cvo[0:64, 0:T]))
        st.dinc(gpsimd.dma_start(out=dram["C_n"][:, :], in_=cvo[64:128, 0:T]))

    sp1h = u_t[0:32, 2 * D:3 * D]
    sp2h = sq[0:32, 0:D]
    ones32 = u_t[0:32, 0:T]
    at_T = dtb[:, HALO:TH]
    ac_dt0 = ac.n
    dv_dt0 = dv.n
    @block.vector
    def _(vector):
        vector.wait_ge(ev.h, ev_zx_end)
        dv.inc(vector.tensor_relu(caT[:, :], dtb[:, HALO:TH]))
        vector.memset(ones32, 1.0)

    @block.scalar
    def _(scalar):
        scalar.wait_ge(ev.h, ev_zx_end)
        scalar.wait_ge(dv.h, dv_dt0 + 1)
        for h in range(2):
            c0 = HALO + h * D
            scalar.activation(out=sp1h, in_=dtb[:, c0:c0 + D], func=AF.Abs)
            scalar.activation(out=sp2h, in_=sp1h, func=AF.Exp, scale=-1.0)
            scalar.activation(out=sp1h, in_=sp2h, func=AF.Ln, bias=1.0)
            scalar.copy(dtb[:, c0:c0 + D], sp1h)
        ac.inc(scalar.activation(out=aneg[:, :], in_=alog[:, :], func=AF.Exp))
    ac_dt1 = ac.n

    @block.vector
    def _(vector):
        vector.wait_ge(ac.h, ac_dt1)
        vector.wait_ge(ldm.h, ld_misc)
        vector.tensor_add(at_T, at_T, caT[:, :])
        dv.inc(vector.tensor_scalar_mul(aneg[:, :], aneg[:, :], -1.0))
        dv.flush(vector)
        dv.inc(vector.tensor_scalar_mul(at_T, at_T, aneg[0:32, 0:1]))
        dv.flush(vector)
        dv.inc(vector.tensor_tensor_scan(caT[:, :], at_T, ones32, 0.0,
                                         ALU.add, ALU.mult))
    dv_scan = dv.n

    @block.vector
    def _(vector):
        vector.wait_ge(wldi.h, 16)
        dv.inc(vector.tensor_copy(idnf[:, :], idn[:, :]))
    dv_idnf = dv.n

    pe_ca0 = pe.n
    ev_ca0 = ev.n
    @block.tensor
    def _(tensor):
        tensor.wait_ge(dv.h, dv_idnf)
        tensor.wait_ge(dv.h, dv_scan)
        for i in range(16):
            if i >= 2:
                tensor.wait_ge(ev.h, ev_ca0 + i - 1)
            pe.inc(tensor.transpose(PS[4 + (i % 2)][0:128, 0:32],
                                    caT[:, i * 128:(i + 1) * 128], idnf[0:32, 0:32]))

    @block.vector
    def _(vector):
        for i in range(16):
            vector.wait_ge(pe.h, pe_ca0 + i + 1)
            if i >= 2:
                vector.wait_ge(stcs[i % 2].h, 16 * (i // 2))
            ev.inc(vector.tensor_copy(catt[:, (i % 2) * 32:(i % 2) * 32 + 32],
                                      PS[4 + (i % 2)][0:128, 0:32]))

    @block.sync
    def _(sync):
        for i in range(16):
            sync.wait_ge(ev.h, ev_ca0 + i + 1)
            stcs[i % 2].dinc(sync.dma_start(out=dram["ca_t"][i * 128:(i + 1) * 128, :],
                           in_=catt[:, (i % 2) * 32:(i % 2) * 32 + 32]))
        sync.wait_ge(dv.h, dv_scan)
        sto.dinc(sync.dma_start(out=dram["caT"][:, :], in_=caT[:, :]))

    @block.gpsimd
    def _(gpsimd):
        gpsimd.wait_ge(st.h, st.n)
        for sm in stxs + stcs:
            gpsimd.wait_ge(sm.h, sm.n)
        gpsimd.wait_ge(sto.h, sto.n)

    return {}



import dataclasses


def bc0(col, n):
    return dataclasses.replace(col, ap=[col.ap[0], [0, n]])


def host_inputs_phase2(I, core):
    b, q = divmod(core, 4)
    sel32 = np.eye(32, dtype=np.float32)
    um = np.zeros((128, 256), np.float32)
    for p in range(128):
        um[p, :p] = -3e38
    m1 = np.zeros((4, 1), np.float32)
    m1[:q, 0] = 1.0
    m2 = np.zeros((4, 4), np.float32)
    for r in range(4):
        for j in range(4):
            if r < j < q:
                m2[j, r] = 1.0
    idn = np.eye(128, dtype=np.float32)
    import ml_dtypes
    return {"sel32": sel32, "umaskw": um, "m1": m1, "m2": m2,
            "idn128b": idn.astype(ml_dtypes.bfloat16)}


def p2_dram_decls(ck, kind="ExternalOutput", p1kind="ExternalInput"):
    mk = ck.dout if kind == "ExternalOutput" else ck.dint
    mki = ck.din if p1kind == "ExternalInput" else ck.dint
    d = {}
    if p1kind == "ExternalInput":
        d["X_t"] = ck.din("X_t", [T, DIN], BF16)
        d["BC_t"] = ck.din("BC_t", [T, 128], BF16)
        d["B_n"] = ck.din("B_n", [64, T], BF16)
        d["C_n"] = ck.din("C_n", [64, T], BF16)
        d["caT"] = ck.din("caT", [NH0, T], F32)
        d["ca_t"] = ck.din("ca_t", [T, NH0], F32)
    d["sel32"] = ck.din("sel32", [32, 32], F32)
    d["umaskw"] = ck.din("umaskw", [128, 256], F32)
    d["m1"] = ck.din("m1", [4, 1], F32)
    d["m2"] = ck.din("m2", [4, 4], F32)
    d["Y"] = mk("Y", [T, DIN], F32)
    d["cc2_in"] = ck.dint("cc2_in", [65, T], F32)
    d["cc2_out"] = ck.dint("cc2_out", [4 * 65, T], F32)
    return d


def emit_phase2(ck, block, dram, sb=None, groups=None):
    nc = ck.nc
    pe = ck.sem("g_pe")
    dv = ck.sem("g_dv")
    ac = ck.sem("g_ac")
    ev = ck.sem("g_ev")
    ld2 = ck.sem("p2_ld")
    gld2 = ck.sem("p2_gld")
    cc = ck.sem("p2_cc")
    ccd = ck.sem("p2_ccd")
    stys = [ck.sem(f"p2_sty{j}") for j in range(2)]

    NSC = 8
    Xs = ck.sb("p2_X", [128, 16 * DIN], BF16)
    Bt = ck.sb("p2_Bt", [128, 16 * 64], BF16)
    Bn = ck.sb("p2_Bn", [64, T], BF16)
    Cn = ck.sb("p2_Cn", [64, T], BF16)
    caT = ck.sb("p2_caT", [32, T], F32)
    ca_t = ck.sb("p2_cat", [128, 16 * 32], F32)
    nca_t = ck.sb("p2_ncat", [128, 16 * 32], F32)
    ES = ck.sb("p2_es", [128, 16 * 32], F32)
    TC = ck.sb("p2_tc", [32, 2 * NSC], F32)
    L = ck.sb("p2_L", [64, DIN], F32)
    P = ck.sb("p2_P", [64, DIN], F32)
    P16 = ck.sb("p2_P16", [64, 64], BF16)
    Lg = ck.sb("p2_Lg", [64, DIN], F32)
    W4 = ck.sb("p2_W4", [4, 32], F32)
    W4T = ck.sb("p2_W4T", [32, 4], F32)
    wcols = ck.sb("p2_wcols", [64, 32], F32)
    TOT = ck.sb("p2_TOT", [4, 32], F32)
    sel32 = ck.sb("p2_sel", [32, 32], F32)
    seldg = ck.sb("p2_seldg", [32, 32], F32)
    ones32c = ck.sb("p2_ones32c", [32, 1], F32)
    umaskw = ck.sb("p2_umask", [128, 256], F32)
    m1 = ck.sb("p2_m1", [4, 1], F32)
    m2 = ck.sb("p2_m2", [4, 4], F32)
    idnf4 = ck.sb("p2_idnf4", [4, 4], F32)
    zcol = ck.sb("p2_zcol", [128, 1], F32)
    arg0 = ck.sb("p2_arg0", [128, 2 * 256], F32)
    arg1 = ck.sb("p2_arg1", [128, 2 * 128], F32)
    SD0 = ck.sb("p2_sd0", [128, 2 * 256], BF16)
    SD1 = ck.sb("p2_sd1", [128, 2 * 128], BF16)
    Xd = ck.sb("p2_xd", [128, 2 * 128], BF16)
    E2c = ck.sb("p2_e2c", [128, 2 * 2], F32)
    tcc = ck.sb("p2_tcc", [64, 2], F32)
    noffc = ck.sb("p2_noffc", [128, 2], F32)
    zrow = ck.sb("p2_zrow", [1, T], F32)
    Ych = ck.sb("p2_ych", [128, 2 * 2 * DIN], F32)
    ytmp = ck.sb("p2_ytmp", [128, 2 * 64], F32)
    xdw = ck.sb("p2_xdw", [128, 16 * 64], BF16)
    PS = ck.PS

    @block.sync
    def _(sync):
        for i in range(16):
            ld2.dinc(sync.dma_start(out=Xs[:, i * DIN:(i + 1) * DIN],
                                    in_=dram["X_t"][i * 128:(i + 1) * 128, :]))
            ld2.dinc(sync.dma_start(out=Bt[:, i * 64:(i + 1) * 64],
                                    in_=dram["BC_t"][i * 128:(i + 1) * 128, 0:64]))
            ld2.dinc(sync.dma_start(out=ca_t[:, i * 32:(i + 1) * 32],
                                    in_=dram["ca_t"][i * 128:(i + 1) * 128, :]))
        ld2.dinc(sync.dma_start(out=Bn[:, :], in_=dram["B_n"][:, :]))
        ld2.dinc(sync.dma_start(out=Cn[:, :], in_=dram["C_n"][:, :]))
        ld2.dinc(sync.dma_start(out=caT[:, :], in_=dram["caT"][:, :]))
        ld2.dinc(sync.dma_start(out=sel32[:, :], in_=dram["sel32"][:, :]))
        ld2.dinc(sync.dma_start(out=umaskw[:, :], in_=dram["umaskw"][:, :]))
        ld2.dinc(sync.dma_start(out=m1[:, :], in_=dram["m1"][:, :]))
        ld2.dinc(sync.dma_start(out=m2[:, :], in_=dram["m2"][:, :]))
        ld2.dinc(sync.dma_start(out=idnf4[:, :], in_=dram["sel32"][0:4, 0:4]))
    ld_all = ld2.n

    dv_pro0 = dv.n
    @block.vector
    def _(vector):
        vector.wait_ge(ld2.h, ld_all)
        vector.memset(zcol[:, :], 0.0)
        vector.memset(zrow[:, :], 0.0)
        vector.memset(ones32c[:, :], 1.0)
        dv.inc(vector.tensor_scalar_mul(nca_t[:, :], ca_t[:, :], -1.0))
        dv.inc(vector.tensor_scalar_mul(seldg[:, :], sel32[:, :],
                                        caT[0:32, T - 1:T]))
        vector.tensor_copy(TC[:, 0:NSC], caT[0:32, QSUB - 1:T:QSUB])
        vector.memset(TC[:, NSC:NSC + 1], 0.0)
        dv.inc(vector.tensor_copy(TC[:, NSC + 1:2 * NSC], caT[0:32, QSUB - 1:T - QSUB:QSUB]))
        dv.flush(vector)
        dv.inc(vector.tensor_sub(TC[:, 0:NSC], TC[:, 0:NSC], TC[:, NSC:2 * NSC]))
    dv_pro = dv.n

    pe_pro0 = pe.n
    @block.tensor
    def _(tensor):
        tensor.wait_ge(dv.h, dv_pro)
        pe.inc(tensor.matmul(PS[4][:, 0:32], bc0(ones32c[:, 0:1], 128),
                             seldg[:, :], start=True, stop=True))
    pe_pro = pe.n

    ac_pro0 = ac.n
    @block.scalar
    def _(scalar):
        scalar.wait_ge(dv.h, dv_pro)
        ac.inc(scalar.activation(out=TC[:, 0:NSC], in_=TC[:, 0:NSC], func=AF.Exp))
    ac_pro = ac.n

    dv_es0 = dv.n
    @block.vector
    def _(vector):
        vector.wait_ge(pe.h, pe_pro)
        for i in range(15):
            vector.tensor_add(ES[:, i * 32:(i + 1) * 32],
                              nca_t[:, i * 32:(i + 1) * 32], PS[4][:, 0:32])
        dv.inc(vector.tensor_add(ES[:, 15 * 32:16 * 32],
                                 nca_t[:, 15 * 32:16 * 32], PS[4][:, 0:32]))
    dv_es = dv.n

    ac_es0 = ac.n
    @block.scalar
    def _(scalar):
        scalar.wait_ge(dv.h, dv_es)
        ac.inc(scalar.activation(out=ES[:, :], in_=ES[:, :], func=AF.Exp))
    ac_es = ac.n

    dvA = {}
    peA = {}
    evA = {}
    for h in range(32):
        @block.vector
        def _(vector, h=h):
            vector.wait_ge(ac.h, ac_es)
            if h >= 1:
                vector.wait_ge(pe.h, peA[h - 1])
            for i in range(16):
                inst = vector.tensor_scalar_mul(
                    xdw[:, i * 64:(i + 1) * 64],
                    Xs[:, i * DIN + h * 64: i * DIN + h * 64 + 64],
                    ES[:, i * 32 + h: i * 32 + h + 1])
            dv.inc(inst)
            dvA[h] = dv.n

        @block.tensor
        def _(tensor, h=h):
            tensor.wait_ge(dv.h, dvA[h])
            if h >= 2:
                tensor.wait_ge(ev.h, evA[h - 2])
            for i in range(15):
                tensor.matmul(PS[h % 2][0:64, 0:64], Bt[:, i * 64:(i + 1) * 64],
                              xdw[:, i * 64:(i + 1) * 64], start=(i == 0), stop=False)
            pe.inc(tensor.matmul(PS[h % 2][0:64, 0:64], Bt[:, 15 * 64:16 * 64],
                                 xdw[:, 15 * 64:16 * 64], start=False, stop=True))
            peA[h] = pe.n

        @block.vector
        def _(vector, h=h):
            vector.wait_ge(pe.h, peA[h])
            ev.inc(vector.tensor_copy(L[:, h * 64:(h + 1) * 64], PS[h % 2][0:64, 0:64]))
            evA[h] = ev.n
    ev_passA = ev.n

    @block.gpsimd
    def _(gpsimd):
        gpsimd.wait_ge(ev.h, ev_passA)
        ccd.dinc(gpsimd.dma_start(out=dram["cc2_in"][0:64, :], in_=L[:, :]))
        ccd.dinc(gpsimd.dma_start(out=dram["cc2_in"][64:65, 0:32],
                                  in_=caT[0:32, T - 1:T]))
        ccd.dinc(gpsimd.dma_start(out=dram["cc2_in"][64:65, 32:T],
                                  in_=zrow[:, 32:T]))
        gpsimd.wait_ge(ccd.h, ccd.n)
        gpsimd.collective_compute(
            "AllGather", ALU.bypass,
            replica_groups=groups or [[0, 1, 2, 3], [4, 5, 6, 7]],
            ins=[dram["cc2_in"].ap().opt()],
            outs=[dram["cc2_out"].ap().opt()],
        ).then_inc(cc.h, 1)
        cc.n += 1
        gpsimd.wait_ge(cc.h, cc.n)
        src = dram["cc2_out"].ap().rearrange("(r x) t -> r x t", x=65)
        ccd.dinc(gpsimd.dma_start(out=TOT[:, :], in_=src[:, 64, 0:32]))
    ccd_tot = ccd.n

    dv_cb0 = dv.n
    @block.vector
    def _(vector):
        vector.wait_ge(ccd.h, ccd_tot)
        dv.inc(vector.memset(P[:, :], 0.0))
    dv_pz = dv.n

    pe_cb0 = pe.n
    @block.tensor
    def _(tensor):
        tensor.wait_ge(ccd.h, ccd_tot)
        pe.inc(tensor.matmul(PS[4][0:4, 32:64], m2[:, :], TOT[:, :],
                             start=True, stop=True))
    pe_w4 = pe.n

    @block.scalar
    def _(scalar):
        scalar.wait_ge(pe.h, pe_w4)
        ac.inc(scalar.activation(out=W4[:, :], in_=PS[4][0:4, 32:64], func=AF.Exp))
    ac_w4 = ac.n

    @block.vector
    def _(vector):
        vector.wait_ge(ac.h, ac_w4)
        dv.inc(vector.tensor_scalar_mul(W4[:, :], W4[:, :], m1[0:4, 0:1]))
    dv_w4 = dv.n

    @block.tensor
    def _(tensor):
        tensor.wait_ge(dv.h, dv_w4)
        pe.inc(tensor.transpose(PS[5][0:32, 0:4], W4[:, :], idnf4[:, :]))
    pe_w4t = pe.n

    @block.vector
    def _(vector):
        vector.wait_ge(pe.h, pe_w4t)
        dv.inc(vector.tensor_copy(W4T[:, :], PS[5][0:32, 0:4]))
    dv_w4t = dv.n

    ccd_lg = {}
    dv_stt = {}
    pe_wc = {}
    for r in range(4):
        @block.gpsimd
        def _(gpsimd, r=r):
            if r >= 1:
                gpsimd.wait_ge(dv.h, dv_stt.get(r - 1, dv_w4t))
            src = dram["cc2_out"].ap().rearrange("(r x) t -> r x t", x=65)
            ccd.dinc(gpsimd.dma_start(out=Lg[:, :], in_=src[r, 0:64, :]))
            ccd_lg[r] = ccd.n

        @block.tensor
        def _(tensor, r=r):
            tensor.wait_ge(dv.h, dv_w4t)
            if r >= 1:
                tensor.wait_ge(dv.h, dv_stt.get(r - 1, 0))
            for h in range(31):
                tensor.matmul(PS[(r % 2)][0:64, 256 + h:256 + h + 1],
                              bc0(sel32[:, h:h + 1], 64), W4T[0:32, r:r + 1],
                              start=True, stop=True)
            pe.inc(tensor.matmul(PS[(r % 2)][0:64, 256 + 31:256 + 32],
                                 bc0(sel32[:, 31:32], 64), W4T[0:32, r:r + 1],
                                 start=True, stop=True))
            pe_wc[r] = pe.n

        @block.vector
        def _(vector, r=r):
            vector.wait_ge(pe.h, pe_wc[r])
            dv.inc(vector.tensor_copy(wcols[:, :], PS[(r % 2)][0:64, 256:288]))
            dv.flush(vector)
            vector.wait_ge(ccd.h, ccd_lg[r])
            for h in range(31):
                vector.scalar_tensor_tensor(
                    P[:, h * 64:(h + 1) * 64], Lg[:, h * 64:(h + 1) * 64],
                    wcols[:, h:h + 1], P[:, h * 64:(h + 1) * 64],
                    ALU.mult, ALU.add)
            dv.inc(vector.scalar_tensor_tensor(
                P[:, 31 * 64:32 * 64], Lg[:, 31 * 64:32 * 64],
                wcols[:, 31:32], P[:, 31 * 64:32 * 64],
                ALU.mult, ALU.add))
            dv_stt[r] = dv.n
    dv_P0 = dv.n

    nsel32 = ck.sb("p2_nsel", [32, 32], F32)
    ESB = ck.sb("p2_esb", [128, 16 * 32], F32)
    TCe = ck.sb("p2_tce", [32, NSC], F32)

    dv_pb0 = dv.n
    @block.vector
    def _(vector):
        vector.wait_ge(ld2.h, ld_all)
        vector.tensor_copy(TCe[:, :], caT[0:32, QSUB - 1:T:QSUB])
        dv.inc(vector.tensor_scalar_mul(nsel32[:, :], sel32[:, :], -1.0))
    dv_nsel = dv.n

    pe_esb = {}
    dv_esb = {}
    for c in range(NSC):
        @block.vector
        def _(vector, c=c):
            if c == 0:
                vector.wait_ge(dv.h, dv_nsel)
            dv.inc(vector.tensor_scalar_mul(seldg[:, :], sel32[:, :],
                                            TCe[0:32, c:c + 1]))
            dv_esb[(c, 'd')] = dv.n

        @block.tensor
        def _(tensor, c=c):
            tensor.wait_ge(dv.h, dv_esb[(c, 'd')])
            if c >= 1:
                tensor.wait_ge(dv.h, dv_esb[(c - 1, 'a')])
            pe.inc(tensor.matmul(PS[5][:, 0:32], bc0(ones32c[:, 0:1], 128),
                                 seldg[:, :], start=True, stop=True))
            pe_esb[c] = pe.n

        @block.vector
        def _(vector, c=c):
            vector.wait_ge(pe.h, pe_esb[c])
            vector.tensor_add(ESB[:, (2 * c) * 32:(2 * c + 1) * 32],
                              nca_t[:, (2 * c) * 32:(2 * c + 1) * 32], PS[5][:, 0:32])
            dv.inc(vector.tensor_add(ESB[:, (2 * c + 1) * 32:(2 * c + 2) * 32],
                                     nca_t[:, (2 * c + 1) * 32:(2 * c + 2) * 32],
                                     PS[5][:, 0:32]))
            dv_esb[(c, 'a')] = dv.n

    @block.scalar
    def _(scalar):
        scalar.wait_ge(dv.h, dv_esb[(NSC - 1, 'a')])
        ac.inc(scalar.activation(out=ESB[:, :], in_=ESB[:, :], func=AF.Exp))
    ac_esb = ac.n

    Tn = NSC * 32
    pe_p1, pe_p2 = {}, {}
    dv_d1a, dv_d1b, dv_d2 = {}, {}, {}
    ac_e = {}

    def emit_p2_tail(tensor, tp):
        cp, hp = divmod(tp, 32)
        pp = tp % 2
        l0p = QSUB * cp
        t0p, t1p = 2 * cp, 2 * cp + 1
        tensor.wait_ge(dv.h, dv_d1b[tp])
        if tp >= 2:
            tensor.wait_ge(dv.h, dv_d2[tp - 2])
        tensor.matmul(PS[4 + pp][:, 0:64], SD0[:, pp * 256:pp * 256 + 128],
                      Xs[:, t0p * DIN + hp * 64:t0p * DIN + hp * 64 + 64],
                      start=True, stop=True)
        tensor.matmul(PS[4 + pp][:, 64:128], SD0[:, pp * 256 + 128:pp * 256 + 256],
                      Xs[:, t0p * DIN + hp * 64:t0p * DIN + hp * 64 + 64],
                      start=True, stop=False)
        tensor.matmul(PS[4 + pp][:, 64:128], SD1[:, pp * 128:pp * 128 + 128],
                      Xs[:, t1p * DIN + hp * 64:t1p * DIN + hp * 64 + 64],
                      start=False, stop=True)
        tensor.matmul(PS[4 + pp][:, 128:192], Cn[:, l0p:l0p + 128],
                      P16[:, :], start=True, stop=True)
        tensor.matmul(PS[4 + pp][:, 192:256], Cn[:, l0p + 128:l0p + 256],
                      P16[:, :], start=True, stop=True)
        tensor.matmul(PS[4 + pp][0:64, 256:320], Bt[:, t0p * 64:(t0p + 1) * 64],
                      Xd[:, pp * 128:pp * 128 + 64], start=True, stop=False)
        pe.inc(tensor.matmul(PS[4 + pp][0:64, 256:320], Bt[:, t1p * 64:(t1p + 1) * 64],
                             Xd[:, pp * 128 + 64:pp * 128 + 128], start=False, stop=True))
        pe_p2[tp] = pe.n

    for t in range(Tn + 1):
        c, h = divmod(t, 32)
        p = t % 2
        if t < Tn:
            l0 = QSUB * c
            tt0, tt1 = 2 * c, 2 * c + 1

            @block.tensor
            def _(tensor, t=t, c=c, h=h, p=p, l0=l0, tt0=tt0, tt1=tt1):
                if t == 0:
                    tensor.wait_ge(ac.h, ac_esb)
                    tensor.wait_ge(dv.h, dv_P0)
                if t >= 2:
                    tensor.wait_ge(dv.h, dv_d1b[t - 2])
                    tensor.wait_ge(ac.h, ac_e[t - 2])
                tensor.matmul(PS[2 + p][:, 0:256], bc0(sel32[:, h:h + 1], 128),
                              caT[0:32, l0:l0 + 256], start=True, stop=True)
                if c >= 1:
                    tensor.matmul(PS[2 + p][:, 256:257], bc0(nsel32[:, h:h + 1], 128),
                                  caT[0:32, l0 - 1:l0], start=True, stop=True)
                tensor.matmul(PS[2 + p][0:64, 260:261], bc0(sel32[:, h:h + 1], 64),
                              TC[0:32, c:c + 1], start=True, stop=True)
                tensor.matmul(PS[0 + p][:, 0:256], Bn[:, l0:l0 + 128],
                              Cn[:, l0:l0 + 256], start=True, stop=True)
                pe.inc(tensor.matmul(PS[0 + p][:, 256:384], Bn[:, l0 + 128:l0 + 256],
                                     Cn[:, l0 + 128:l0 + 256], start=True, stop=True))
                pe_p1[t] = pe.n
                if t >= 1:
                    emit_p2_tail(tensor, t - 1)

            @block.vector
            def _(vector, t=t, p=p):
                vector.wait_ge(pe.h, pe_p1[t])
                vector.tensor_add(arg0[:, p * 256:(p + 1) * 256],
                                  PS[2 + p][:, 0:256], umaskw[:, :])
                if t >= 32:
                    vector.tensor_copy(noffc[:, p:p + 1], PS[2 + p][:, 256:257])
                dv.inc(vector.tensor_add(arg1[:, p * 128:(p + 1) * 128],
                                         PS[2 + p][:, 128:256], umaskw[:, 0:128]))
                dv_d1a[t] = dv.n

            @block.scalar
            def _(scalar, t=t, c=c, h=h, p=p, tt0=tt0, tt1=tt1):
                scalar.wait_ge(dv.h, dv_d1a[t])
                scalar.activation(out=arg0[:, p * 256:(p + 1) * 256],
                                  in_=arg0[:, p * 256:(p + 1) * 256], func=AF.Exp,
                                  bias=nca_t[:, tt0 * 32 + h:tt0 * 32 + h + 1])
                scalar.activation(out=arg1[:, p * 128:(p + 1) * 128],
                                  in_=arg1[:, p * 128:(p + 1) * 128], func=AF.Exp,
                                  bias=nca_t[:, tt1 * 32 + h:tt1 * 32 + h + 1])
                e2bias = noffc[:, p:p + 1] if c >= 1 else zcol[:, 0:1]
                scalar.activation(out=E2c[:, p * 2:p * 2 + 1],
                                  in_=ca_t[:, tt0 * 32 + h:tt0 * 32 + h + 1],
                                  func=AF.Exp, bias=e2bias)
                ac.inc(scalar.activation(out=E2c[:, p * 2 + 1:p * 2 + 2],
                                         in_=ca_t[:, tt1 * 32 + h:tt1 * 32 + h + 1],
                                         func=AF.Exp, bias=e2bias))
                ac_e[t] = ac.n

            @block.vector
            def _(vector, t=t, c=c, h=h, p=p, tt0=tt0, tt1=tt1):
                vector.wait_ge(ac.h, ac_e[t])
                if t >= 1:
                    vector.wait_ge(pe.h, pe_p2[t - 1])
                if t >= 32:
                    vector.wait_ge(dv.h, dv_d2[t - 32])
                vector.tensor_mul(SD0[:, p * 256:(p + 1) * 256],
                                  arg0[:, p * 256:(p + 1) * 256], PS[0 + p][:, 0:256])
                vector.tensor_mul(SD1[:, p * 128:(p + 1) * 128],
                                  arg1[:, p * 128:(p + 1) * 128], PS[0 + p][:, 256:384])
                vector.tensor_scalar_mul(Xd[:, p * 128:p * 128 + 64],
                                         Xs[:, tt0 * DIN + h * 64:tt0 * DIN + h * 64 + 64],
                                         ESB[:, tt0 * 32 + h:tt0 * 32 + h + 1])
                vector.tensor_scalar_mul(Xd[:, p * 128 + 64:p * 128 + 128],
                                         Xs[:, tt1 * DIN + h * 64:tt1 * DIN + h * 64 + 64],
                                         ESB[:, tt1 * 32 + h:tt1 * 32 + h + 1])
                vector.tensor_copy(P16[:, :], P[:, h * 64:(h + 1) * 64])
                dv.inc(vector.tensor_copy(tcc[:, p:p + 1], PS[2 + p][0:64, 260:261]))
                dv_d1b[t] = dv.n

        if t == Tn:
            @block.tensor
            def _(tensor):
                emit_p2_tail(tensor, Tn - 1)

        if t >= 1:
            tp = t - 1
            cp, hp = divmod(tp, 32)
            pp = tp % 2

            @block.vector
            def _(vector, tp=tp, cp=cp, hp=hp, pp=pp):
                vector.wait_ge(pe.h, pe_p2[tp])
                if hp == 0 and cp >= 2:
                    vector.wait_ge(stys[cp % 2].h, 16 * 2 * (cp // 2))
                ypar = cp % 2
                for lt in range(2):
                    vector.tensor_scalar_mul(
                        ytmp[:, lt * 64:(lt + 1) * 64],
                        PS[4 + pp][:, 128 + lt * 64:128 + (lt + 1) * 64],
                        E2c[:, pp * 2 + lt:pp * 2 + lt + 1])
                for lt in range(2):
                    vector.tensor_add(
                        Ych[:, (ypar * 2 + lt) * DIN + hp * 64:(ypar * 2 + lt) * DIN + hp * 64 + 64],
                        ytmp[:, lt * 64:(lt + 1) * 64],
                        PS[4 + pp][:, lt * 64:(lt + 1) * 64])
                dv.inc(vector.scalar_tensor_tensor(
                    P[:, hp * 64:(hp + 1) * 64], P[:, hp * 64:(hp + 1) * 64],
                    tcc[:, pp:pp + 1], PS[4 + pp][0:64, 256:320],
                    ALU.mult, ALU.add))
                dv_d2[tp] = dv.n

        if t % 32 == 0 and t >= 32:
            cs = t // 32 - 1

            @block.sync
            def _(sync, cs=cs):
                sync.wait_ge(dv.h, dv_d2[cs * 32 + 31])
                ypar = cs % 2
                for tt in range(2):
                    stys[ypar].dinc(sync.dma_start(
                        out=dram["Y"][(2 * cs + tt) * 128:(2 * cs + tt + 1) * 128, :],
                        in_=Ych[:, (ypar * 2 + tt) * DIN:(ypar * 2 + tt + 1) * DIN]))
    @block.sync
    def _(sync):
        sync.wait_ge(dv.h, dv_d2[Tn - 1])
        ypar = (NSC - 1) % 2
        for tt in range(2):
            stys[ypar].dinc(sync.dma_start(
                out=dram["Y"][(2 * (NSC - 1) + tt) * 128:(2 * (NSC - 1) + tt + 1) * 128, :],
                in_=Ych[:, (ypar * 2 + tt) * DIN:(ypar * 2 + tt + 1) * DIN]))

    @block.gpsimd
    def _(gpsimd):
        for sm in stys:
            gpsimd.wait_ge(sm.h, sm.n)
    return {}


def _p2_make_tail(ck, dram):
    pass



def host_inputs_phase34(I, core):
    b, q = divmod(core, 4)
    W2h = np.ascontiguousarray(np.asarray(I["L0_attn_W_in2"]).T)
    W3h = np.ascontiguousarray(np.asarray(I["L0_attn_W_in3"]).T)
    g = np.asarray(I["L0_attn_ln_g"]).astype(np.float32)
    bt = np.asarray(I["L0_attn_ln_b"]).astype(np.float32)
    gbc = np.ascontiguousarray(np.broadcast_to(g[None, :], (128, DIN)))
    bbc = np.ascontiguousarray(np.broadcast_to(bt[None, :], (128, DIN)))
    WgT = np.ascontiguousarray(np.asarray(I["L0_ffn_Wg"]).T)
    W1 = np.asarray(I["L0_ffn_W1"])
    W1il = np.concatenate([
        np.concatenate([W1[:, :, k * 512:(k + 1) * 512],
                        W1[:, :, 2048 + k * 512:2048 + (k + 1) * 512]], axis=2)
        for k in range(4)], axis=2)
    W2e = np.asarray(I["L0_ffn_W2"])
    m8 = np.zeros((8, 1), np.float32)
    m8[:core, 0] = 1.0
    trius = np.triu(np.ones((128, 128), np.float32), 1)
    return {"w_in2T": W2h.astype(np.float32), "w_in3T": W3h.astype(np.float32),
            "ln_g_bc": gbc, "ln_b_bc": bbc, "WgT": WgT.astype(np.float32),
            "w1il": np.ascontiguousarray(W1il).astype(np.float32),
            "w2e": np.ascontiguousarray(W2e).astype(np.float32),
            "m8": m8, "trius": trius}


def p34_dram_decls(ck, kind="ExternalOutput", p1kind="ExternalInput", ngather=8):
    mk = ck.dout if kind == "ExternalOutput" else ck.dint
    d = {}
    if p1kind == "ExternalInput":
        d["u_loc"] = ck.din("u_loc", [17 * 128, D], F32)
        d["idn128"] = ck.din("idn128", [128, 128], BF16)
        d["xn0T"] = ck.din("xn0T", [D, T], BF16)
        d["Y"] = ck.din("Y", [T, DIN], F32)
    d["w_in2T"] = ck.din("w_in2T", [D, DIN], F32)
    d["w_in3T"] = ck.din("w_in3T", [DIN, D], F32)
    d["ln_g_bc"] = ck.din("ln_g_bc", [128, DIN], F32)
    d["ln_b_bc"] = ck.din("ln_b_bc", [128, DIN], F32)
    d["WgT"] = ck.din("WgT", [D, 8], F32)
    d["w1il"] = ck.din("w1il", [8, D, 4096], F32)
    d["w2e"] = ck.din("w2e", [8, DIN, D], F32)
    d["m8"] = ck.din("m8", [8, 1], F32)
    d["trius"] = ck.din("trius", [128, 128], F32)
    d["x1"] = mk("x1", [T, D], F32)
    d["cc4_in"] = ck.dint("cc4_in", [1, 8], F32)
    d["cc4_out"] = ck.dint("cc4_out", [ngather, 8], F32)
    return d


def emit_phase34(ck, block, dram, groups=None, cap=5120.0, nexperts=8, dbg=False):
    nc = ck.nc
    pe = ck.sem("g_pe")
    dv = ck.sem("g_dv")
    ac = ck.sem("g_ac")
    ev = ck.sem("g_ev")
    ld3 = ck.sem("p3_ld")
    uld = [ck.sem(f"p3_uld{j}") for j in range(2)]
    yld = [ck.sem(f"p3_yld{j}") for j in range(2)]
    wld3 = ck.sem("p3_wld")
    w1ld = [ck.sem(f"p4_w1ld{j}") for j in range(2)]
    cc4 = ck.sem("p4_cc")
    ccd4 = ck.sem("p4_ccd")
    st3 = [ck.sem(f"p3_st{j}") for j in range(2)]

    xn0T = ck.sb("p34_xn0T", [128, 8 * T], BF16)
    abf = ck.sb("p34_abf", [128, 60 * 1024], BF16)
    af32 = ck.sb("p34_af32", [128, 12 * 1024], F32)
    w2T = abf[:, 0:16 * 1024]
    w3T = abf[:, 16 * 1024:32 * 1024]
    hnb = abf[:, 32 * 1024:34 * 1024]
    hnT = abf[:, 34 * 1024:36 * 1024]
    gbc = af32[:, 0:2048]
    bbc = af32[:, 2048:4096]
    hbuf = af32[:, 4096:6144]
    ybuf = af32[:, 6144:8192]
    x1st = af32[:, 8192:9216]
    ubuf = af32[:, 9216:11264]
    w1b = abf[:, 0:32 * 1024]
    w2b = abf[:, 32 * 1024:48 * 1024]
    hgT = abf[:, 48 * 1024:56 * 1024]
    yTb = abf[:, 56 * 1024:60 * 1024]
    stage = af32[:, 0:2048]
    stat = ck.sb("p34_stat", [128, 8], F32)
    idn = ck.sb("p34_idn", [128, 128], BF16)
    PS = ck.PS
    PB = ck.PB

    NTl = 16

    @block.sync
    def _(sync):
        for j in range(8):
            ld3.dinc(sync.dma_start(out=xn0T[:, j * T:(j + 1) * T],
                                    in_=dram["xn0T"][j * 128:(j + 1) * 128, :]))
        ld3.dinc(sync.dma_start(out=gbc[:, :], in_=dram["ln_g_bc"][:, :]))
        ld3.dinc(sync.dma_start(out=bbc[:, :], in_=dram["ln_b_bc"][:, :]))
    ld3_a = ld3.n

    @block.gpsimd
    def _(gpsimd):
        wld3.dinc(gpsimd.dma_start(out=idn[:, :], in_=dram["idn128"][:, :]))
        for j in range(8):
            wld3.dinc(gpsimd.dma_start(out=w2T[:, j * DIN:(j + 1) * DIN],
                             in_=dram["w_in2T"][j * 128:(j + 1) * 128, :]))
        for j in range(16):
            wld3.dinc(gpsimd.dma_start(out=w3T[:, j * D:(j + 1) * D],
                             in_=dram["w_in3T"][j * 128:(j + 1) * 128, :]))
    wld3_a = wld3.n

    dv_p3 = {}
    @block.sync
    def _(sync):
        sl = 0
        uld[0].dinc(sync.dma_start(out=ubuf[:, 0:D],
                     in_=dram["u_loc"][HALO:HALO + 128, :]))
        uld[1].dinc(sync.dma_start(out=ubuf[:, D:2 * D],
                     in_=dram["u_loc"][HALO + 128:HALO + 256, :]))
        yld[0].dinc(sync.dma_start(out=ybuf[:, 0:DIN],
                     in_=dram["Y"][0:128, :]))

    import types
    ns = types.SimpleNamespace(**{k: v for k, v in locals().items() if k != 'ns'})
    return _p34_body(ns)



def p5_dram_decls(ck, kind="ExternalOutput", xin="ExternalInput"):
    d = {}
    if xin == "ExternalInput":
        d["x1"] = ck.din("x1", [T, D], F32)
        d["idn128"] = ck.din("idn128", [128, 128], BF16)
    mk = ck.dout if kind == "ExternalOutput" else ck.dint
    d["xn1T"] = mk("xn1T", [D, T], BF16)
    d["cc5_in"] = ck.dint("cc5_in", [D, T], BF16)
    d["cc5_out"] = ck.dint("cc5_out", [4 * D, T], BF16)
    return d


def emit_phase5(ck, block, dram, groups=None):
    nc = ck.nc
    pe = ck.sem("g_pe")
    dv = ck.sem("g_dv")
    ac = ck.sem("g_ac")
    ev = ck.sem("g_ev")
    ld5 = [ck.sem(f"p5_ld{j}") for j in range(3)]
    il5 = ck.sem("p5_il")
    st5 = ck.sem("p5_st")
    cc5 = ck.sem("p5_cc")
    ccd5 = ck.sem("p5_ccd")

    x_t = ck.sb("p5_x", [128, 3 * D], F32)
    mu = ck.sb("p5_mu", [128, 16], F32)
    nmu = ck.sb("p5_nmu", [128, 16], F32)
    rstd = ck.sb("p5_rstd", [128, 16], F32)
    sq = ck.sb("p5_sq", [128, D], F32)
    var = ck.sb("p5_var", [128, 16], F32)
    xn_t = ck.sb("p5_xnt", [128, 2 * D], BF16)
    idn = ck.sb("p5_idn", [128, 128], BF16)
    xn1T = ck.sb("p5_xn1T", [128, 8 * T], BF16)
    PS = ck.PS
    PB = ck.PB
    NT5 = 16

    dv0 = dv.n
    dvc_s1, dvc_s3, dvc_s5 = {}, {}, {}
    acc_s2, acc_s4 = {}, {}
    _dv = dv0
    for i in range(NT5 + 2):
        if i <= NT5 - 1:
            _dv += 2; dvc_s1[i] = _dv
        if 1 <= i <= NT5:
            _dv += 3; dvc_s3[i - 1] = _dv
        if 2 <= i <= NT5 + 1:
            _dv += 1; dvc_s5[i - 2] = _dv
    _ac = ac.n
    for i in range(NT5 + 1):
        if i <= NT5 - 1:
            _ac += 1; acc_s2[i] = _ac
        if 1 <= i <= NT5:
            _ac += 1; acc_s4[i - 1] = _ac

    @block.sync
    def _(sync):
        for i in range(NT5):
            if i >= 3:
                sync.wait_ge(dv.h, dvc_s5[i - 3])
            ld5[i % 3].dinc(sync.dma_start(out=x_t[:, (i % 3) * D:(i % 3 + 1) * D],
                            in_=dram["x1"][i * 128:(i + 1) * 128, :]))

    @block.gpsimd
    def _(gpsimd):
        il5.dinc(gpsimd.dma_start(out=idn[:, :], in_=dram["idn128"][:, :]))

    pe0 = pe.n
    pe_tr = lambda i: pe0 + 8 * (i + 1)

    def s1(vector, i):
        vector.wait_ge(ld5[i % 3].h, 16 * (i // 3 + 1))
        ut = x_t[:, (i % 3) * D:(i % 3 + 1) * D]
        dv.inc(vector.reduce_sum(mu[:, i:i + 1], ut, axis=AX.X))
        dv.flush(vector)
        dv.inc(vector.tensor_scalar_mul(nmu[:, i:i + 1], mu[:, i:i + 1], -1.0 / D))
        assert dv.n == dvc_s1[i]

    def s3(vector, i):
        vector.wait_ge(ac.h, acc_s2[i])
        dv.inc(vector.reduce_sum(var[:, i:i + 1], sq[:, :], axis=AX.X))
        dv.flush(vector)
        dv.inc(vector.tensor_scalar(var[:, i:i + 1], var[:, i:i + 1], 1.0 / D, 1e-5,
                                    ALU.mult, ALU.add))
        dv.flush(vector)
        dv.inc(vector.reciprocal(rstd[:, i:i + 1], var[:, i:i + 1]))
        assert dv.n == dvc_s3[i]

    def s5(vector, i):
        vector.wait_ge(ac.h, acc_s4[i])
        if i >= 2:
            vector.wait_ge(pe.h, pe_tr(i - 2))
        ut = x_t[:, (i % 3) * D:(i % 3 + 1) * D]
        dv.inc(vector.tensor_scalar(xn_t[:, (i % 2) * D:(i % 2 + 1) * D], ut,
                                    nmu[:, i:i + 1], rstd[:, i:i + 1],
                                    ALU.add, ALU.mult))
        assert dv.n == dvc_s5[i]

    ev_tr0 = ev.n

    def evtr(vector, j):
        for jd in range(8):
            k = j * 8 + jd
            vector.wait_ge(pe.h, pe0 + k + 1)
            ev.inc(vector.tensor_copy(
                xn1T[:, jd * T + j * 128: jd * T + (j + 1) * 128],
                PB[k % 2][:, 0:128]))

    @block.vector
    def _(vector):
        for i in range(NT5 + 3):
            if i <= NT5 - 1:
                s1(vector, i)
            if 1 <= i <= NT5:
                s3(vector, i - 1)
            if 2 <= i <= NT5 + 1:
                s5(vector, i - 2)
            if 3 <= i <= NT5 + 2:
                evtr(vector, i - 3)

    @block.scalar
    def _(scalar):
        for i in range(NT5 + 1):
            if i <= NT5 - 1:
                scalar.wait_ge(dv.h, dvc_s1[i])
                if i >= 1:
                    scalar.wait_ge(dv.h, dvc_s3[i - 1])
                ut = x_t[:, (i % 3) * D:(i % 3 + 1) * D]
                ac.inc(scalar.activation(out=sq[:, :], in_=ut, func=AF.Square,
                                         bias=nmu[:, i:i + 1]))
                assert ac.n == acc_s2[i]
            if 1 <= i <= NT5:
                j = i - 1
                scalar.wait_ge(dv.h, dvc_s3[j])
                ac.inc(scalar.activation(out=rstd[:, j:j + 1], in_=rstd[:, j:j + 1],
                                         func=AF.Sqrt))
                assert ac.n == acc_s4[j]

    @block.tensor
    def _(tensor):
        tensor.wait_ge(il5.h, 16)
        for i in range(NT5):
            tensor.wait_ge(dv.h, dvc_s5[i])
            for jd in range(8):
                k = i * 8 + jd
                if k >= 2:
                    tensor.wait_ge(ev.h, ev_tr0 + k - 1)
                pe.inc(tensor.transpose(
                    PB[k % 2][:, 0:128],
                    xn_t[:, (i % 2) * D + jd * 128:(i % 2) * D + (jd + 1) * 128],
                    idn[:, :]))

    ev_end = ev.n

    @block.sync
    def _(sync):
        for jd in range(8):
            sync.wait_ge(ev.h, ev_tr0 + (NT5 - 1) * 8 + jd + 1)
            st5.dinc(sync.dma_start(out=dram["xn1T"][jd * 128:(jd + 1) * 128, :],
                     in_=xn1T[:, jd * T:(jd + 1) * T]))

    @block.gpsimd
    def _(gpsimd):
        gpsimd.wait_ge(ev.h, ev_end)
        for jd in range(8):
            ccd5.dinc(gpsimd.dma_start(out=dram["cc5_in"][jd * 128:(jd + 1) * 128, :],
                      in_=xn1T[:, jd * T:(jd + 1) * T]))
        gpsimd.wait_ge(ccd5.h, ccd5.n)
        gpsimd.collective_compute(
            "AllGather", ALU.bypass,
            replica_groups=groups or [[0, 1, 2, 3], [4, 5, 6, 7]],
            ins=[dram["cc5_in"].ap().opt()],
            outs=[dram["cc5_out"].ap().opt()],
        ).then_inc(cc5.h, 1)
        cc5.n += 1
        gpsimd.wait_ge(cc5.h, cc5.n)
        gpsimd.wait_ge(st5.h, st5.n)
    return {}


"""Body of phase 3+4 (P3 per-tile chain, MoE routing, experts)."""


"""Body of phase 3+4 (P3 per-tile chain, MoE routing, experts)."""



def _p34_body(ns):
    cap = getattr(ns, 'cap', 5120.0)
    nexperts = getattr(ns, 'nexperts', 8)
    dbg = getattr(ns, 'dbg', False)
    ck, block, dram, groups = ns.ck, ns.block, ns.dram, ns.groups
    nc = ns.nc
    pe, dv, ac, ev = ns.pe, ns.dv, ns.ac, ns.ev
    ld3, uld, yld, wld3, w1ld = ns.ld3, ns.uld, ns.yld, ns.wld3, ns.w1ld
    cc4, ccd4 = ns.cc4, ns.ccd4
    xn0T, w2T, w3T = ns.xn0T, ns.w2T, ns.w3T
    gbc, bbc, ubuf, ybuf, hbuf, hnb, hnT, stat, idn = (
        ns.gbc, ns.bbc, ns.ubuf, ns.ybuf, ns.hbuf, ns.hnb, ns.hnT, ns.stat, ns.idn)
    PS, PB = ns.PS, ns.PB
    NTl = ns.NTl
    ld3_a, wld3_a = ns.ld3_a, ns.wld3_a
    dv_p3 = ns.dv_p3

    u2b2 = ck.sb("p34_u2b2", [128, 2 * 512], F32)
    w1b, w2b, hgT, stage = ns.w1b, ns.w2b, ns.hgT, ns.stage
    yTb = ns.yTb
    x1st = ns.x1st
    logitsT = ns.af32[0:8, 4096:6144]
    logits_t = ck.sb("p34_lg", [128, 16 * 8], F32)
    w_all = ck.sb("p34_w", [128, 16 * 8], F32)
    H2 = ck.sb("p34_h2", [128, 16 * 8], F32)
    eq1 = ck.sb("p34_eq1", [128, 8], F32)
    crows = ck.sb("p34_crow", [1, 16 * 8], F32)
    brow = ck.sb("p34_brow", [1, 8], F32)
    wgT = ck.sb("p34_wgT", [128, 8 * 8], BF16)
    m8 = ck.sb("p34_m8", [8, 1], F32)
    trius = ck.sb("p34_trius", [128, 128], F32)
    idnf = ck.sb("p34_idnf", [128, 128], F32)
    ones1 = ck.sb("p34_ones1", [128, 1], F32)
    onesr = ns.af32[0:1, 6144:6272]
    cntg = ns.af32[0:8, 6400:6408]
    gl = ck.sem("p34_gl")
    stx1 = [ck.sem(f"p34_stx{j}") for j in range(2)]

    @block.gpsimd
    def _(gpsimd):
        gl.dinc(gpsimd.dma_start(
            out=wgT[:, :].rearrange("p (kt e) -> p kt e", kt=8),
            in_=dram["WgT"].ap().rearrange("(kt kp) e -> kp kt e", kp=128)))
        gl.dinc(gpsimd.dma_start(out=m8[:, :], in_=dram["m8"][:, :]))
        gl.dinc(gpsimd.dma_start(out=trius[:, :], in_=dram["trius"][:, :]))
    gl_r = gl.n

    pe_u2, ac_u2, dv_h = {}, {}, {}
    dv_nmu, dv_var, ac_sq, ac_rs, dv_hn = {}, {}, {}, {}, {}
    pe_tr3, ev_tr3, pe_w3 = {}, {}, {}

    for i in range(NTl):
        sl = i % 2

        if i >= 1:
            @block.sync
            def _(sync, i=i, sl=sl):
                sync.wait_ge(dv.h, dv_h[(i - 1, 3)])
                if i >= 2:
                    sync.wait_ge(dv.h, dv_p3[i - 2])
                    uld[sl].dinc(sync.dma_start(out=ubuf[:, sl * D:(sl + 1) * D],
                                 in_=dram["u_loc"][3 + i * 128:3 + (i + 1) * 128, :]))
                yld[0].dinc(sync.dma_start(out=ybuf[:, 0:DIN],
                             in_=dram["Y"][i * 128:(i + 1) * 128, :]))

        for nchunk in range(4):
            @block.tensor
            def _(tensor, i=i, nchunk=nchunk):
                if i == 0 and nchunk == 0:
                    tensor.wait_ge(ld3.h, ld3_a)
                    tensor.wait_ge(wld3.h, wld3_a)
                pp = nchunk % 2
                key = (i, nchunk - 2) if nchunk >= 2 else (i - 1, nchunk + 2)
                if key in ac_u2:
                    tensor.wait_ge(ac.h, ac_u2[key])
                for kb in range(7):
                    tensor.matmul(PS[pp][:, 0:512],
                                  xn0T[:, kb * T + i * 128:kb * T + (i + 1) * 128],
                                  w2T[:, kb * DIN + nchunk * 512:kb * DIN + (nchunk + 1) * 512],
                                  start=(kb == 0), stop=False)
                pe.inc(tensor.matmul(PS[pp][:, 0:512],
                                     xn0T[:, 7 * T + i * 128:7 * T + (i + 1) * 128],
                                     w2T[:, 7 * DIN + nchunk * 512:7 * DIN + (nchunk + 1) * 512],
                                     start=False, stop=True))
                pe_u2[(i, nchunk)] = pe.n

            @block.scalar
            def _(scalar, i=i, nchunk=nchunk):
                scalar.wait_ge(pe.h, pe_u2[(i, nchunk)])
                for key in [(i, nchunk - 2), (i - 1, nchunk + 2)]:
                    if key in dv_h:
                        scalar.wait_ge(dv.h, dv_h[key])
                        break
                ac.inc(scalar.activation(
                    out=u2b2[:, (nchunk % 2) * 512:(nchunk % 2 + 1) * 512],
                    in_=PS[nchunk % 2][:, 0:512], func=AF.Gelu_apprx_tanh))
                ac_u2[(i, nchunk)] = ac.n

            @block.vector
            def _(vector, i=i, sl=sl, nchunk=nchunk):
                vector.wait_ge(ac.h, ac_u2[(i, nchunk)])
                if nchunk == 0:
                    vector.wait_ge(yld[0].h, 16 * (i + 1))
                dv.inc(vector.tensor_mul(
                    hbuf[:, nchunk * 512:(nchunk + 1) * 512],
                    u2b2[:, (nchunk % 2) * 512:(nchunk % 2 + 1) * 512],
                    ybuf[:, nchunk * 512:(nchunk + 1) * 512]))
                dv_h[(i, nchunk)] = dv.n

        @block.vector
        def _(vector, i=i):
            if i >= 1:
                vector.wait_ge(ac.h, ac_rs[i - 1])
            dv.inc(vector.reduce_sum(stat[:, 0:1], hbuf[:, :], axis=AX.X))
            dv.flush(vector)
            dv.inc(vector.tensor_scalar_mul(stat[:, 1:2], stat[:, 0:1], -1.0 / DIN))
            dv_nmu[i] = dv.n

        dv_sqr = {}
        for nchunk in range(4):
            @block.scalar
            def _(scalar, i=i, nchunk=nchunk):
                if nchunk == 0:
                    scalar.wait_ge(dv.h, dv_nmu[i])
                if nchunk >= 2:
                    scalar.wait_ge(dv.h, dv_sqr[nchunk - 2])
                ac.inc(scalar.activation(
                    out=u2b2[:, (nchunk % 2) * 512:(nchunk % 2 + 1) * 512],
                    in_=hbuf[:, nchunk * 512:(nchunk + 1) * 512],
                    func=AF.Square, bias=stat[:, 1:2]))
                ac_sq[(i, nchunk)] = ac.n

            @block.vector
            def _(vector, i=i, nchunk=nchunk):
                vector.wait_ge(ac.h, ac_sq[(i, nchunk)])
                dv.inc(vector.reduce_sum(stat[:, 4 + nchunk:5 + nchunk],
                                         u2b2[:, (nchunk % 2) * 512:(nchunk % 2 + 1) * 512],
                                         axis=AX.X))
                dv_sqr[nchunk] = dv.n

        @block.vector
        def _(vector, i=i):
            dv.flush(vector)
            vector.tensor_add(stat[:, 4:5], stat[:, 4:5], stat[:, 5:6])
            dv.inc(vector.tensor_add(stat[:, 6:7], stat[:, 6:7], stat[:, 7:8]))
            dv.flush(vector)
            dv.inc(vector.tensor_add(stat[:, 4:5], stat[:, 4:5], stat[:, 6:7]))
            dv.flush(vector)
            dv.inc(vector.tensor_scalar(stat[:, 4:5], stat[:, 4:5], 1.0 / DIN, 1e-5,
                                        ALU.mult, ALU.add))
            dv.flush(vector)
            dv.inc(vector.reciprocal(stat[:, 2:3], stat[:, 4:5]))
            dv_var[i] = dv.n

        @block.scalar
        def _(scalar, i=i):
            scalar.wait_ge(dv.h, dv_var[i])
            ac.inc(scalar.activation(out=stat[:, 2:3], in_=stat[:, 2:3], func=AF.Sqrt))
            ac_rs[i] = ac.n

        @block.vector
        def _(vector, i=i):
            vector.wait_ge(ac.h, ac_rs[i])
            for nchunk in range(4):
                vector.tensor_scalar(u2b2[:, (nchunk % 2) * 512:(nchunk % 2 + 1) * 512],
                                     hbuf[:, nchunk * 512:(nchunk + 1) * 512],
                                     stat[:, 1:2], stat[:, 2:3], ALU.add, ALU.mult)
                vector.tensor_mul(u2b2[:, (nchunk % 2) * 512:(nchunk % 2 + 1) * 512],
                                  u2b2[:, (nchunk % 2) * 512:(nchunk % 2 + 1) * 512],
                                  gbc[:, nchunk * 512:(nchunk + 1) * 512])
                dv.inc(vector.tensor_add(hnb[:, nchunk * 512:(nchunk + 1) * 512],
                                         u2b2[:, (nchunk % 2) * 512:(nchunk % 2 + 1) * 512],
                                         bbc[:, nchunk * 512:(nchunk + 1) * 512]))
            dv_hn[i] = dv.n

        for j in range(16):
            @block.tensor
            def _(tensor, i=i, j=j):
                k = i * 16 + j
                if j == 0:
                    tensor.wait_ge(dv.h, dv_hn[i])
                    if i == 0:
                        tensor.wait_ge(wld3.h, wld3_a)
                if k >= 2:
                    tensor.wait_ge(ev.h, ev_tr3[k - 2])
                pe.inc(tensor.transpose(PB[k % 2][:, 0:128],
                                        hnb[:, j * 128:(j + 1) * 128], idn[:, :]))
                pe_tr3[k] = pe.n

            @block.vector
            def _(vector, i=i, j=j):
                k = i * 16 + j
                vector.wait_ge(pe.h, pe_tr3[k])
                ev.inc(vector.tensor_copy(hnT[:, j * 128:(j + 1) * 128],
                                          PB[k % 2][:, 0:128]))
                ev_tr3[k] = ev.n

        @block.tensor
        def _(tensor, i=i):
            tensor.wait_ge(ev.h, ev_tr3[i * 16 + 15])
            for nchunk in range(2):
                if i >= 1:
                    tensor.wait_ge(dv.h, dv_p3.get(i - 1, 0))
                for kb in range(15):
                    tensor.matmul(PS[2 + nchunk][:, 0:512],
                                  hnT[:, kb * 128:(kb + 1) * 128],
                                  w3T[:, kb * D + nchunk * 512:kb * D + (nchunk + 1) * 512],
                                  start=(kb == 0), stop=False)
                pe.inc(tensor.matmul(PS[2 + nchunk][:, 0:512],
                                     hnT[:, 15 * 128:16 * 128],
                                     w3T[:, 15 * D + nchunk * 512:15 * D + (nchunk + 1) * 512],
                                     start=False, stop=True))
                pe_w3[(i, nchunk)] = pe.n

        @block.vector
        def _(vector, i=i, sl=sl):
            vector.wait_ge(pe.h, pe_w3[(i, 1)])
            vector.wait_ge(uld[sl].h, 16 * (i // 2 + 1))
            if i >= 1:
                vector.wait_ge(stx1[0].h, 16 * i)
            xs = x1st[:, 0:D]
            vector.tensor_add(xs[:, 0:512], PS[2][:, 0:512],
                              ubuf[:, sl * D:sl * D + 512])
            dv.inc(vector.tensor_add(xs[:, 512:1024], PS[3][:, 0:512],
                                     ubuf[:, sl * D + 512:(sl + 1) * D]))
            dv_p3[i] = dv.n

        @block.sync
        def _(sync, i=i):
            sync.wait_ge(dv.h, dv_p3[i])
            stx1[0].dinc(sync.dma_start(
                out=dram["x1"][i * 128:(i + 1) * 128, :],
                in_=x1st[:, 0:D]))

    pe_lg = {}
    dv_lgT = {}

    @block.vector
    def _(vector):
        vector.memset(ones1[:, :], 1.0)
        vector.memset(cntg[:, :], 0.0)
        vector.memset(onesr[:, :], 1.0)
        vector.memset(crows[:, :], 0.0)
        dv.inc(vector.tensor_copy(idnf[:, :], idn[:, :]))
    dv_msets = dv.n

    for tc in range(4):
        @block.tensor
        def _(tensor, tc=tc):
            if tc == 0:
                tensor.wait_ge(gl.h, gl_r)
                tensor.wait_ge(dv.h, dv_p3[NTl - 1])
            if tc >= 2:
                tensor.wait_ge(dv.h, dv_lgT[tc - 2])
            for kb in range(7):
                tensor.matmul(PS[4 + tc % 2][0:8, 0:512], wgT[:, kb * 8:(kb + 1) * 8],
                              xn0T[:, kb * T + tc * 512:kb * T + (tc + 1) * 512],
                              start=(kb == 0), stop=False)
            pe.inc(tensor.matmul(PS[4 + tc % 2][0:8, 0:512], wgT[:, 7 * 8:8 * 8],
                                 xn0T[:, 7 * T + tc * 512:7 * T + (tc + 1) * 512],
                                 start=False, stop=True))
            pe_lg[tc] = pe.n

        @block.vector
        def _(vector, tc=tc):
            vector.wait_ge(pe.h, pe_lg[tc])
            dv.inc(vector.tensor_copy(logitsT[:, tc * 512:(tc + 1) * 512],
                                      PS[4 + tc % 2][0:8, 0:512]))
            dv_lgT[tc] = dv.n
    dv_lgT_all = dv.n

    pe_lgt, ev_lgt = {}, {}
    for i in range(16):
        @block.tensor
        def _(tensor, i=i):
            if i == 0:
                tensor.wait_ge(dv.h, dv_lgT_all)
                tensor.wait_ge(dv.h, dv_msets)
            if i >= 2:
                tensor.wait_ge(ev.h, ev_lgt[i - 2])
            pe.inc(tensor.transpose(PS[2 + (i % 2)][0:128, 0:8],
                                    logitsT[:, i * 128:(i + 1) * 128], idnf[0:8, 0:8]))
            pe_lgt[i] = pe.n

        @block.vector
        def _(vector, i=i):
            vector.wait_ge(pe.h, pe_lgt[i])
            ev.inc(vector.tensor_copy(logits_t[:, i * 8:(i + 1) * 8],
                                      PS[2 + (i % 2)][0:128, 0:8]))
            ev_lgt[i] = ev.n

    dv_rt, ac_sg = {}, {}
    for i in range(16):
        @block.vector
        def _(vector, i=i):
            vector.wait_ge(ev.h, ev_lgt[i])
            lg = logits_t[:, i * 8:(i + 1) * 8]
            dv.inc(vector.reduce_max(stat[:, 0:1], lg, axis=AX.X))
            dv.flush(vector)
            dv.inc(vector.tensor_scalar(eq1[:, :], lg, stat[:, 0:1], None, ALU.is_equal))
            dv.flush(vector)
            dv.inc(vector.scalar_tensor_tensor(u2b2[:, 0:8], eq1[:, :], -1e30, lg,
                                               ALU.mult, ALU.add))
            dv.flush(vector)
            dv.inc(vector.reduce_max(stat[:, 1:2], u2b2[:, 0:8], axis=AX.X))
            dv.flush(vector)
            dv.inc(vector.tensor_sub(stat[:, 2:3], stat[:, 0:1], stat[:, 1:2]))
            dv_rt[(i, 'd')] = dv.n

        @block.scalar
        def _(scalar, i=i):
            scalar.wait_ge(dv.h, dv_rt[(i, 'd')])
            ac.inc(scalar.activation(out=stat[:, 3:4], in_=stat[:, 2:3], func=AF.Sigmoid))
            ac_sg[i] = ac.n

        @block.vector
        def _(vector, i=i):
            vector.wait_ge(ac.h, ac_sg[i])
            lg = logits_t[:, i * 8:(i + 1) * 8]
            vector.tensor_scalar(stat[:, 4:5], stat[:, 3:4], -1.0, 1.0,
                                 ALU.mult, ALU.add)
            dv.inc(vector.tensor_scalar(H2[:, i * 8:(i + 1) * 8], lg, stat[:, 1:2],
                                        None, ALU.is_ge))
            dv.flush(vector)
            dv.inc(vector.tensor_sub(u2b2[:, 0:8], H2[:, i * 8:(i + 1) * 8], eq1[:, :]))
            dv.flush(vector)
            vector.tensor_scalar_mul(u2b2[:, 0:8], u2b2[:, 0:8], stat[:, 4:5])
            dv.inc(vector.tensor_scalar_mul(u2b2[:, 8:16], eq1[:, :], stat[:, 3:4]))
            dv.flush(vector)
            dv.inc(vector.tensor_add(w_all[:, i * 8:(i + 1) * 8], u2b2[:, 0:8],
                                     u2b2[:, 8:16]))
            dv_rt[(i, 'w')] = dv.n

    pe_cnt, ev_cr = {}, {}
    csrow = ns.af32[0:1, 6528:6656]

    @block.tensor
    def _(tensor):
        for i in range(16):
            tensor.wait_ge(dv.h, dv_rt[(i, 'w')])
            pe.inc(tensor.matmul(PS[0][0:1, i * 8:(i + 1) * 8], ones1[:, 0:1],
                                 H2[:, i * 8:(i + 1) * 8], start=True, stop=True))
            pe_cnt[i] = pe.n

    @block.vector
    def _(vector):
        vector.wait_ge(pe.h, pe_cnt[15])
        dv.inc(vector.tensor_copy(csrow[:, :], PS[0][0:1, 0:128]))
        dv.flush(vector)
        for i in range(1, 16):
            dv.inc(vector.tensor_add(crows[:, i * 8:(i + 1) * 8],
                                     crows[:, (i - 1) * 8:i * 8],
                                     csrow[:, (i - 1) * 8:i * 8]))
            dv.flush(vector)
        dv.inc(vector.tensor_add(brow[:, :], crows[:, 15 * 8:16 * 8],
                                 csrow[:, 15 * 8:16 * 8]))
        dv.flush(vector)
    ev_cnt_all = ev.n
    dv_cnt_all = dv.n

    @block.gpsimd
    def _(gpsimd):
        gpsimd.wait_ge(dv.h, dv_cnt_all)
        ccd4.dinc(gpsimd.dma_start(out=dram["cc4_in"][:, :], in_=brow[:, :]))
        gpsimd.wait_ge(ccd4.h, ccd4.n)
        gpsimd.collective_compute(
            "AllGather", ALU.bypass,
            replica_groups=groups or [list(range(8))],
            ins=[dram["cc4_in"].ap().opt()],
            outs=[dram["cc4_out"].ap().opt()],
        ).then_inc(cc4.h, 1)
        cc4.n += 1
        gpsimd.wait_ge(cc4.h, cc4.n)
        ccd4.dinc(gpsimd.dma_start(out=cntg[0:dram["cc4_out"].shape[0], :],
                                   in_=dram["cc4_out"][:, :]))
    ccd4_all = ccd4.n

    @block.tensor
    def _(tensor):
        tensor.wait_ge(ccd4.h, ccd4_all)
        tensor.wait_ge(gl.h, gl_r)
        pe.inc(tensor.matmul(PS[1][0:1, 0:8], m8[:, :], cntg[:, :],
                             start=True, stop=True))
    pe_base = pe.n

    @block.vector
    def _(vector):
        vector.wait_ge(pe.h, pe_base)
        dv.inc(vector.tensor_copy(brow[:, :], PS[1][0:1, 0:8]))
    dv_base = dv.n

    pe_pref, dv_w = {}, {}
    for i in range(16):
        @block.tensor
        def _(tensor, i=i):
            tensor.wait_ge(dv.h, dv_base)
            if i >= 2:
                tensor.wait_ge(dv.h, dv_w.get(i - 2, 0))
            tensor.matmul(PS[2 + (i % 2)][:, 0:8], trius[:, :],
                          H2[:, i * 8:(i + 1) * 8], start=True, stop=False)
            tensor.matmul(PS[2 + (i % 2)][:, 0:8], onesr[:, :],
                          crows[:, i * 8:(i + 1) * 8], start=False, stop=False)
            pe.inc(tensor.matmul(PS[2 + (i % 2)][:, 0:8], onesr[:, :],
                                 brow[:, :], start=False, stop=True))
            pe_pref[i] = pe.n

        @block.vector
        def _(vector, i=i):
            vector.wait_ge(pe.h, pe_pref[i])
            dv.inc(vector.tensor_scalar(u2b2[:, 0:8], PS[2 + (i % 2)][:, 0:8], cap,
                                        None, ALU.is_lt))
            dv.flush(vector)
            dv.inc(vector.tensor_mul(w_all[:, i * 8:(i + 1) * 8],
                                     w_all[:, i * 8:(i + 1) * 8], u2b2[:, 0:8]))
            dv_w[i] = dv.n
    dv_w_all = dv.n

    accd = ck.sem("p34_accd")
    sg3 = [ck.sem(f"p34_sg3{j}") for j in range(2)]
    pe_pair, ac_hg, dv_hg = {}, {}, {}
    pe_y, pe_yd, ac_y = {}, {}, {}
    pe_tr4, pe_tr4d, pe_trk, dv_c4 = {}, {}, {}, {}
    acc_n = {}
    pe_done = {}
    for e in range(nexperts):
        @block.gpsimd
        def _(gpsimd, e=e):
            if e >= 1:
                gpsimd.wait_ge(pe.h, pe_done[e - 1])
            src1 = dram["w1il"][e].rearrange("(kt kp) n -> kp kt n", kp=128)
            w1ld[0].dinc(gpsimd.dma_start(
                out=w1b[:, :].rearrange("p (kt n) -> p kt n", kt=8), in_=src1))
            src2 = dram["w2e"][e].rearrange("(kt kp) n -> kp kt n", kp=128)
            w1ld[0].dinc(gpsimd.dma_start(
                out=w2b[:, :].rearrange("p (kt n) -> p kt n", kt=16), in_=src2))

        for tc in range(4):
            for k in range(16):
                @block.tensor
                def _(tensor, e=e, tc=tc, k=k):
                    if tc == 0 and k == 0:
                        tensor.wait_ge(w1ld[0].h, 32 * (e + 1))
                        if e == 0:
                            tensor.wait_ge(dv.h, dv_w_all)
                    p = k % 2
                    ji = (k // 4) * 8 + (k % 4)
                    jg = ji + 4
                    prevs = [(e, tc, k - 2), (e, tc - 1, 14 + k), (e - 1, 3, 14 + k)]
                    for pk in prevs:
                        if pk in ac_hg:
                            tensor.wait_ge(ac.h, ac_hg[pk])
                            tensor.wait_ge(dv.h, dv_hg[pk])
                            break
                    mm = None
                    for jj, bank in ((ji, 0 + 2 * p), (jg, 1 + 2 * p)):
                        for kb in range(7):
                            tensor.matmul(PS[bank][:, 0:512],
                                          w1b[:, kb * 4096 + jj * 128:kb * 4096 + (jj + 1) * 128],
                                          xn0T[:, kb * T + tc * 512:kb * T + (tc + 1) * 512],
                                          start=(kb == 0), stop=False)
                        mm = tensor.matmul(PS[bank][:, 0:512],
                                           w1b[:, 7 * 4096 + jj * 128:7 * 4096 + (jj + 1) * 128],
                                           xn0T[:, 7 * T + tc * 512:7 * T + (tc + 1) * 512],
                                           start=False, stop=True)
                    pe.inc(mm)
                    pe_pair[(e, tc, k)] = pe.n

                @block.scalar
                def _(scalar, e=e, tc=tc, k=k):
                    p = k % 2
                    scalar.wait_ge(pe.h, pe_pair[(e, tc, k)])
                    if (e, tc, k - 2) in dv_hg:
                        scalar.wait_ge(dv.h, dv_hg[(e, tc, k - 2)])
                    ac.inc(scalar.activation(out=u2b2[:, p * 512:(p + 1) * 512],
                                             in_=PS[0 + 2 * p][:, 0:512], func=AF.Gelu_apprx_tanh))
                    ac_hg[(e, tc, k)] = ac.n

                @block.vector
                def _(vector, e=e, tc=tc, k=k):
                    p = k % 2
                    vector.wait_ge(ac.h, ac_hg[(e, tc, k)])
                    if (e, tc) != (0, 0) and k == 0:
                        prev = (e, tc - 1) if tc >= 1 else (e - 1, 3)
                        vector.wait_ge(pe.h, pe_y[prev])
                    dv.inc(vector.tensor_mul(hgT[:, k * 512:(k + 1) * 512],
                                             u2b2[:, p * 512:(p + 1) * 512],
                                             PS[1 + 2 * p][:, 0:512]))
                    dv_hg[(e, tc, k)] = dv.n

            for dd in range(8):
                @block.tensor
                def _(tensor, e=e, tc=tc, dd=dd):
                    if dd == 0:
                        tensor.wait_ge(dv.h, dv_hg[(e, tc, 15)])
                    prevs = [(e, tc, dd - 2), (e, tc - 1, dd + 6), (e - 1, 3, dd + 6)]
                    for pk in prevs:
                        if pk in ac_y:
                            tensor.wait_ge(ac.h, ac_y[pk])
                            break
                    for kb in range(15):
                        tensor.matmul(PS[4 + dd % 2][:, 0:512],
                                      w2b[:, kb * D + dd * 128:kb * D + (dd + 1) * 128],
                                      hgT[:, kb * 512:(kb + 1) * 512],
                                      start=(kb == 0), stop=False)
                    pe.inc(tensor.matmul(PS[4 + dd % 2][:, 0:512],
                                         w2b[:, 15 * D + dd * 128:15 * D + (dd + 1) * 128],
                                         hgT[:, 15 * 512:16 * 512],
                                         start=False, stop=True))
                    pe_yd[(e, tc, dd)] = pe.n
                    if dd == 7:
                        pe_y[(e, tc)] = pe.n

                @block.scalar
                def _(scalar, e=e, tc=tc, dd=dd):
                    scalar.wait_ge(pe.h, pe_yd[(e, tc, dd)])
                    prevs = [(e, tc, dd - 2), (e, tc - 1, dd + 6), (e - 1, 3, dd + 6)]
                    for pk in prevs:
                        if pk in pe_tr4d:
                            scalar.wait_ge(pe.h, pe_tr4d[pk])
                            break
                    ac.inc(scalar.copy(yTb[:, dd * 512:(dd + 1) * 512],
                                       PS[4 + dd % 2][:, 0:512]))
                    ac_y[(e, tc, dd)] = ac.n

            for tb in range(4):
                for dd in range(8):
                    @block.tensor
                    def _(tensor, e=e, tc=tc, tb=tb, dd=dd):
                        kk = tb * 8 + dd
                        if tb == 0 and dd == 0:
                            tensor.wait_ge(ac.h, ac_y[(e, tc, 7)])
                        prevs = [(e, tc, kk - 2), (e, tc - 1, kk + 30), (e - 1, 3, kk + 30)]
                        for pk in prevs:
                            if pk in dv_c4:
                                tensor.wait_ge(dv.h, dv_c4[pk])
                                break
                        pe.inc(tensor.transpose(
                            PB[kk % 2][:, 0:128],
                            yTb[:, dd * 512 + tb * 128:dd * 512 + (tb + 1) * 128],
                            idn[:, :]))
                        pe_trk[(e, tc, kk)] = pe.n
                        if tb == 3 and dd == 7:
                            for d2 in range(8):
                                pe_tr4d[(e, tc, d2)] = pe.n

                    @block.vector
                    def _(vector, e=e, tc=tc, tb=tb, dd=dd):
                        kk = tb * 8 + dd
                        it = tc * 4 + tb
                        vector.wait_ge(pe.h, pe_trk[(e, tc, kk)])
                        if dd == 0:
                            prevs = [(e, tc, tb - 2), (e, tc - 1, tb + 2), (e - 1, 3, tb + 2)]
                            for pk in prevs:
                                if pk in acc_n:
                                    vector.wait_ge(accd.h, acc_n[pk])
                                    break
                        dv.inc(vector.tensor_scalar_mul(
                            stage[:, (tb % 2) * D + dd * 128:(tb % 2) * D + (dd + 1) * 128],
                            PB[kk % 2][:, 0:128],
                            w_all[:, it * 8 + e:it * 8 + e + 1]))
                        dv_c4[(e, tc, kk)] = dv.n

                @block.gpsimd
                def _(gpsimd, e=e, tc=tc, tb=tb):
                    it = tc * 4 + tb
                    gpsimd.wait_ge(dv.h, dv_c4[(e, tc, tb * 8 + 7)])
                    if e == 0 and tc == 0 and tb == 0:
                        gpsimd.wait_ge(stx1[0].h, stx1[0].n)
                    if tc == 0 and tb == 0 and e >= 1:
                        gpsimd.wait_ge(accd.h, acc_n[(e - 1, 3, 3)])
                    accd.dinc(gpsimd.dma_start(
                        out=dram["x1"][it * 128:(it + 1) * 128, :],
                        in_=stage[:, (tb % 2) * D:(tb % 2 + 1) * D],
                        accum_op=ALU.add))
                    acc_n[(e, tc, tb)] = accd.n

            if tc == 3:
                pe_done[e] = None

        pe_done[e] = pe_y[(e, 3)]

    if dbg:
        dbgo = ck.dout("dbg_w", [128, 16 * 8], F32)
        dbgo2 = ck.dout("dbg_lg", [128, 16 * 8], F32)
        dbgo3 = ck.dout("dbg_cr", [1, 16 * 8], F32)
        dbgo4 = ck.dout("dbg_br", [1, 8], F32)
        dbgo5 = ck.dout("dbg_h", [128, DIN], F32)
        dbgo6 = ck.dout("dbg_hn", [128, DIN], F32)
        dbgo7 = ck.dout("dbg_hnT", [128, DIN], BF16)
        dbgo8 = ck.dout("dbg_w3t", [128, 2 * D], BF16)
        sdb = ck.sem("p34_sdb")

        @block.gpsimd
        def _(gpsimd):
            gpsimd.wait_ge(dv.h, dv_w_all)
            sdb.dinc(gpsimd.dma_start(out=dbgo5[:, :], in_=hbuf[:, :]))
            sdb.dinc(gpsimd.dma_start(out=dbgo6[:, :], in_=hnb[:, :]))
            sdb.dinc(gpsimd.dma_start(out=dbgo7[:, :], in_=hnT[:, :]))
            sdb.dinc(gpsimd.dma_start(out=dbgo8[:, :], in_=w3T[:, 0:2 * D]))
            sdb.dinc(gpsimd.dma_start(out=dbgo[:, :], in_=w_all[:, :]))
            sdb.dinc(gpsimd.dma_start(out=dbgo2[:, :], in_=logits_t[:, :]))
            sdb.dinc(gpsimd.dma_start(out=dbgo3[:, :], in_=crows[:, :]))
            sdb.dinc(gpsimd.dma_start(out=dbgo4[:, :], in_=brow[:, :]))
            gpsimd.wait_ge(sdb.h, sdb.n)

    @block.gpsimd
    def _(gpsimd):
        gpsimd.wait_ge(accd.h, accd.n)
        for sm in stx1:
            gpsimd.wait_ge(sm.h, sm.n)
    return {}


"""Phase 6: mLSTM attention. One (batch, head) per core: b = core//4, h = core%4."""

from concourse.aot_env import allow_ici_all_to_all

DH = 256
NLS = 16
NST = 64
VW = 260


def host_inputs_p6(I, core):
    b, h = divmod(core, 4)
    wq = np.asarray(I["L1_attn_wq"]); wk = np.asarray(I["L1_attn_wk"]); wv = np.asarray(I["L1_attn_wv"])
    wqkv = np.zeros((1024, 768), np.float32)
    for j in range(64):
        blk = 64 * h + j
        rr = 256 * h + 4 * j
        wqkv[rr:rr + 4, 4 * j:4 * j + 4] = wq[blk].T
        wqkv[rr:rr + 4, 256 + 4 * j:256 + 4 * j + 4] = wk[blk].T
        wqkv[rr:rr + 4, 512 + 4 * j:512 + 4 * j + 4] = wv[blk].T
    wi3 = np.ascontiguousarray(np.asarray(I["L1_attn_wi"])[h].reshape(3, 1024))
    wf3 = np.ascontiguousarray(np.asarray(I["L1_attn_wf"])[h].reshape(3, 1024))
    bi_s = np.full((1, 1), np.asarray(I["L1_attn_bi"])[h], np.float32)
    bf_s = np.full((1, 1), np.asarray(I["L1_attn_bf"])[h], np.float32)
    gam = np.asarray(I["L1_attn_ln_w"])[h * 256:(h + 1) * 256].astype(np.float32)
    gamma_bc = np.ascontiguousarray(np.broadcast_to(gam[None, :], (128, 256)))
    um = np.zeros((4, 128, 512), np.float32)
    for v in range(4):
        for p in range(128):
            um[v, p, :v * 128 + p] = -3e38
    idn = np.eye(128, dtype=np.float32)
    import ml_dtypes
    return {"wqkv": wqkv, "wi3": wi3, "wf3": wf3, "bi_s": bi_s, "bf_s": bf_s,
            "gamma_bc": gamma_bc, "umaskd": um,
            "idn128": idn.astype(ml_dtypes.bfloat16)}


def p6_dram_decls(ck, kind="ExternalOutput", xgin="ExternalInput", ngroup=4):
    d = {}
    if xgin == "ExternalInput":
        d["xg"] = ck.din("xg", [ngroup * D, T], BF16)
        d["idn128"] = ck.din("idn128", [128, 128], BF16)
    d["wqkv"] = ck.din("wqkv", [D, 768], F32)
    d["wi3"] = ck.din("wi3", [3, 1024], F32)
    d["wf3"] = ck.din("wf3", [3, 1024], F32)
    d["bi_s"] = ck.din("bi_s", [1, 1], F32)
    d["bf_s"] = ck.din("bf_s", [1, 1], F32)
    d["gamma_bc"] = ck.din("gamma_bc", [128, 256], F32)
    d["umaskd"] = ck.din("umaskd", [4, 128, 512], F32)
    mk = ck.dout if kind == "ExternalOutput" else ck.dint
    d["a1s"] = mk("a1s", [ngroup * T, DH], F32)
    d["a1l"] = mk("a1l", [T, D], F32)
    d["cc6_in"] = ck.dint("cc6_in", [ngroup * T, DH], F32)
    d["cc6_out"] = ck.dint("cc6_out", [ngroup * ngroup * T, DH], F32)
    return d


def emit_phase6(ck, block, dram, groups=None, ngroup=4, myq_masks=None):
    nc = ck.nc
    pe = ck.sem("g_pe")
    dv = ck.sem("g_dv")
    ac = ck.sem("g_ac")
    ev = ck.sem("g_ev")
    xl6 = [ck.sem(f"p6_xl{j}") for j in range(2)]
    wl6 = ck.sem("p6_wl")
    st6 = [ck.sem(f"p6_st{j}") for j in range(2)]
    cc6 = ck.sem("p6_cc")
    ccd6 = ck.sem("p6_ccd")
    SEQ = ngroup * T

    qT = ck.sb("p6_qT", [128, 2 * SEQ], BF16)
    kT = ck.sb("p6_kT", [128, 2 * SEQ], BF16)
    vs = ck.sb("p6_vs", [128, (SEQ // 128) * VW], BF16)
    R0 = ck.sb("p6_R0", [1, SEQ], F32)
    ar6 = ck.sb("p6_ar6", [128, SEQ], F32)
    R1 = ar6[0:1, 0:SEQ]
    R2 = ar6[32:33, 0:SEQ]
    wi3 = ar6[0:3, 0:2048]
    wgate = ck.sb("p6_wg", [128, 2], BF16)
    bib = ck.sb("p6_bib", [1, 2], F32)
    gamb = ck.sb("p6_gam", [128, 256], F32)
    umaskd = ck.sb("p6_um", [128, 4 * 512], F32)
    idn = ck.sb("p6_idn", [128, 128], BF16)
    idnf = ck.sb("p6_idnf", [128, 128], F32)
    onesr = ck.sb("p6_onesr", [1, 128], F32)
    rm16 = ck.sb("p6_rm16", [1, 16], F32)
    r_col = ck.sb("p6_rcol", [128, NST], F32)
    m_col = ck.sb("p6_mcol", [128, NST], F32)
    em_col = ck.sb("p6_emcol", [128, NST], F32)
    er_all = ck.sb("p6_erall", [128, NLS * NST], F32)
    growb = ck.sb("p6_growb", [128, 512], F32)
    diagb = ck.sb("p6_diagb", [128, 512], F32)
    q2 = ck.sb("p6_q2", [128, 2 * 512], BF16)
    SD = ck.sb("p6_sd", [128, 2 * 512], BF16)
    argd = ck.sb("p6_argd", [128, 512], F32)
    a1st = ck.sb("p6_a1st", [128, 2 * 256], F32)
    est = ck.sb("p6_est", [128, 16], F32)
    ones3 = ck.sb("p6_ones3", [3, 1], F32)
    wgc = ck.sb("p6_wgc", [128, 16], BF16)
    bcol = ck.sb("p6_bcol", [128, 2], F32)
    g_wqkv = ck.nc.sbuf_tensor("p6_wqkv", [128, 8 * 768], BF16)
    wqkv = g_wqkv.__enter__()
    g_xgc = ck.nc.sbuf_tensor("p6_xgc", [128, 8 * 512], BF16)
    xgc = g_xgc.__enter__()
    PS = ck.PS
    PB = ck.PB

    @block.gpsimd
    def _(gpsimd):
        wl6.dinc(gpsimd.dma_start(out=idn[:, :], in_=dram["idn128"][:, :]))
        for kt in range(8):
            wl6.dinc(gpsimd.dma_start(out=wqkv[:, kt * 768:(kt + 1) * 768],
                     in_=dram["wqkv"][kt * 128:(kt + 1) * 128, :]))
        wl6.dinc(gpsimd.dma_start(out=wi3[0:3, 0:1024], in_=dram["wi3"][:, :]))
        wl6.dinc(gpsimd.dma_start(out=wi3[0:3, 1024:2048], in_=dram["wf3"][:, :]))
        wl6.dinc(gpsimd.dma_start(out=bib[:, 0:1], in_=dram["bi_s"][:, :]))
        wl6.dinc(gpsimd.dma_start(out=bib[:, 1:2], in_=dram["bf_s"][:, :]))
        wl6.dinc(gpsimd.dma_start(out=gamb[:, :], in_=dram["gamma_bc"][:, :]))
        wl6.dinc(gpsimd.dma_start(
            out=umaskd[:, :].rearrange("p (v x) -> p v x", v=4),
            in_=dram["umaskd"].ap().rearrange("v p x -> p v x")))
    wl6_a = wl6.n

    dv0 = dv.n
    @block.vector
    def _(vector):
        vector.wait_ge(wl6.h, wl6_a)
        vector.memset(onesr[:, :], 1.0)
        vector.memset(vs[:, :].rearrange("p (t w) -> p t w", w=VW)[:, :, 256:260], 0.0)
        vector.memset(vs[:, :].rearrange("p (t w) -> p t w", w=VW)[:, :, 256:257], 1.0)
        vector.tensor_copy(idnf[:, :], idn[:, :])
        dv.inc(vector.tensor_scalar(gamb[:, :], gamb[:, :], 1.0, None, ALU.add))
    dv_init = dv.n

    dv_o3 = dv.n
    @block.vector
    def _(vector):
        dv.inc(vector.memset(ones3[:, :], 1.0))
    dv_o3 = dv.n

    pe_gsum_d, dv_gsum_d = {}, {}
    for seg in range(4):
        @block.tensor
        def _(tensor, seg=seg):
            if seg == 0:
                tensor.wait_ge(dv.h, dv_o3)
                tensor.wait_ge(wl6.h, wl6_a)
            if seg >= 2:
                tensor.wait_ge(dv.h, dv_gsum_d[seg - 2])
            pe.inc(tensor.matmul(PS[4 + seg % 2][0:1, 0:512], ones3[:, :],
                                 wi3[0:3, seg * 512:(seg + 1) * 512],
                                 start=True, stop=True))
            pe_gsum_d[seg] = pe.n

        @block.vector
        def _(vector, seg=seg):
            vector.wait_ge(pe.h, pe_gsum_d[seg])
            dv.inc(vector.tensor_copy(wi3[0:1, seg * 512:(seg + 1) * 512],
                                      PS[4 + seg % 2][0:1, 0:512]))
            dv_gsum_d[seg] = dv.n
    dv_gsum = dv.n

    pe_gt, dv_gt = {}, {}
    for k in range(16):
        g, jd = divmod(k, 8)

        @block.tensor
        def _(tensor, g=g, jd=jd, k=k):
            if k == 0:
                tensor.wait_ge(dv.h, dv_gsum)
                tensor.wait_ge(dv.h, dv_init)
            if k >= 2:
                tensor.wait_ge(dv.h, dv_gt[k - 2])
            pe.inc(tensor.transpose(PS[5][0:128, k % 2:k % 2 + 1],
                                    wi3[0:1, g * 1024 + jd * 128:g * 1024 + (jd + 1) * 128],
                                    idnf[0:1, 0:1]))
            pe_gt[k] = pe.n

        @block.vector
        def _(vector, g=g, jd=jd, k=k):
            vector.wait_ge(pe.h, pe_gt[k])
            dv.inc(vector.tensor_copy(wgc[:, g * 8 + jd:g * 8 + jd + 1],
                                      PS[5][0:128, k % 2:k % 2 + 1]))
            dv_gt[k] = dv.n
    dv_wgc = dv.n

    pe_qk, ac_qk, dv_qk = {}, {}, {}
    ig_row = R1
    fg_row = R0[0:1, 0:SEQ]
    for c in range(16):
        r = c // 4

        @block.gpsimd
        def _(gpsimd, c=c, r=r):
            if c >= 1:
                gpsimd.wait_ge(pe.h, pe_qk[(c - 1, 'g')])
            for jd in range(8):
                xl6[c % 2].dinc(gpsimd.dma_start(
                    out=xgc[:, jd * 512:(jd + 1) * 512] if c % 2 == 0 else
                        xgc[:, jd * 512:(jd + 1) * 512],
                    in_=dram["xg"][r * D + jd * 128:r * D + (jd + 1) * 128,
                                   (c % 4) * 512:(c % 4 + 1) * 512]))

        @block.tensor
        def _(tensor, c=c):
            tensor.wait_ge(xl6[c % 2].h, 16 * 8 * (c // 2 + 1))
            if c == 0:
                tensor.wait_ge(dv.h, dv_wgc)
            else:
                tensor.wait_ge(dv.h, dv_qk[(c - 1, 'gev')])
            for qk in range(2):
                for mt in range(2):
                    bank = qk * 2 + mt
                    for kt in range(8):
                        mm = tensor.matmul(
                            PS[bank][:, 0:512],
                            wqkv[:, kt * 768 + qk * 256 + mt * 128:kt * 768 + qk * 256 + (mt + 1) * 128],
                            xgc[:, kt * 512:(kt + 1) * 512],
                            start=(kt == 0), stop=(kt == 7))
            pe.inc(mm)
            pe_qk[(c, 'qk')] = pe.n
            for st in range(4):
                for kt in range(8):
                    mm = tensor.matmul(
                        PS[4][:, st * 320:st * 320 + 256] if False else PS[4][:, st * 128:st * 128 + 256] if False else PS[4 + st % 2][:, (st // 2) * 256:(st // 2) * 256 + 256],
                        xgc[:, kt * 512 + st * 128:kt * 512 + st * 128 + 128],
                        wqkv[:, kt * 768 + 512:kt * 768 + 768],
                        start=(kt == 0), stop=(kt == 7))
            pe.inc(mm)
            pe_qk[(c, 'v')] = pe.n
            pe_qk[(c, 'all')] = pe.n

        @block.vector
        def _(vector, c=c):
            vector.wait_ge(pe.h, pe_qk[(c, 'qk')])
            for qk in range(2):
                for mt in range(2):
                    bank = qk * 2 + mt
                    dst = qT if qk == 0 else kT
                    dv.inc(vector.tensor_copy(
                        dst[:, mt * SEQ + c * 512:mt * SEQ + (c + 1) * 512],
                        PS[bank][:, 0:512]))
            vector.wait_ge(pe.h, pe_qk[(c, 'v')])
            for st in range(4):
                dv.inc(vector.tensor_copy(
                    vs[:, (c * 4 + st) * VW:(c * 4 + st) * VW + 256],
                    PS[4 + st % 2][:, (st // 2) * 256:(st // 2) * 256 + 256]))
            dv_qk[(c, 'ev')] = dv.n

        @block.tensor
        def _(tensor, c=c):
            tensor.wait_ge(dv.h, dv_qk[(c, 'ev')])
            for g in range(2):
                for kt in range(8):
                    mm = tensor.matmul(PS[g][0:1, 0:512],
                                       wgc[:, g * 8 + kt:g * 8 + kt + 1],
                                       xgc[:, kt * 512:(kt + 1) * 512],
                                       start=(kt == 0), stop=(kt == 7))
            pe.inc(mm)
            pe_qk[(c, 'g')] = pe.n

        @block.vector
        def _(vector, c=c):
            vector.wait_ge(pe.h, pe_qk[(c, 'g')])
            vector.tensor_scalar(ig_row[0:1, c * 512:(c + 1) * 512],
                                 PS[0][0:1, 0:512], bib[0:1, 0:1], None, ALU.add)
            dv.inc(vector.tensor_scalar(fg_row[0:1, c * 512:(c + 1) * 512],
                                        PS[1][0:1, 0:512], bib[0:1, 1:2], None,
                                        ALU.add))
            dv_qk[(c, 'gev')] = dv.n
    pe_qkv_all = pe.n
    dv_qkv_all = dv.n

    if getattr(ck, 'p6_stage', 99) <= 1:
        g_xgc.__exit__(None, None, None)
        g_wqkv.__exit__(None, None, None)

        @block.sync
        def _(sync):
            sync.wait_ge(dv.h, dv.n)
            sync.wait_ge(pe.h, pe.n)
        return {}

    R3 = ar6[64:65, 0:SEQ]
    growr = argd[0:1, 0:512]
    LN16 = 2.772588722239781
    g_xgc.__exit__(None, None, None)
    g_wqkv.__exit__(None, None, None)

    NCH = 32
    CW = SEQ // NCH
    rtmp = ck.sb("p6_rtmp", [1, 4 * CW], F32)
    rtmp2 = ck.sb("p6_rtmp2", [1, 2 * CW + 8], F32)
    tF = rtmp2[0:1, 0:CW]
    z5 = rtmp2[0:1, CW:2 * CW]
    cF = rtmp2[0:1, 2 * CW:2 * CW + 1]
    cM = rtmp2[0:1, 2 * CW + 1:2 * CW + 2]
    ninit = rtmp2[0:1, 2 * CW + 2:2 * CW + 3]

    dv_pre = dv.n
    @block.vector
    def _(vector):
        vector.wait_ge(dv.h, dv_qkv_all)
        vector.memset(z5[:, :], 0.0)
        vector.memset(cF[:, :], 0.0)
        dv.inc(vector.memset(cM[:, :], -1e30))
    dv_pre = dv.n

    ac_rc, dv_rc = {}, {}
    for c in range(NCH):
        p = c % 2
        t1 = rtmp[0:1, (2 * p) * CW:(2 * p + 1) * CW]
        t2 = rtmp[0:1, (2 * p + 1) * CW:(2 * p + 2) * CW]
        fgc = fg_row[0:1, c * CW:(c + 1) * CW]
        igc = ig_row[0:1, c * CW:(c + 1) * CW]
        mc = R3[0:1, c * CW:(c + 1) * CW]

        @block.scalar
        def _(scalar, c=c, t1=t1, t2=t2, fgc=fgc):
            if c == 0:
                scalar.wait_ge(dv.h, dv_qkv_all)
            if c >= 2:
                scalar.wait_ge(dv.h, dv_rc[c - 2])
            scalar.activation(out=t1, in_=fgc, func=AF.Abs)
            scalar.activation(out=t2, in_=t1, func=AF.Exp, scale=-1.0)
            scalar.activation(out=t1, in_=t2, func=AF.Ln, bias=1.0)
            ac.inc(scalar.activation(out=t2, in_=fgc, func=AF.Relu, scale=-1.0))
            ac_rc[c] = ac.n

        @block.vector
        def _(vector, c=c, t1=t1, t2=t2, fgc=fgc, igc=igc, mc=mc):
            vector.wait_ge(ac.h, ac_rc[c])
            if c == 0:
                vector.wait_ge(dv.h, dv_pre)
            dv.inc(vector.tensor_add(t1, t1, t2)); dv.flush(vector)
            dv.inc(vector.tensor_scalar_mul(fgc, t1, -1.0)); dv.flush(vector)
            dv.inc(vector.tensor_tensor_scan(tF, fgc, z5, 0.0, ALU.add, ALU.add))
            dv.flush(vector)
            dv.inc(vector.tensor_scalar(tF, tF, cF, None, ALU.add)); dv.flush(vector)
            dv.inc(vector.tensor_copy(cF, tF[0:1, CW - 1:CW])); dv.flush(vector)
            dv.inc(vector.tensor_sub(igc, igc, tF)); dv.flush(vector)
            dv.inc(vector.tensor_tensor_scan(t2, igc, z5, -1e30, ALU.max, ALU.add))
            dv.flush(vector)
            dv.inc(vector.tensor_scalar(t2, t2, cM, None, ALU.max)); dv.flush(vector)
            dv.inc(vector.tensor_copy(cM, t2[0:1, CW - 1:CW])); dv.flush(vector)
            dv.inc(vector.tensor_add(t1, tF, t2)); dv.flush(vector)
            dv.inc(vector.tensor_copy(mc, t1)); dv.flush(vector)
            dv.inc(vector.tensor_copy(fgc, t2))
            dv_rc[c] = dv.n

    @block.vector
    def _(vector):
        dv.flush(vector)
        dv.inc(vector.tensor_copy(rm16[:, :], fg_row[0:1, 0:SEQ:512]))
        dv.flush(vector)
        dv.inc(vector.tensor_scalar_mul(R0[:, :], fg_row, -1.0))
        dv.flush(vector)
    dv_rows = dv.n

    if getattr(ck, 'p6_stage', 99) <= 2:
        @block.sync
        def _(sync):
            sync.wait_ge(dv.h, dv.n)
            sync.wait_ge(pe.h, pe.n)
            sync.wait_ge(ac.h, ac.n)
        return {}

    rscr = ck.dint("p6_rscr", [2, SEQ], F32)
    r2d = ck.sb("p6_r2d", [64, 128], F32)
    trs = ck.sem("p6_trs")
    pe_tc, dv_tc = {}, {}

    @block.sync
    def _(sync):
        sync.wait_ge(dv.h, dv_rows)
        trs.dinc(sync.dma_start(out=rscr[0:1, :], in_=ig_row))
        trs.dinc(sync.dma_start(out=rscr[1:2, :], in_=R3[0:1, :]))
        sync.wait_ge(trs.h, trs.n)

    for w in range(2):
        @block.sync
        def _(sync, w=w):
            if w == 1:
                sync.wait_ge(pe.h, pe_tc[0])
            trs.dinc(sync.dma_start(
                out=r2d[:, :],
                in_=rscr.ap()[w:w + 1, :].rearrange("o (st p) -> (o st) p", p=128)))

        @block.tensor
        def _(tensor, w=w):
            tensor.wait_ge(trs.h, 32 + 16 * (w + 1))
            if w == 1:
                tensor.wait_ge(dv.h, dv_tc[0])
            pe.inc(tensor.transpose(PS[5][0:128, 0:64], r2d[:, :],
                                    idnf[0:64, 0:64]))
            pe_tc[w] = pe.n

        @block.vector
        def _(vector, w=w):
            vector.wait_ge(pe.h, pe_tc[w])
            dst = r_col if w == 0 else m_col
            dv.inc(vector.tensor_copy(dst[:, :], PS[5][0:128, 0:64]))
            dv_tc[w] = dv.n
    dv_tabs = dv.n
    tab_n = trs.n

    if getattr(ck, 'p6_stage', 99) <= 2.5:
        @block.sync
        def _(sync):
            sync.wait_ge(dv.h, dv.n)
            sync.wait_ge(pe.h, pe.n)
            sync.wait_ge(ac.h, ac.n)
            sync.wait_ge(trs.h, tab_n)
        return {}

    ac_em = ac.n
    @block.scalar
    def _(scalar):
        scalar.wait_ge(dv.h, dv_tabs)
        ac.inc(scalar.activation(out=em_col[:, :], in_=m_col[:, :], func=AF.Exp,
                                 scale=-1.0))
    ac_em = ac.n

    pe_er, dv_er, ac_er = {}, {}, {}
    for L in range(NLS):
        @block.tensor
        def _(tensor, L=L):
            if L == 0:
                tensor.wait_ge(dv.h, dv_tabs)
            if L >= 2:
                tensor.wait_ge(ac.h, ac_er[L - 2])
            pe.inc(tensor.matmul(PS[4][0:128, L % 2:L % 2 + 1], onesr[:, :],
                                 R0[0:1, L * 512:L * 512 + 1], start=True, stop=True))
            pe_er[L] = pe.n

        @block.vector
        def _(vector, L=L):
            vector.wait_ge(pe.h, pe_er[L])
            dv.inc(vector.tensor_copy(bcol[:, L % 2:L % 2 + 1],
                                      PS[4][0:128, L % 2:L % 2 + 1]))
            dv_er[L] = dv.n

        @block.vector
        def _(vector, L=L):
            vector.wait_ge(dv.h, dv_er[L])
            sl = er_all[:, L * NST:(L + 1) * NST]
            vector.tensor_scalar(sl, r_col[:, :], bcol[:, L % 2:L % 2 + 1], None,
                                 ALU.add)
            vector.tensor_relu(est[:, 0:NST] if False else m_col[:, :] if False else est[:, 0:16], sl[:, 0:16]) if False else None
            vector.tensor_relu(argd[:, 0:NST], sl)
            dv.inc(vector.tensor_sub(sl, sl, argd[:, 0:NST]))
            dv_er[(L, 'c')] = dv.n

        @block.scalar
        def _(scalar, L=L):
            scalar.wait_ge(dv.h, dv_er[(L, 'c')])
            ac.inc(scalar.activation(out=er_all[:, L * NST:(L + 1) * NST],
                                     in_=er_all[:, L * NST:(L + 1) * NST],
                                     func=AF.Exp))
            ac_er[L] = ac.n
    ac_er_all = ac.n

    if getattr(ck, 'p6_stage', 99) <= 3:
        @block.sync
        def _(sync):
            sync.wait_ge(dv.h, dv.n)
            sync.wait_ge(pe.h, pe.n)
            sync.wait_ge(ac.h, ac.n)
        return {}

    if getattr(ck, 'p6_dbg', False):
        ck.p6_dsd = ck.dout("dbg_sd", [128, 512], BF16)
        ck.p6_dar = ck.dout("dbg_ar", [128, 512], F32)
    pe_m1, pe_m2, dv_sd, ac_pre, dv_pre = {}, {}, {}, {}, {}
    dv_epi = {}
    for L in range(NLS):
        @block.tensor
        def _(tensor, L=L):
            if L == 0:
                tensor.wait_ge(ac.h, ac_er_all)
                tensor.wait_ge(dv.h, dv_qkv_all)
            if L >= 1:
                tensor.wait_ge(dv.h, dv_epi[L - 1])
            pe.inc(tensor.matmul(PS[4][:, 0:512], onesr[:, :],
                                 R0[0:1, L * 512:(L + 1) * 512], start=True, stop=True))
            pe_m1[(L, 'pro')] = pe.n

        @block.scalar
        def _(scalar, L=L):
            scalar.wait_ge(dv.h, dv_rows)
            ac.inc(scalar.activation(out=growr[:, :], in_=R0[0:1, L * 512:(L + 1) * 512],
                                     func=AF.Exp, bias=rm16[0:1, L:L + 1]))
            ac_pre[L] = ac.n

        @block.tensor
        def _(tensor, L=L):
            tensor.wait_ge(ac.h, ac_pre[L])
            pe.inc(tensor.matmul(PS[5][:, 0:512], onesr[:, :], growr[:, :],
                                 start=True, stop=True))
            pe_m1[(L, 'pro2')] = pe.n

        @block.vector
        def _(vector, L=L):
            vector.wait_ge(pe.h, pe_m1[(L, 'pro2')])
            vector.tensor_scalar(diagb[:, :], PS[4][:, 0:512], 1.0, -LN16,
                                 ALU.mult, ALU.add)
            dv.inc(vector.tensor_scalar_mul(growb[:, :], PS[5][:, 0:512], 1.0 / 16.0))
            dv.flush(vector)
            for kt in range(2):
                dv.inc(vector.tensor_mul(q2[:, kt * 512:(kt + 1) * 512],
                                         qT[:, kt * SEQ + L * 512:kt * SEQ + (L + 1) * 512],
                                         growb[:, :]))
            dv_pre[L] = dv.n

        njs = 4 * L + 4

        def emit_m2(tensor, L, j, final=False):
            pp = j % 2
            tensor.wait_ge(dv.h, dv_sd[(L, j)])
            for lt in range(4):
                mm = tensor.matmul(
                    PS[lt][:, 0:257],
                    SD[:, pp * 512 + lt * 128:pp * 512 + (lt + 1) * 128],
                    vs[:, j * VW:j * VW + 257],
                    start=(j == 0), stop=final)
            pe.inc(mm)
            pe_m2[(L, j)] = pe.n

        for j in range(njs):
            diag = j >= 4 * L
            pp = j % 2

            @block.tensor
            def _(tensor, L=L, j=j, diag=diag, pp=pp, njs=njs, emit_m2=emit_m2):
                if j == 0:
                    tensor.wait_ge(dv.h, dv_pre[L])
                if (L, j - 2) in dv_sd:
                    tensor.wait_ge(dv.h, dv_sd[(L, j - 2)])
                for kt in range(2):
                    mm = tensor.matmul(
                        PS[4 + pp][:, 0:512],
                        kT[:, kt * SEQ + j * 128:kt * SEQ + (j + 1) * 128],
                        q2[:, kt * 512:(kt + 1) * 512] if not diag else
                        qT[:, kt * SEQ + L * 512:kt * SEQ + (L + 1) * 512],
                        start=(kt == 0), stop=(kt == 1))
                pe.inc(mm)
                pe_m1[(L, j)] = pe.n
                if j >= 1:
                    emit_m2(tensor, L, j - 1)

            if diag:
                @block.vector
                def _(vector, L=L, j=j, pp=pp):
                    v = j - 4 * L
                    dv.inc(vector.tensor_add(argd[:, :], diagb[:, :],
                                             umaskd[:, v * 512:(v + 1) * 512]))
                    dv_sd[(L, j, 'arg')] = dv.n

                if getattr(ck, 'p6_dbg', False) and L == 0 and j == 0:
                    sdb0 = ck.sem("p6_sdb0")

                    @block.sync
                    def _(sync, L=L, j=j):
                        sync.wait_ge(dv.h, dv_sd[(L, j, 'arg')])
                        sdb0.dinc(sync.dma_start(out=ck.p6_dar[:, :], in_=argd[:, :]))

                @block.scalar
                def _(scalar, L=L, j=j):
                    scalar.wait_ge(dv.h, dv_sd[(L, j, 'arg')])
                    ac.inc(scalar.activation(out=argd[:, :], in_=argd[:, :],
                                             func=AF.Exp, bias=r_col[:, j:j + 1]))
                    ac_pre[(L, j)] = ac.n

                @block.vector
                def _(vector, L=L, j=j, pp=pp):
                    vector.wait_ge(ac.h, ac_pre[(L, j)])
                    vector.wait_ge(pe.h, pe_m1[(L, j)])
                    if (L, j - 2) in pe_m2:
                        vector.wait_ge(pe.h, pe_m2[(L, j - 2)])
                    dv.inc(vector.tensor_mul(SD[:, pp * 512:(pp + 1) * 512],
                                             argd[:, :], PS[4 + pp][:, 0:512]))
                    dv_sd[(L, j)] = dv.n

                if getattr(ck, 'p6_dbg', False) and L == 0 and j == 0:
                    sdb1 = ck.sem("p6_sdb1")

                    @block.sync
                    def _(sync, L=L, j=j, pp=pp):
                        sync.wait_ge(dv.h, dv_sd[(L, j)])
                        sdb1.dinc(sync.dma_start(out=ck.p6_dsd[:, :],
                                                 in_=SD[:, pp * 512:(pp + 1) * 512]))
            else:
                @block.vector
                def _(vector, L=L, j=j, pp=pp):
                    vector.wait_ge(pe.h, pe_m1[(L, j)])
                    if (L, j - 2) in pe_m2:
                        vector.wait_ge(pe.h, pe_m2[(L, j - 2)])
                    dv.inc(vector.tensor_scalar_mul(
                        SD[:, pp * 512:(pp + 1) * 512], PS[4 + pp][:, 0:512],
                        er_all[:, L * NST + j:L * NST + j + 1]))
                    dv_sd[(L, j)] = dv.n

            if j == njs - 1:
                @block.tensor
                def _(tensor, L=L, j=j, emit_m2=emit_m2):
                    emit_m2(tensor, L, j, final=True)

        for lt in range(4):
            @block.vector
            def _(vector, L=L, lt=lt):
                vector.wait_ge(pe.h, pe_m2[(L, njs - 1)])
                ct = L * 4 + lt
                dv.inc(vector.tensor_copy(est[:, 0:1], PS[lt][:, 256:257]))
                dv.flush(vector)
                dv_epi[(L, lt, 'rs')] = dv.n

            @block.scalar
            def _(scalar, L=L, lt=lt):
                scalar.wait_ge(dv.h, dv_epi[(L, lt, 'rs')])
                ac.inc(scalar.activation(out=est[:, 1:2], in_=est[:, 0:1], func=AF.Abs))
                ac_pre[(L, lt, 'abs')] = ac.n

            @block.vector
            def _(vector, L=L, lt=lt):
                ct = L * 4 + lt
                vector.wait_ge(ac.h, ac_pre[(L, lt, 'abs')])
                dv.inc(vector.tensor_tensor(est[:, 2:3], est[:, 1:2],
                                            em_col[:, ct:ct + 1], ALU.max))
                dv.flush(vector)
                dv.inc(vector.tensor_scalar(est[:, 2:3], est[:, 2:3], 1e-8, None,
                                            ALU.add))
                dv.flush(vector)
                dv.inc(vector.reciprocal(est[:, 3:4], est[:, 2:3]))
                dv.flush(vector)
                sl = lt % 2
                dv.inc(vector.tensor_scalar_mul(a1st[:, sl * 256:(sl + 1) * 256],
                                                PS[lt][:, 0:256], est[:, 3:4]))
                if getattr(ck, 'p6_dbg', False) and L == 0 and lt == 0:
                    vector.tensor_copy(diagb[:, 0:256], PS[lt][:, 0:256])
                    vector.tensor_copy(diagb[:, 256:258], est[:, 0:2])
                dv.inc(vector.reduce_sum(est[:, 4:5], a1st[:, sl * 256:(sl + 1) * 256],
                                         axis=AX.X))
                dv.flush(vector)
                dv.inc(vector.tensor_scalar_mul(est[:, 5:6], est[:, 4:5], -1.0 / 256))
                dv_epi[(L, lt, 'mu')] = dv.n

            @block.scalar
            def _(scalar, L=L, lt=lt):
                scalar.wait_ge(dv.h, dv_epi[(L, lt, 'mu')])
                sl = lt % 2
                ac.inc(scalar.activation(out=est[:, 8:8 + 128] if False else argd[:, 0:256],
                                         in_=a1st[:, sl * 256:(sl + 1) * 256],
                                         func=AF.Square, bias=est[:, 5:6]))
                ac_pre[(L, lt, 'sq')] = ac.n

            @block.vector
            def _(vector, L=L, lt=lt):
                vector.wait_ge(ac.h, ac_pre[(L, lt, 'sq')])
                dv.inc(vector.reduce_sum(est[:, 6:7], argd[:, 0:256], axis=AX.X))
                dv.flush(vector)
                dv.inc(vector.tensor_scalar(est[:, 6:7], est[:, 6:7], 1.0 / 256, 1e-5,
                                            ALU.mult, ALU.add))
                dv.flush(vector)
                dv.inc(vector.reciprocal(est[:, 7:8], est[:, 6:7]))
                dv_epi[(L, lt, 'v')] = dv.n

            @block.scalar
            def _(scalar, L=L, lt=lt):
                scalar.wait_ge(dv.h, dv_epi[(L, lt, 'v')])
                ac.inc(scalar.activation(out=est[:, 7:8], in_=est[:, 7:8], func=AF.Sqrt))
                ac_pre[(L, lt, 'rs2')] = ac.n

            @block.vector
            def _(vector, L=L, lt=lt):
                vector.wait_ge(ac.h, ac_pre[(L, lt, 'rs2')])
                sl = lt % 2
                ct = L * 4 + lt
                if ct >= 2:
                    vector.wait_ge(st6[sl].h, 16 * 2 * (ct // 2))
                vector.tensor_scalar(a1st[:, sl * 256:(sl + 1) * 256],
                                     a1st[:, sl * 256:(sl + 1) * 256],
                                     est[:, 5:6], est[:, 7:8], ALU.add, ALU.mult)
                dv.inc(vector.tensor_mul(a1st[:, sl * 256:(sl + 1) * 256],
                                         a1st[:, sl * 256:(sl + 1) * 256], gamb[:, :]))
                dv_epi[(L, lt)] = dv.n

            @block.sync
            def _(sync, L=L, lt=lt):
                sync.wait_ge(dv.h, dv_epi[(L, lt)])
                ct = L * 4 + lt
                sl = lt % 2
                st6[sl].dinc(sync.dma_start(
                    out=dram["a1s"][ct * 128:(ct + 1) * 128, :],
                    in_=a1st[:, sl * 256:(sl + 1) * 256]))
                st6[sl].dinc(sync.dma_start(
                    out=dram["cc6_in"][ct * 128:(ct + 1) * 128, :],
                    in_=a1st[:, sl * 256:(sl + 1) * 256]))
        dv_epi[L] = dv.n

    if getattr(ck, 'p6_dbg', False):
        dq = ck.dout("dbg_qT", [128, 2 * SEQ], BF16)
        dk = ck.dout("dbg_kT", [128, 2 * SEQ], BF16)
        dvs = ck.dout("dbg_vs", [128, 8 * VW], BF16)
        dr = ck.dout("dbg_r", [1, SEQ], F32)
        dm = ck.dout("dbg_m", [1, SEQ], F32)
        dnr = ck.dout("dbg_nrm", [1, SEQ], F32)
        der = ck.dout("dbg_er", [128, NLS * NST], F32)
        do2 = ck.dout("dbg_o2", [128, 512], F32)

        sdb6 = ck.sem("p6_sdb")

        @block.gpsimd
        def _(gpsimd):
            for sm in st6:
                gpsimd.wait_ge(sm.h, sm.n)
            sdb6.dinc(gpsimd.dma_start(out=dq[:, :], in_=qT[:, :]))
            sdb6.dinc(gpsimd.dma_start(out=dk[:, :], in_=kT[:, :]))
            sdb6.dinc(gpsimd.dma_start(out=dvs[:, :], in_=vs[:, 0:8 * VW]))
            sdb6.dinc(gpsimd.dma_start(out=dr[:, :], in_=ig_row))
            sdb6.dinc(gpsimd.dma_start(out=dm[:, :], in_=R3[0:1, :]))
            sdb6.dinc(gpsimd.dma_start(out=dnr[:, :], in_=R0[:, :]))
            sdb6.dinc(gpsimd.dma_start(out=der[:, :], in_=er_all[:, :]))
            sdb6.dinc(gpsimd.dma_start(out=do2[:, :], in_=diagb[:, 0:512]))
            gpsimd.wait_ge(sdb6.h, sdb6.n)

    import dataclasses as _dc
    import concourse.bass as bass

    if getattr(ck, 'p6_no_a1l', False):
        @block.gpsimd
        def _(gpsimd):
            for sm in st6:
                gpsimd.wait_ge(sm.h, sm.n)
        return {}

    @block.gpsimd
    def _(gpsimd):
        for sm in st6:
            gpsimd.wait_ge(sm.h, sm.n)
        gpsimd.collective_compute(
            "AllGather", ALU.bypass,
            replica_groups=groups or [[0, 1, 2, 3], [4, 5, 6, 7]],
            ins=[dram["cc6_in"].ap().opt()],
            outs=[dram["cc6_out"].ap().opt()],
        ).then_inc(cc6.h, 1)
        cc6.n += 1
        gpsimd.wait_ge(cc6.h, cc6.n)
        pid = gpsimd.partition_id()
        qv = gpsimd.scalar_reg_alu(ALU.mod, pid, ngroup)
        for r in range(4):
            src = dram["cc6_out"].ap()[bass.ds(qv * T, T), :]
            src = _dc.replace(src, offset=src.offset + r * ngroup * T * DH)
            ccd6.dinc(gpsimd.dma_start(
                out=dram["a1l"][:, r * 256:(r + 1) * 256],
                in_=src))
        gpsimd.wait_ge(ccd6.h, ccd6.n)
    return {}


"""Phase 7: L1 MoE (dense, top-2-of-softmax scaling) + shared expert + final sum."""


NE = 8


def host_inputs_p7(I, core):
    d = {}
    d["gwT"] = np.ascontiguousarray(np.asarray(I["L1_ffn_gate_w"]).T).astype(np.float32)
    d["gb_bc"] = np.ascontiguousarray(
        np.broadcast_to(np.asarray(I["L1_ffn_gate_b"])[None, :], (128, 8))).astype(np.float32)
    w1 = np.asarray(I["L1_ffn_w1"]); w3 = np.asarray(I["L1_ffn_w3"]); w2 = np.asarray(I["L1_ffn_w2"])
    d["w1T"] = np.ascontiguousarray(np.transpose(w1, (0, 2, 1)))
    d["w3T"] = np.ascontiguousarray(np.transpose(w3, (0, 2, 1)))
    d["w2T"] = np.ascontiguousarray(np.transpose(w2, (0, 2, 1)))
    d["b1c"] = np.ascontiguousarray(np.asarray(I["L1_ffn_b1"]).reshape(8, 8, 128).transpose(0, 2, 1)).astype(np.float32)
    d["b3c"] = np.ascontiguousarray(np.asarray(I["L1_ffn_b3"]).reshape(8, 8, 128).transpose(0, 2, 1)).astype(np.float32)
    d["b2bc"] = np.ascontiguousarray(
        np.broadcast_to(np.asarray(I["L1_ffn_b2"])[:, None, :], (8, 128, 1024))).astype(np.float32)
    d["ws1T"] = np.ascontiguousarray(np.asarray(I["L1_ffn_ws1"]).T).astype(np.float32)
    d["ws3T"] = np.ascontiguousarray(np.asarray(I["L1_ffn_ws3"]).T).astype(np.float32)
    d["ws2T"] = np.ascontiguousarray(np.asarray(I["L1_ffn_ws2"]).T).astype(np.float32)
    d["bs1c"] = np.ascontiguousarray(np.asarray(I["L1_ffn_bs1"]).reshape(16, 128).T).astype(np.float32)
    d["bs3c"] = np.ascontiguousarray(np.asarray(I["L1_ffn_bs3"]).reshape(16, 128).T).astype(np.float32)
    d["bs2bc"] = np.ascontiguousarray(
        np.broadcast_to(np.asarray(I["L1_ffn_bs2"])[None, :], (128, 1024))).astype(np.float32)
    idn = np.eye(128, dtype=np.float32)
    import ml_dtypes
    d["idn128"] = idn.astype(ml_dtypes.bfloat16)
    return d


def p7_dram_decls(ck, xin="ExternalInput"):
    d = {}
    if xin == "ExternalInput":
        d["xn1T"] = ck.din("xn1T", [D, T], BF16)
        d["a1l"] = ck.din("a1l", [T, D], F32)
        d["x1"] = ck.din("x1", [T, D], F32)
        d["idn128"] = ck.din("idn128", [128, 128], BF16)
    d["gwT"] = ck.din("gwT", [D, 8], F32)
    d["gb_bc"] = ck.din("gb_bc", [128, 8], F32)
    d["w1T"] = ck.din("w1T", [8, D, 1024], F32)
    d["w3T"] = ck.din("w3T", [8, D, 1024], F32)
    d["w2T"] = ck.din("w2T", [8, 1024, D], F32)
    d["b1c"] = ck.din("b1c", [8, 128, 8], F32)
    d["b3c"] = ck.din("b3c", [8, 128, 8], F32)
    d["b2bc"] = ck.din("b2bc", [8, 128, 1024], F32)
    d["ws1T"] = ck.din("ws1T", [D, 2048], F32)
    d["ws3T"] = ck.din("ws3T", [D, 2048], F32)
    d["ws2T"] = ck.din("ws2T", [2048, D], F32)
    d["bs1c"] = ck.din("bs1c", [128, 16], F32)
    d["bs3c"] = ck.din("bs3c", [128, 16], F32)
    d["bs2bc"] = ck.din("bs2bc", [128, 1024], F32)
    d["out"] = ck.dout("out", [T, D], F32)
    return d


def emit_phase7(ck, block, dram):
    nc = ck.nc
    pe = ck.sem("g_pe")
    dv = ck.sem("g_dv")
    ac = ck.sem("g_ac")
    ev = ck.sem("g_ev")
    gl7 = ck.sem("p7_gl")
    wl7 = [ck.sem(f"p7_wl{j}") for j in range(2)]
    bl7 = [ck.sem(f"p7_bl{j}") for j in range(2)]
    acc7 = ck.sem("p7_acc")
    stb = [ck.sem(f"p7_stb{j}") for j in range(2)]

    xn1T = ck.sb("p7_xn1T", [128, 8 * T], BF16)
    wbf = ck.sb("p7_wbf", [128, 48 * 1024], BF16)
    w1b = wbf[:, 0:8 * 1024]
    w3b = wbf[:, 8 * 1024:16 * 1024]
    w2b = wbf[:, 16 * 1024:24 * 1024]
    ws1b = wbf[:, 0:16 * 1024]
    ws3b = wbf[:, 16 * 1024:32 * 1024]
    ws2b = wbf[:, 32 * 1024:48 * 1024]
    hgT = ck.sb("p7_hgT", [128, 16 * 512], BF16)
    yTb = ck.sb("p7_yT", [128, 8 * 512], BF16)
    stage = ck.sb("p7_stage", [128, 2 * D], F32)
    t1b = ck.sb("p7_t1", [128, 2 * 512], F32)
    t2b = ck.sb("p7_t2", [128, 2 * 512], F32)
    xab = ck.sb("p7_xab", [128, 2 * D], F32)
    gwT = ck.sb("p7_gwT", [128, 8 * 8], BF16)
    gbb = ck.sb("p7_gbb", [128, 8], F32)
    lgT = ck.sb("p7_lgT", [8, T], F32)
    lg = ck.sb("p7_lg", [128, 16 * 8], F32)
    w_all = ck.sb("p7_w", [128, 16 * 8], F32)
    stat = ck.sb("p7_stat", [128, 8], F32)
    eq = ck.sb("p7_eq", [128, 8], F32)
    b1s = ck.sb("p7_b1s", [128, 2 * 8], F32)
    b2s = ck.sb("p7_b2s", [128, 2 * 1024], F32)
    idn = ck.sb("p7_idn", [128, 128], BF16)
    idnf = ck.sb("p7_idnf", [128, 128], F32)
    PS = ck.PS
    PB = ck.PB

    @block.gpsimd
    def _(gpsimd):
        gl7.dinc(gpsimd.dma_start(out=idn[:, :], in_=dram["idn128"][:, :]))
        for j in range(8):
            gl7.dinc(gpsimd.dma_start(out=xn1T[:, j * T:(j + 1) * T],
                     in_=dram["xn1T"][j * 128:(j + 1) * 128, :]))
        gl7.dinc(gpsimd.dma_start(
            out=gwT[:, :].rearrange("p (kt e) -> p kt e", kt=8),
            in_=dram["gwT"].ap().rearrange("(kt kp) e -> kp kt e", kp=128)))
        gl7.dinc(gpsimd.dma_start(out=gbb[:, :], in_=dram["gb_bc"][:, :]))
    gl7_a = gl7.n

    xl7 = [ck.sem(f"p7_xl{j}") for j in range(2)]
    dv_b = {}
    for i in range(16):
        sl = i % 2

        @block.sync
        def _(sync, i=i, sl=sl):
            if i >= 2:
                sync.wait_ge(dv.h, dv_b[i - 2])
            xl7[sl].dinc(sync.dma_start(out=xab[:, sl * D:(sl + 1) * D],
                         in_=dram["x1"][i * 128:(i + 1) * 128, :]))
            xl7[sl].dinc(sync.dma_start(out=stage[:, sl * D:(sl + 1) * D],
                         in_=dram["a1l"][i * 128:(i + 1) * 128, :]))

        @block.vector
        def _(vector, i=i, sl=sl):
            vector.wait_ge(xl7[sl].h, 32 * (i // 2 + 1))
            if i >= 2:
                vector.wait_ge(stb[sl].h, 16 * (i // 2))
            dv.inc(vector.tensor_add(stage[:, sl * D:(sl + 1) * D],
                                     stage[:, sl * D:(sl + 1) * D],
                                     xab[:, sl * D:(sl + 1) * D]))
            dv_b[i] = dv.n

        @block.sync
        def _(sync, i=i, sl=sl):
            sync.wait_ge(dv.h, dv_b[i])
            stb[sl].dinc(sync.dma_start(out=dram["out"][i * 128:(i + 1) * 128, :],
                         in_=stage[:, sl * D:(sl + 1) * D]))
    st_base0 = (stb[0].n, stb[1].n)

    pe_lg, dv_lgT = {}, {}

    @block.vector
    def _(vector):
        vector.wait_ge(gl7.h, gl7_a)
        dv.inc(vector.tensor_copy(idnf[:, :], idn[:, :]))
    dv_ms = dv.n

    for tc in range(4):
        @block.tensor
        def _(tensor, tc=tc):
            if tc == 0:
                tensor.wait_ge(gl7.h, gl7_a)
            if tc >= 2:
                tensor.wait_ge(dv.h, dv_lgT[tc - 2])
            for kb in range(7):
                tensor.matmul(PS[4 + tc % 2][0:8, 0:512], gwT[:, kb * 8:(kb + 1) * 8],
                              xn1T[:, kb * T + tc * 512:kb * T + (tc + 1) * 512],
                              start=(kb == 0), stop=False)
            pe.inc(tensor.matmul(PS[4 + tc % 2][0:8, 0:512], gwT[:, 7 * 8:8 * 8],
                                 xn1T[:, 7 * T + tc * 512:7 * T + (tc + 1) * 512],
                                 start=False, stop=True))
            pe_lg[tc] = pe.n

        @block.vector
        def _(vector, tc=tc):
            vector.wait_ge(pe.h, pe_lg[tc])
            dv.inc(vector.tensor_copy(lgT[:, tc * 512:(tc + 1) * 512],
                                      PS[4 + tc % 2][0:8, 0:512]))
            dv_lgT[tc] = dv.n

    pe_lt, ev_lt = {}, {}
    for i in range(16):
        @block.tensor
        def _(tensor, i=i):
            if i == 0:
                tensor.wait_ge(dv.h, dv_lgT[3])
                tensor.wait_ge(dv.h, dv_ms)
            if i >= 2:
                tensor.wait_ge(ev.h, ev_lt[i - 2])
            pe.inc(tensor.transpose(PS[4 + (i % 2)][0:128, 0:8],
                                    lgT[:, i * 128:(i + 1) * 128], idnf[0:8, 0:8]))
            pe_lt[i] = pe.n

        @block.vector
        def _(vector, i=i):
            vector.wait_ge(pe.h, pe_lt[i])
            ev.inc(vector.tensor_add(lg[:, i * 8:(i + 1) * 8],
                                     PS[4 + (i % 2)][0:128, 0:8], gbb[:, :]))
            ev_lt[i] = ev.n

    dv_rt, ac_rt = {}, {}
    for i in range(16):
        @block.vector
        def _(vector, i=i):
            vector.wait_ge(ev.h, ev_lt[i])
            sl_ = lg[:, i * 8:(i + 1) * 8]
            dv.inc(vector.reduce_max(stat[:, 0:1], sl_, axis=AX.X))
            dv.flush(vector)
            dv.inc(vector.tensor_scalar_mul(stat[:, 1:2], stat[:, 0:1], -1.0))
            dv_rt[(i, 'm')] = dv.n

        @block.scalar
        def _(scalar, i=i):
            scalar.wait_ge(dv.h, dv_rt[(i, 'm')])
            ac.inc(scalar.activation(out=eq[:, :], in_=lg[:, i * 8:(i + 1) * 8],
                                     func=AF.Exp, bias=stat[:, 1:2]))
            ac_rt[i] = ac.n

        @block.vector
        def _(vector, i=i):
            vector.wait_ge(ac.h, ac_rt[i])
            dv.inc(vector.reduce_sum(stat[:, 2:3], eq[:, :], axis=AX.X))
            dv.flush(vector)
            dv.inc(vector.reciprocal(stat[:, 3:4], stat[:, 2:3]))
            dv.flush(vector)
            sc = w_all[:, i * 8:(i + 1) * 8]
            dv.inc(vector.tensor_scalar_mul(sc, eq[:, :], stat[:, 3:4]))
            dv.flush(vector)
            dv.inc(vector.reduce_max(stat[:, 4:5], sc, axis=AX.X))
            dv.flush(vector)
            dv.inc(vector.tensor_scalar(eq[:, :], sc, stat[:, 4:5], None, ALU.is_equal))
            dv.flush(vector)
            dv.inc(vector.scalar_tensor_tensor(eq[:, :], eq[:, :], -1e30, sc,
                                               ALU.mult, ALU.add))
            dv.flush(vector)
            dv.inc(vector.reduce_max(stat[:, 5:6], eq[:, :], axis=AX.X))
            dv.flush(vector)
            dv.inc(vector.tensor_scalar(eq[:, :], sc, stat[:, 5:6], None, ALU.is_ge))
            dv.flush(vector)
            dv.inc(vector.tensor_mul(sc, sc, eq[:, :]))
            dv_rt[(i, 'w')] = dv.n
    dv_w_all = dv.n

    pe_h1, pe_h3, ac_h1, dv_hg, pe_y, ac_y = {}, {}, {}, {}, {}, {}
    pe_trk, dv_c = {}, {}
    acc_n = {}
    pe_done = {}

    def expert_weights(gpsimd, e):
        if e >= 1:
            gpsimd.wait_ge(dv.h, dv_c[(e - 1, 3, 31)])
        for nm, buf in (("w1T", w1b), ("w3T", w3b), ("w2T", w2b)):
            src = dram[nm][e].rearrange("(kt kp) n -> kp kt n", kp=128)
            wl7[0].dinc(gpsimd.dma_start(
                out=buf[:, :].rearrange("p (kt n) -> p kt n", kt=8), in_=src))
        bl7[0].dinc(gpsimd.dma_start(out=b1s[:, 0:8], in_=dram["b1c"][e]))
        bl7[0].dinc(gpsimd.dma_start(out=b1s[:, 8:16], in_=dram["b3c"][e]))
        bl7[0].dinc(gpsimd.dma_start(out=b2s[:, 0:1024], in_=dram["b2bc"][e]))

    for e in range(NE):
        @block.gpsimd
        def _(gpsimd, e=e):
            expert_weights(gpsimd, e)

        for tc in range(4):
            for it in range(8):
                pp = it % 2

                @block.tensor
                def _(tensor, e=e, tc=tc, it=it, pp=pp):
                    if tc == 0 and it == 0:
                        tensor.wait_ge(wl7[0].h, 48 * (e + 1))
                        tensor.wait_ge(bl7[0].h, 48 * (e + 1))
                        if e == 0:
                            tensor.wait_ge(dv.h, dv_w_all)
                    prevs = [(e, tc, it - 2), (e, tc - 1, it + 6), (e - 1, 3, it + 6)]
                    for pk in prevs:
                        if pk in dv_hg:
                            tensor.wait_ge(dv.h, dv_hg[pk])
                            break
                    for kb in range(8):
                        tensor.matmul(PS[0 + pp][:, 0:512],
                                      w1b[:, kb * 1024 + it * 128:kb * 1024 + (it + 1) * 128],
                                      xn1T[:, kb * T + tc * 512:kb * T + (tc + 1) * 512],
                                      start=(kb == 0), stop=(kb == 7))
                    for kb in range(8):
                        mm = tensor.matmul(PS[2 + pp][:, 0:512],
                                           w3b[:, kb * 1024 + it * 128:kb * 1024 + (it + 1) * 128],
                                           xn1T[:, kb * T + tc * 512:kb * T + (tc + 1) * 512],
                                           start=(kb == 0), stop=(kb == 7))
                    pe.inc(mm)
                    pe_h1[(e, tc, it)] = pe.n

                @block.scalar
                def _(scalar, e=e, tc=tc, it=it, pp=pp):
                    scalar.wait_ge(pe.h, pe_h1[(e, tc, it)])
                    if (e, tc, it - 2) in dv_hg:
                        scalar.wait_ge(dv.h, dv_hg[(e, tc, it - 2)])
                    ac.inc(scalar.activation(out=t1b[:, pp * 512:(pp + 1) * 512],
                                             in_=PS[0 + pp][:, 0:512], func=AF.Silu,
                                             bias=b1s[:, it:it + 1]))
                    ac_h1[(e, tc, it)] = ac.n

                @block.vector
                def _(vector, e=e, tc=tc, it=it, pp=pp):
                    vector.wait_ge(ac.h, ac_h1[(e, tc, it)])
                    if it == 0 and (e, tc) != (0, 0):
                        prev = (e, tc - 1) if tc >= 1 else (e - 1, 3)
                        vector.wait_ge(pe.h, pe_y[prev])
                    vector.tensor_scalar(t2b[:, pp * 512:(pp + 1) * 512],
                                         PS[2 + pp][:, 0:512], b1s[:, 8 + it:9 + it],
                                         None, ALU.add)
                    dv.inc(vector.tensor_mul(hgT[:, it * 512:(it + 1) * 512],
                                             t1b[:, pp * 512:(pp + 1) * 512],
                                             t2b[:, pp * 512:(pp + 1) * 512]))
                    dv_hg[(e, tc, it)] = dv.n

            for dd in range(8):
                @block.tensor
                def _(tensor, e=e, tc=tc, dd=dd):
                    if dd == 0:
                        tensor.wait_ge(dv.h, dv_hg[(e, tc, 7)])
                    prevs = [(e, tc, dd - 2), (e, tc - 1, dd + 6), (e - 1, 3, dd + 6)]
                    for pk in prevs:
                        if pk in ac_y:
                            tensor.wait_ge(dv.h, ac_y[pk])
                            break
                    for kb in range(8):
                        mm = tensor.matmul(PS[4 + dd % 2][:, 0:512],
                                           w2b[:, kb * 1024 + dd * 128:kb * 1024 + (dd + 1) * 128],
                                           hgT[:, kb * 512:(kb + 1) * 512],
                                           start=(kb == 0), stop=(kb == 7))
                    pe.inc(mm)
                    pe_y[(e, tc, dd)] = pe.n
                    if dd == 7:
                        pe_y[(e, tc)] = pe.n

                @block.vector
                def _(vector, e=e, tc=tc, dd=dd):
                    vector.wait_ge(pe.h, pe_y[(e, tc, dd)])
                    prevs = [(e, tc, dd - 2, 't'), (e, tc - 1, dd + 6, 't'), (e - 1, 3, dd + 6, 't')]
                    for pk in prevs:
                        if pk in pe_trk:
                            vector.wait_ge(pe.h, pe_trk[pk])
                            break
                    dv.inc(vector.tensor_copy(yTb[:, dd * 512:(dd + 1) * 512],
                                              PS[4 + dd % 2][:, 0:512]))
                    ac_y[(e, tc, dd)] = dv.n

            for tb in range(4):
                for dd in range(8):
                    kk = tb * 8 + dd

                    @block.tensor
                    def _(tensor, e=e, tc=tc, tb=tb, dd=dd, kk=kk):
                        if tb == 0 and dd == 0:
                            tensor.wait_ge(dv.h, ac_y[(e, tc, 7)])
                        prevs = [(e, tc, kk - 2), (e, tc - 1, kk + 30), (e - 1, 3, kk + 30)]
                        for pk in prevs:
                            if pk in dv_c:
                                tensor.wait_ge(dv.h, dv_c[pk])
                                break
                        pe.inc(tensor.transpose(
                            PB[kk % 2][:, 0:128],
                            yTb[:, dd * 512 + tb * 128:dd * 512 + (tb + 1) * 128],
                            idn[:, :]))
                        pe_trk[(e, tc, kk)] = pe.n
                        if tb == 3 and dd == 7:
                            for d2 in range(8):
                                pe_trk[(e, tc, d2, 't')] = pe.n

                    @block.vector
                    def _(vector, e=e, tc=tc, tb=tb, dd=dd, kk=kk):
                        it_ = tc * 4 + tb
                        vector.wait_ge(pe.h, pe_trk[(e, tc, kk)])
                        if e == 0 and tc == 0 and dd == 0 and tb < 2:
                            vector.wait_ge(stb[tb].h, st_base0[tb])
                        if dd == 0:
                            prevs = [(e, tc, tb - 2), (e, tc - 1, tb + 2), (e - 1, 3, tb + 2)]
                            for pk in prevs:
                                if pk in acc_n:
                                    vector.wait_ge(acc7.h, acc_n[pk])
                                    break
                        sslice = stage[:, (tb % 2) * D + dd * 128:(tb % 2) * D + (dd + 1) * 128]
                        vector.tensor_add(sslice, PB[kk % 2][:, 0:128],
                                          b2s[:, dd * 128:(dd + 1) * 128])
                        dv.inc(vector.tensor_scalar_mul(
                            sslice, sslice, w_all[:, it_ * 8 + e:it_ * 8 + e + 1]))
                        dv_c[(e, tc, kk)] = dv.n

                @block.gpsimd
                def _(gpsimd, e=e, tc=tc, tb=tb):
                    it_ = tc * 4 + tb
                    gpsimd.wait_ge(dv.h, dv_c[(e, tc, tb * 8 + 7)])
                    if e == 0 and tc == 0 and tb == 0:
                        gpsimd.wait_ge(stb[0].h, stb[0].n)
                        gpsimd.wait_ge(stb[1].h, stb[1].n)
                    if tc == 0 and tb == 0 and e >= 1:
                        gpsimd.wait_ge(acc7.h, acc_n[(e - 1, 3, 3)])
                    acc7.dinc(gpsimd.dma_start(
                        out=dram["out"][it_ * 128:(it_ + 1) * 128, :],
                        in_=stage[:, (tb % 2) * D:(tb % 2 + 1) * D],
                        accum_op=ALU.add))
                    acc_n[(e, tc, tb)] = acc7.n

        pe_done[e] = pe_y[(e, 3)]

    bs13 = ck.sb("p7_bs13", [128, 32], F32)
    b2s2 = b2s[:, 1024:2048]
    E = NE

    @block.gpsimd
    def _(gpsimd):
        gpsimd.wait_ge(dv.h, dv_c[(NE - 1, 3, 31)])
        for nm, buf in (("ws1T", ws1b), ("ws3T", ws3b)):
            wl7[1].dinc(gpsimd.dma_start(
                out=buf[:, :].rearrange("p (kt n) -> p kt n", kt=8),
                in_=dram[nm].ap().rearrange("(kt kp) n -> kp kt n", kp=128)))
        wl7[1].dinc(gpsimd.dma_start(
            out=ws2b[:, :].rearrange("p (kt n) -> p kt n", kt=16),
            in_=dram["ws2T"].ap().rearrange("(kt kp) n -> kp kt n", kp=128)))
        bl7[1].dinc(gpsimd.dma_start(out=bs13[:, 0:16], in_=dram["bs1c"][:, :]))
        bl7[1].dinc(gpsimd.dma_start(out=bs13[:, 16:32], in_=dram["bs3c"][:, :]))
        bl7[1].dinc(gpsimd.dma_start(out=b2s2, in_=dram["bs2bc"][:, :]))

    for tc in range(4):
        for it in range(16):
            pp = it % 2

            @block.tensor
            def _(tensor, tc=tc, it=it, pp=pp):
                if tc == 0 and it == 0:
                    tensor.wait_ge(wl7[1].h, 48)
                    tensor.wait_ge(bl7[1].h, 48)
                prevs = [(E, tc, it - 2), (E, tc - 1, it + 14), (NE - 1, 3, 7)]
                for pk in prevs:
                    if pk in dv_hg:
                        tensor.wait_ge(dv.h, dv_hg[pk])
                        break
                for kb in range(8):
                    tensor.matmul(PS[0 + pp][:, 0:512],
                                  ws1b[:, kb * 2048 + it * 128:kb * 2048 + (it + 1) * 128],
                                  xn1T[:, kb * T + tc * 512:kb * T + (tc + 1) * 512],
                                  start=(kb == 0), stop=(kb == 7))
                for kb in range(8):
                    mm = tensor.matmul(PS[2 + pp][:, 0:512],
                                       ws3b[:, kb * 2048 + it * 128:kb * 2048 + (it + 1) * 128],
                                       xn1T[:, kb * T + tc * 512:kb * T + (tc + 1) * 512],
                                       start=(kb == 0), stop=(kb == 7))
                pe.inc(mm)
                pe_h1[(E, tc, it)] = pe.n

            @block.scalar
            def _(scalar, tc=tc, it=it, pp=pp):
                scalar.wait_ge(pe.h, pe_h1[(E, tc, it)])
                if (E, tc, it - 2) in dv_hg:
                    scalar.wait_ge(dv.h, dv_hg[(E, tc, it - 2)])
                ac.inc(scalar.activation(out=t1b[:, pp * 512:(pp + 1) * 512],
                                         in_=PS[0 + pp][:, 0:512], func=AF.Silu,
                                         bias=bs13[:, it:it + 1]))
                ac_h1[(E, tc, it)] = ac.n

            @block.vector
            def _(vector, tc=tc, it=it, pp=pp):
                vector.wait_ge(ac.h, ac_h1[(E, tc, it)])
                if it == 0:
                    prev = (E, tc - 1) if tc >= 1 else (NE - 1, 3)
                    vector.wait_ge(pe.h, pe_y[prev])
                vector.tensor_scalar(t2b[:, pp * 512:(pp + 1) * 512],
                                     PS[2 + pp][:, 0:512], bs13[:, 16 + it:17 + it],
                                     None, ALU.add)
                dv.inc(vector.tensor_mul(hgT[:, it * 512:(it + 1) * 512],
                                         t1b[:, pp * 512:(pp + 1) * 512],
                                         t2b[:, pp * 512:(pp + 1) * 512]))
                dv_hg[(E, tc, it)] = dv.n

        for dd in range(8):
            @block.tensor
            def _(tensor, tc=tc, dd=dd):
                if dd == 0:
                    tensor.wait_ge(dv.h, dv_hg[(E, tc, 15)])
                prevs = [(E, tc, dd - 2), (E, tc - 1, dd + 6), (NE - 1, 3, dd + 6)]
                for pk in prevs:
                    if pk in ac_y:
                        tensor.wait_ge(dv.h, ac_y[pk])
                        break
                for kb in range(16):
                    mm = tensor.matmul(PS[4 + dd % 2][:, 0:512],
                                       ws2b[:, kb * 1024 + dd * 128:kb * 1024 + (dd + 1) * 128],
                                       hgT[:, kb * 512:(kb + 1) * 512],
                                       start=(kb == 0), stop=(kb == 15))
                pe.inc(mm)
                pe_y[(E, tc, dd)] = pe.n
                if dd == 7:
                    pe_y[(E, tc)] = pe.n

            @block.vector
            def _(vector, tc=tc, dd=dd):
                vector.wait_ge(pe.h, pe_y[(E, tc, dd)])
                prevs = [(E, tc, dd - 2, 't'), (E, tc - 1, dd + 6, 't'), (NE - 1, 3, dd + 6, 't')]
                for pk in prevs:
                    if pk in pe_trk:
                        vector.wait_ge(pe.h, pe_trk[pk])
                        break
                dv.inc(vector.tensor_copy(yTb[:, dd * 512:(dd + 1) * 512],
                                          PS[4 + dd % 2][:, 0:512]))
                ac_y[(E, tc, dd)] = dv.n

        for tb in range(4):
            for dd in range(8):
                kk = tb * 8 + dd

                @block.tensor
                def _(tensor, tc=tc, tb=tb, dd=dd, kk=kk):
                    if tb == 0 and dd == 0:
                        tensor.wait_ge(dv.h, ac_y[(E, tc, 7)])
                    prevs = [(E, tc, kk - 2), (E, tc - 1, kk + 30), (NE - 1, 3, kk + 30)]
                    for pk in prevs:
                        if pk in dv_c:
                            tensor.wait_ge(dv.h, dv_c[pk])
                            break
                    pe.inc(tensor.transpose(
                        PB[kk % 2][:, 0:128],
                        yTb[:, dd * 512 + tb * 128:dd * 512 + (tb + 1) * 128],
                        idn[:, :]))
                    pe_trk[(E, tc, kk)] = pe.n
                    if tb == 3 and dd == 7:
                        for d2 in range(8):
                            pe_trk[(E, tc, d2, 't')] = pe.n

                @block.vector
                def _(vector, tc=tc, tb=tb, dd=dd, kk=kk):
                    vector.wait_ge(pe.h, pe_trk[(E, tc, kk)])
                    if dd == 0:
                        prevs = [(E, tc, tb - 2), (E, tc - 1, tb + 2), (NE - 1, 3, tb + 2)]
                        for pk in prevs:
                            if pk in acc_n:
                                vector.wait_ge(acc7.h, acc_n[pk])
                                break
                    sslice = stage[:, (tb % 2) * D + dd * 128:(tb % 2) * D + (dd + 1) * 128]
                    dv.inc(vector.tensor_add(sslice, PB[kk % 2][:, 0:128],
                                             b2s2[:, dd * 128:(dd + 1) * 128]))
                    dv_c[(E, tc, kk)] = dv.n

            @block.gpsimd
            def _(gpsimd, tc=tc, tb=tb):
                it_ = tc * 4 + tb
                gpsimd.wait_ge(dv.h, dv_c[(E, tc, tb * 8 + 7)])
                if tc == 0 and tb == 0:
                    gpsimd.wait_ge(acc7.h, acc_n[(NE - 1, 3, 3)])
                acc7.dinc(gpsimd.dma_start(
                    out=dram["out"][it_ * 128:(it_ + 1) * 128, :],
                    in_=stage[:, (tb % 2) * D:(tb % 2 + 1) * D],
                    accum_op=ALU.add))
                acc_n[(E, tc, tb)] = acc7.n

    @block.sync
    def _(sync):
        sync.wait_ge(acc7.h, acc7.n)
    return {}


"""Single-launch fused program: P1..P7 chained through Internal DRAM."""

G2 = [[0, 1, 2, 3], [4, 5, 6, 7]]


def emit_barrier(ck, block):
    marks = [(s, s.n) for s in ck.sems.values() if s.n > 0]
    for engname in ("sync", "gpsimd", "tensor", "vector", "scalar"):
        def _body(eng, marks=marks):
            for s, n in marks:
                eng.wait_ge(s.h, n)
        getattr(block, engname)(_body)


def build_fused():
    ck = CoreKernel()
    d1 = p1_dram_decls(ck, kind="Internal")
    d2 = p2_dram_decls(ck, kind="Internal", p1kind="none")
    for k in ["X_t", "BC_t", "B_n", "C_n", "caT", "ca_t"]:
        d2[k] = d1[k]
    d34 = p34_dram_decls(ck, kind="Internal", p1kind="none")
    for k in ["u_loc", "idn128", "xn0T"]:
        d34[k] = d1[k]
    d34["Y"] = d2["Y"]
    d5 = p5_dram_decls(ck, kind="Internal", xin="none")
    d5["x1"] = d34["x1"]
    d5["idn128"] = d1["idn128"]
    d6 = p6_dram_decls(ck, kind="Internal", xgin="none")
    d6["xg"] = d5["cc5_out"]
    d6["idn128"] = d1["idn128"]
    d7 = p7_dram_decls(ck, xin="none")
    d7["xn1T"] = d5["xn1T"]
    d7["a1l"] = d6["a1l"]
    d7["x1"] = d34["x1"]
    d7["idn128"] = d1["idn128"]

    with ck.nc.Block() as block:
        ck.push_scope()
        emit_phase1(ck, block, d1)
        ck.pop_scope()
        emit_barrier(ck, block)

        ck.push_scope()
        emit_phase2(ck, block, d2)
        ck.pop_scope()
        emit_barrier(ck, block)

        ck.push_scope()
        emit_phase34(ck, block, d34)
        ck.pop_scope()
        emit_barrier(ck, block)

        ck.push_scope()
        emit_phase5(ck, block, d5, groups=G2)
        ck.pop_scope()
        emit_barrier(ck, block)

        ck.push_scope()
        emit_phase6(ck, block, d6, groups=G2)
        ck.pop_scope()
        emit_barrier(ck, block)

        ck.push_scope()
        emit_phase7(ck, block, d7)
        ck.pop_scope()
    ck.close()
    return ck


def fused_host_inputs(I, c):
    im = {}
    im.update(host_inputs_phase1(I, c))
    p2h = host_inputs_phase2(I, c)
    p2h.pop("idn128b", None)
    im.update(p2h)
    im.update(host_inputs_phase34(I, c))
    p6h = host_inputs_p6(I, c)
    p6h.pop("idn128", None)
    im.update(p6h)
    p7h = host_inputs_p7(I, c)
    p7h.pop("idn128", None)
    im.update(p7h)
    return im


from concourse.bass_utils import run_bass_kernel_spmd

_PHASES = {}


def _phase(name, builder):
    if name not in _PHASES:
        _PHASES[name] = builder()
    return _PHASES[name]


def _build_p1():
    ck = CoreKernel()
    d = p1_dram_decls(ck, kind="ExternalOutput")
    with ck.nc.Block() as block:
        emit_phase1(ck, block, d)
    ck.close()
    return ck


def _build_p2():
    ck = CoreKernel()
    d = p2_dram_decls(ck)
    with ck.nc.Block() as block:
        emit_phase2(ck, block, d)
    ck.close()
    return ck


def _build_p34():
    ck = CoreKernel()
    d = p34_dram_decls(ck)
    with ck.nc.Block() as block:
        emit_phase34(ck, block, d)
    ck.close()
    return ck


def _build_p5():
    ck = CoreKernel()
    d = p5_dram_decls(ck)
    with ck.nc.Block() as block:
        emit_phase5(ck, block, d, groups=[[0, 1, 2, 3], [4, 5, 6, 7]])
    ck.close()
    return ck


def _build_p6():
    ck = CoreKernel()
    ck.p6_no_a1l = True
    d = p6_dram_decls(ck)
    with ck.nc.Block() as block:
        emit_phase6(ck, block, d, groups=[[0, 1, 2, 3], [4, 5, 6, 7]])
    ck.close()
    return ck


def _build_p7():
    ck = CoreKernel()
    d = p7_dram_decls(ck)
    with ck.nc.Block() as block:
        emit_phase7(ck, block, d)
    ck.close()
    return ck


def kernel(**I):
    cores = list(range(8))
    if 'ck' not in _PHASES:
        _PHASES['ck'] = build_fused()
    ckf = _PHASES['ck']
    ims = [fused_host_inputs(I, c) for c in cores]
    r = run_bass_kernel_spmd(ckf.nc, ims, core_ids=cores).results
    out = np.empty((2, 8192, 1024), np.float32)
    for c in cores:
        b, q = divmod(c, 4)
        out[b, q * T:(q + 1) * T] = np.asarray(r[c]["out"], np.float32)
    return out


def kernel_unfused(**I):
    cores = list(range(8))

    ck1 = _phase('p1', _build_p1)
    ims1 = [host_inputs_phase1(I, c) for c in cores]
    r1 = run_bass_kernel_spmd(ck1.nc, ims1, core_ids=cores).results

    ck2 = _phase('p2', _build_p2)
    ims2 = []
    for c in cores:
        im = host_inputs_phase2(I, c)
        if "idn128b" in im:
            del im["idn128b"]
        for k in ["X_t", "BC_t", "B_n", "C_n", "caT", "ca_t"]:
            im[k] = np.asarray(r1[c][k])
        ims2.append(im)
    r2 = run_bass_kernel_spmd(ck2.nc, ims2, core_ids=cores).results

    ck3 = _phase('p34', _build_p34)
    ims3 = []
    for c in cores:
        im = host_inputs_phase34(I, c)
        p1h = host_inputs_phase1(I, c)
        im["u_loc"] = p1h["u_loc"]
        im["idn128"] = p1h["idn128"]
        im["xn0T"] = np.asarray(r1[c]["xn0T"])
        im["Y"] = np.asarray(r2[c]["Y"])
        ims3.append(im)
    r3 = run_bass_kernel_spmd(ck3.nc, ims3, core_ids=cores).results
    x1 = [np.asarray(r3[c]["x1"], np.float32) for c in cores]

    import ml_dtypes
    idn = np.eye(128, dtype=np.float32).astype(ml_dtypes.bfloat16)
    ck5 = _phase('p5', _build_p5)
    ims5 = [{"x1": x1[c], "idn128": idn} for c in cores]
    r5 = run_bass_kernel_spmd(ck5.nc, ims5, core_ids=cores).results
    xn1T = [np.asarray(r5[c]["xn1T"]) for c in cores]

    ck6 = _phase('p6', _build_p6)
    ims6 = []
    for c in cores:
        b = c // 4
        im = host_inputs_p6(I, c)
        im["xg"] = np.concatenate([xn1T[4 * b + q] for q in range(4)], axis=0)
        ims6.append(im)
    r6 = run_bass_kernel_spmd(ck6.nc, ims6, core_ids=cores).results
    a1s = [np.asarray(r6[c]["a1s"], np.float32) for c in cores]
    a1l = []
    for c in cores:
        b, q = divmod(c, 4)
        a1l.append(np.hstack([a1s[4 * b + hh][q * T:(q + 1) * T]
                              for hh in range(4)]))

    ck7 = _phase('p7', _build_p7)
    ims7 = []
    for c in cores:
        im = host_inputs_p7(I, c)
        im["xn1T"] = xn1T[c]
        im["a1l"] = a1l[c]
        im["x1"] = x1[c]
        ims7.append(im)
    r7 = run_bass_kernel_spmd(ck7.nc, ims7, core_ids=cores).results

    out = np.empty((2, 8192, 1024), np.float32)
    for c in cores:
        b, q = divmod(c, 4)
        out[b, q * T:(q + 1) * T] = np.asarray(r7[c]["out"], np.float32)
    return out
